# version 01
# baseline (speedup 1.0000x reference)
import math
import numpy as np
import ml_dtypes
from contextlib import ExitStack

import concourse.bass as bass
import concourse.mybir as mybir
from concourse.bass_utils import run_bass_kernel_spmd

F32 = mybir.dt.float32
BF16 = mybir.dt.bfloat16
AF = mybir.ActivationFunctionType
ALU = mybir.AluOpType
AX = mybir.AxisListType

TOK = 1024
DM = 2048
DFF = 8192
NCH = 16
NFB = 64
NH = 16
CH = 128
NLOC = 8
HALF = 512
QSCALE = float(1.0 / np.sqrt(128.0))
INV_D = float(1.0 / 2048.0)
MEAN_SCL = float(0.7 / 16777216.0)
LN_EPS = 1e-5

NCORES = 8



FENCE_SAME_ENGINE = True


class Env:
    def fence(self, which, e, ins):
        if not FENCE_SAME_ENGINE:
            return
        sem, cnt = self.fences[which]
        cnt += 1
        self.fences[which][1] = cnt
        ins.then_inc(sem, 1)
        e.wait_ge(sem, cnt)


def _mk_sems(env, names):
    d = {}
    for n in names:
        d[n] = env.ctx.enter_context(env.nc.semaphore(n))
    return d



class Barrier:

    def __init__(self, env):
        env.bar_count = getattr(env, "bar_count", 0) + 1
        self.k = env.bar_count
        self.sem = env.bar_sem

    def _do(self, eng):
        eng.drain().then_inc(self.sem, 1)
        eng.wait_ge(self.sem, 5 * self.k)

    def sync(self, e): self._do(e)
    def gpsimd(self, e): self._do(e)
    def vector(self, e): self._do(e)
    def scalar(self, e): self._do(e)
    def tensor(self, e): self._do(e)


class Prologue:

    def __init__(self, env):
        self.env = env
        self.s = _mk_sems(env, ["pro_d", "pro_g", "pro_v"])

    def sync(self, e):
        v = self.env.t
        e.dma_start(out=v.b1t[:], in_=self.env.dram["b1t"].ap()).then_inc(self.s["pro_d"], 16)
        e.dma_start(out=v.b2t[:], in_=self.env.dram["b2t"].ap()).then_inc(self.s["pro_d"], 16)
        e.dma_start(out=v.coef[:], in_=self.env.dram["prefix_coef"].ap()).then_inc(self.s["pro_d"], 16)

    def gpsimd(self, e):
        v = self.env.t
        e.iota(v.u_f32[:, 0:128], [[1, 128]], channel_multiplier=128,
               allow_small_or_imprecise_dtypes=True)
        e.iota(v.u_f32[:, 128:129], [[1, 1]], channel_multiplier=129,
               allow_small_or_imprecise_dtypes=True).then_inc(self.s["pro_g"], 1)

    def vector(self, e):
        v = self.env.t
        e.wait_ge(self.s["pro_g"], 1)
        e.tensor_scalar(v.mask[:], v.u_f32[:, 0:128], v.u_f32[:, 128:129], None, ALU.is_ge)
        e.tensor_scalar(v.ident[:], v.u_f32[:, 0:128], v.u_f32[:, 128:129], None, ALU.is_equal)
        e.memset(v.eps[:], LN_EPS)
        e.memset(v.rowstage[:, :], 0.0)
        e.memset(v.onescol[:], 1.0)
        e.memset(v.onesrow[:], 1.0)
        e.memset(v.ones_bf[:], 1.0)
        e.wait_ge(self.s["pro_d"], 48)
        e.tensor_scalar_mul(v.halfb2t[:], v.b2t[:], 0.5)

    def scalar(self, e):
        pass

    def tensor(self, e):
        pass


class ScalePhase:

    def __init__(self, env):
        self.env = env
        self.s = _mk_sems(env, ["sc_a0", "sc_a1", "sc_v", "sc_p", "sc_c", "sc_cc"])

    def sync(self, e):
        v = self.env.t
        for i in range(16):
            if i >= 2:
                e.wait_ge(self.s["sc_v"], i - 1)
            e.dma_start(out=v.land[i % 2][:], in_=self.env.dram["w12slice"].ap()[i]
                        ).then_inc(self.s[f"sc_a{i % 2}"], 16)

    def vector(self, e):
        v = self.env.t
        for i in range(16):
            e.wait_ge(self.s[f"sc_a{i % 2}"], 16 * (i // 2 + 1))
            e.tensor_reduce(v.scalepart[:, i:i + 1], v.land[i % 2][:], AX.X, ALU.add,
                            apply_absolute_value=True).then_inc(self.s["sc_v"], 1)
        e.wait_ge(self.s["sc_v"], 16)
        e.tensor_reduce(v.thr[:, 0:1], v.scalepart[:, 0:8], AX.X, ALU.add)
        e.tensor_reduce(v.thr[:, 1:2], v.scalepart[:, 8:16], AX.X, ALU.add).then_inc(self.s["sc_v"], 1)
        e.wait_ge(self.s["sc_p"], 2)
        e.tensor_copy(v.rowstage[0:1, 0:2], self.env.p[2][0:1, 0:2]).then_inc(self.s["sc_v"], 1)
        e.wait_ge(self.s["sc_c"], 32)
        e.tensor_scalar_mul(v.rowstage[0:1, 8:10], v.rowstage[0:1, 0:2], MEAN_SCL).then_inc(self.s["sc_v"], 1)
        e.wait_ge(self.s["sc_p"], 4)
        e.tensor_copy(v.thr[:, 0:2], self.env.p[3][:, 0:2])
        e.tensor_scalar_mul(v.nthr[:, 0:2], self.env.p[3][:, 0:2], -1.0).then_inc(self.s["sc_v"], 1)

    def tensor(self, e):
        v = self.env.t
        e.wait_ge(self.s["sc_v"], 17)
        e.matmul(self.env.p[2][0:1, 0:1], v.thr[:, 0:1], v.onescol[:, 0:1], start=True, stop=True
                 ).then_inc(self.s["sc_p"], 1)
        e.matmul(self.env.p[2][0:1, 1:2], v.thr[:, 1:2], v.onescol[:, 0:1], start=True, stop=True
                 ).then_inc(self.s["sc_p"], 1)
        e.wait_ge(self.s["sc_v"], 19)
        e.matmul(self.env.p[3][:, 0:1], v.onesrow[0:1, :], v.rowstage[0:1, 8:9], start=True, stop=True
                 ).then_inc(self.s["sc_p"], 1)
        e.matmul(self.env.p[3][:, 1:2], v.onesrow[0:1, :], v.rowstage[0:1, 9:10], start=True, stop=True
                 ).then_inc(self.s["sc_p"], 1)

    def gpsimd(self, e):
        v = self.env.t
        e.wait_ge(self.s["sc_v"], 18)
        e.dma_start(out=self.env.dram["ccs_in"].ap(), in_=v.rowstage[0:1, 0:8]).then_inc(self.s["sc_c"], 16)
        e.wait_ge(self.s["sc_c"], 16)
        e.collective_compute(
            "AllReduce", ALU.add, replica_groups=[list(range(NCORES))],
            ins=[self.env.dram["ccs_in"].ap().opt()], outs=[self.env.dram["ccs_out"].ap().opt()],
        ).then_inc(self.s["sc_cc"], 1)
        e.wait_ge(self.s["sc_cc"], 1)
        e.dma_start(out=v.rowstage[0:1, 0:8], in_=self.env.dram["ccs_out"].ap()).then_inc(self.s["sc_c"], 16)
        e.wait_ge(self.s["sc_c"], 32)

    def scalar(self, e):
        pass


class LNStats:

    def __init__(self, env, tag, src_ap):
        self.env = env
        self.src_ap = src_ap
        self.s = _mk_sems(env, [f"{tag}_b0", f"{tag}_b1", f"{tag}_v", f"{tag}_t", f"{tag}_p"])
        self.tag = tag

    def k(self, n):
        return self.s[f"{self.tag}_{n}"]

    def bwait(self, e, c):
        e.wait_ge(self.k(f"b{c % 2}"), 16 * (c // 2 + 1))

    def gpsimd(self, e):
        v = self.env.t
        for c in range(NCH):
            if c >= 2:
                e.wait_ge(self.k("v"), c - 1)
                e.wait_ge(self.k("t"), c - 1)
            e.dma_start(out=v.xstr[c % 2][:], in_=self.src_ap(c)).then_inc(self.k(f"b{c % 2}"), 16)

    def vector(self, e):
        v = self.env.t
        for c in range(NCH):
            self.bwait(e, c)
            if c >= 2:
                e.wait_ge(self.k("p"), 4 * (c - 2) + 2)
            e.tensor_copy(v.castbuf[c % 2][:], v.xstr[c % 2][:]).then_inc(self.k("v"), 1)

    def scalar(self, e):
        v = self.env.t
        for c in range(NCH):
            self.bwait(e, c)
            if c >= 2:
                e.wait_ge(self.k("p"), 4 * (c - 2) + 4)
            e.activation(v.sqbuf[c % 2][:], v.xstr[c % 2][:], AF.Square).then_inc(self.k("t"), 1)

    def tensor(self, e):
        v = self.env.t
        p = self.env.p
        for c in range(NCH):
            e.wait_ge(self.k("v"), c + 1)
            e.wait_ge(self.k("t"), c + 1)
            st = (c == 0)
            sp = (c == NCH - 1)
            e.matmul(p[2][0:1, :], v.ones_bf[:, 0:1], v.castbuf[c % 2][:, 0:HALF], start=st, stop=sp
                     ).then_inc(self.k("p"), 1)
            e.matmul(p[3][0:1, :], v.ones_bf[:, 0:1], v.castbuf[c % 2][:, HALF:TOK], start=st, stop=sp
                     ).then_inc(self.k("p"), 1)
            e.matmul(p[4][0:1, :], v.ones_bf[:, 0:1], v.sqbuf[c % 2][:, 0:HALF], start=st, stop=sp
                     ).then_inc(self.k("p"), 1)
            e.matmul(p[5][0:1, :], v.ones_bf[:, 0:1], v.sqbuf[c % 2][:, HALF:TOK], start=st, stop=sp
                     ).then_inc(self.k("p"), 1)

    def sync(self, e):
        pass


class LNRowsApply:

    def __init__(self, env, tag, src_ap):
        self.env = env
        self.src_ap = src_ap
        self.s = _mk_sems(env, [f"{tag}_b0", f"{tag}_b1", f"{tag}_v", f"{tag}_t", f"{tag}_p"])
        self.tag = tag

    def k(self, n):
        return self.s[f"{self.tag}_{n}"]

    def _rows(self):
        v = self.env.t
        return v.rowA[:, :], v.rows[0:1, :], v.rowC[:, :]

    def vector(self, e):
        v = self.env.t
        p = self.env.p
        A, B, C = self._rows()
        for j in range(2):
            ins = e.tensor_scalar_mul(A[:, j * HALF:(j + 1) * HALF], p[2 + j][0:1, :], INV_D)
        self.env.fence("v", e, ins)
        ins = e.tensor_mul(C[:, :], A[:, :], A[:, :])
        self.env.fence("v", e, ins)
        for j in range(2):
            ins = e.scalar_tensor_tensor(C[:, j * HALF:(j + 1) * HALF], p[4 + j][0:1, :], INV_D,
                                         C[:, j * HALF:(j + 1) * HALF], ALU.mult, ALU.subtract)
        ins.then_inc(self.k("v"), 1)
        e.wait_ge(self.k("t"), 1)
        ins = e.reciprocal(B[:, :], B[:, :])
        self.env.fence("v", e, ins)
        e.tensor_mul(C[:, :], A[:, :], B[:, :]).then_inc(self.k("v"), 1)
        for c in range(NCH):
            e.wait_ge(self.k(f"b{c % 2}"), 16 * (c // 2 + 1))
            e.wait_ge(self.k("p"), 4)
            if c >= 1:
                e.wait_ge(self.k("v"), 2 + c)
            for h in range(2):
                sl = slice(h * HALF, (h + 1) * HALF)
                tins = e.tensor_mul(v.u_f32[:, h * HALF:(h + 1) * HALF], v.xstr[c % 2][:, sl],
                                    p[2 + h][:, :])
                self.env.fence("v", e, tins)
                ins = e.tensor_sub(self.env.xln_view(c)[:, sl],
                                   v.u_f32[:, h * HALF:(h + 1) * HALF], p[4 + h][:, :])
                if h == 0:
                    self.env.fence("v", e, ins)
            ins.then_inc(self.k("v"), 1)

    def scalar(self, e):
        v = self.env.t
        A, B, C = self._rows()
        e.wait_ge(self.k("v"), 1)
        e.activation(B[:, :], C[:, :], AF.Sqrt, bias=v.eps[0:1, 0:1]).then_inc(self.k("t"), 1)

    def tensor(self, e):
        v = self.env.t
        p = self.env.p
        A, B, C = self._rows()
        e.wait_ge(self.k("v"), 2)
        for j in range(2):
            e.matmul(p[2 + j][:, :], v.onesrow[0:1, :], B[:, j * HALF:(j + 1) * HALF],
                     start=True, stop=True).then_inc(self.k("p"), 1)
            e.matmul(p[4 + j][:, :], v.onesrow[0:1, :], C[:, j * HALF:(j + 1) * HALF],
                     start=True, stop=True).then_inc(self.k("p"), 1)

    def gpsimd(self, e):
        v = self.env.t
        for c in range(NCH):
            if c >= 2:
                e.wait_ge(self.k("v"), 2 + c - 1)
            e.dma_start(out=v.xstr[c % 2][:], in_=self.src_ap(c)).then_inc(self.k(f"b{c % 2}"), 16)

    def sync(self, e):
        pass


class FFNP1:

    def __init__(self, env, tag, thr_col):
        self.env = env
        self.thr_col = thr_col
        self.s = _mk_sems(env, [f"{tag}_a0", f"{tag}_a1", f"{tag}_q", f"{tag}_m", f"{tag}_g"])
        self.tag = tag

    def k(self, n):
        return self.s[f"{self.tag}_{n}"]

    def sync(self, e):
        v = self.env.t
        for F in range(NFB):
            if F >= 2:
                e.wait_ge(self.k("q"), F - 1)
            e.dma_start(out=v.land[F % 2][:], in_=self.env.dram["w1p"].ap()[F]
                        ).then_inc(self.k(f"a{F % 2}"), 16)

    def vector(self, e):
        v = self.env.t
        tc = self.thr_col
        for F in range(NFB):
            e.wait_ge(self.k(f"a{F % 2}"), 16 * (F // 2 + 1))
            if F >= 2:
                e.wait_ge(self.k("m"), 2 * (F - 2) + 2)
            if F >= 1:
                e.wait_ge(self.k("q"), F)
            uins = e.tensor_scalar(v.u_f32[:], v.land[F % 2][:], v.nthr[:, tc:tc + 1], None, ALU.is_le)
            self.env.fence("v", e, uins)
            e.scalar_tensor_tensor(v.wq[F % 2][:], v.land[F % 2][:], v.thr[:, tc:tc + 1],
                                   v.u_f32[:], ALU.is_ge, ALU.subtract).then_inc(self.k("q"), 1)

    def tensor(self, e):
        v = self.env.t
        p = self.env.p
        for F in range(NFB):
            e.wait_ge(self.k("q"), F + 1)
            for h in range(2):
                if F >= 1:
                    e.wait_ge(self.k("g"), 2 * (F - 1) + h + 1)
                for c in range(NCH):
                    mm = e.matmul(p[h][:, :], v.wq[F % 2][:, c * 128:(c + 1) * 128],
                                  self.env.xln_view(c)[:, h * HALF:(h + 1) * HALF],
                                  start=(c == 0), stop=(c == NCH - 1))
                mm.then_inc(self.k("m"), 1)

    def scalar(self, e):
        v = self.env.t
        p = self.env.p
        for F in range(NFB):
            for h in range(2):
                e.wait_ge(self.k("m"), 2 * F + h + 1)
                e.activation(v.h_big[:, F * TOK + h * HALF: F * TOK + (h + 1) * HALF],
                             p[h][:, :], AF.Gelu, bias=v.b1t[:, F:F + 1], scale=1.0
                             ).then_inc(self.k("g"), 1)

    def gpsimd(self, e):
        pass


class FFNP2:

    def __init__(self, env, tag, thr_col, x_src_ap, out_ap, do_stats):
        self.env = env
        self.thr_col = thr_col
        self.x_src_ap = x_src_ap
        self.out_ap = out_ap
        self.do_stats = do_stats
        self.s = _mk_sems(env, [f"{tag}_a0", f"{tag}_a1", f"{tag}_q", f"{tag}_m", f"{tag}_e",
                                f"{tag}_bl0", f"{tag}_bl1", f"{tag}_bs",
                                f"{tag}_v", f"{tag}_t", f"{tag}_p"])
        self.tag = tag

    def k(self, n):
        return self.s[f"{self.tag}_{n}"]

    def load_wait(self, e, D):
        e.wait_ge(self.k(f"bl{D % 2}"), 16 * (D // 2 + 1))

    def gpsimd(self, e):
        v = self.env.t
        e.dma_start(out=v.xstr[0][:], in_=self.x_src_ap(0)).then_inc(self.k("bl0"), 16)
        e.dma_start(out=v.xstr[1][:], in_=self.x_src_ap(1)).then_inc(self.k("bl1"), 16)
        for D in range(NCH):
            e.wait_ge(self.k("e"), 2 * D + 2)
            if self.do_stats:
                e.wait_ge(self.k("v"), self.v_after_cast(D))
                e.wait_ge(self.k("t"), D + 1)
            e.dma_start(out=self.out_ap(D), in_=v.stage[:]).then_inc(self.k("bs"), 16)
            if D + 2 < NCH:
                e.dma_start(out=v.xstr[(D + 2) % 2][:], in_=self.x_src_ap(D + 2)
                            ).then_inc(self.k(f"bl{D % 2}"), 16)
        e.wait_ge(self.k("bs"), 16 * NCH)

    def v_after_cast(self, D):
        assert self.do_stats
        return 2 + 2 * D if D < NCH - 1 else 2 * NCH

    def v_after_xp(self, Dp):
        if Dp == 0:
            return 1
        if self.do_stats:
            return 2 * Dp + 1
        return Dp + 1

    def vector(self, e):
        v = self.env.t
        tc = self.thr_col
        self.load_wait(e, 0)
        e.tensor_scalar(v.xstr[0][:], v.xstr[0][:], v.halfb2t[:, 0:1], None, ALU.add
                        ).then_inc(self.k("v"), 1)
        for D in range(NCH):
            if D >= 1:
                self._emit_evac(e, D - 1)
            for qu in range(4):
                kk = 4 * D + qu
                e.wait_ge(self.k(f"a{kk % 2}"), 16 * (kk // 2 + 1))
                if kk >= 2:
                    e.wait_ge(self.k("m"), kk - 1)
                if kk >= 1:
                    e.wait_ge(self.k("q"), kk)
                uins = e.tensor_scalar(v.u_f32[:], v.land[kk % 2][:], v.nthr[:, tc:tc + 1], None, ALU.is_le)
                self.env.fence("v", e, uins)
                e.scalar_tensor_tensor(v.wq[kk % 2][:], v.land[kk % 2][:], v.thr[:, tc:tc + 1],
                                       v.u_f32[:], ALU.is_ge, ALU.subtract).then_inc(self.k("q"), 1)
        self._emit_evac(e, NCH - 1)

    def _emit_evac(self, e, D):
        v = self.env.t
        if D in getattr(self, "_evac_done", set()):
            return
        self._evac_done = getattr(self, "_evac_done", set())
        self._evac_done.add(D)
        if D >= 1:
            e.wait_ge(self.k("bs"), 16 * D)
        e.wait_ge(self.k("v"), self.v_after_xp(D))
        e.wait_ge(self.k("m"), 4 * (D + 1))
        for h in range(2):
            e.scalar_tensor_tensor(v.stage[:, h * HALF:(h + 1) * HALF], self.env.p[h][:, :], 0.5,
                                   v.xstr[D % 2][:, h * HALF:(h + 1) * HALF], ALU.mult, ALU.add
                                   ).then_inc(self.k("e"), 1)
        if self.do_stats:
            if D >= 2:
                e.wait_ge(self.k("p"), 4 * (D - 2) + 2)
            e.wait_ge(self.k("e"), 2 * D + 2)
            e.tensor_copy(self.env.t.castbuf[D % 2][:], v.stage[:]).then_inc(self.k("v"), 1)
        if D + 1 < NCH:
            self.load_wait(e, D + 1)
            e.tensor_scalar(v.xstr[(D + 1) % 2][:], v.xstr[(D + 1) % 2][:],
                            v.halfb2t[:, D + 1:D + 2], None, ALU.add).then_inc(self.k("v"), 1)

    def tensor(self, e):
        v = self.env.t
        p = self.env.p
        for D in range(NCH):
            if D >= 1:
                e.wait_ge(self.k("e"), 2 * (D - 1) + 2)
            for qu in range(4):
                kk = 4 * D + qu
                e.wait_ge(self.k("q"), kk + 1)
                for i in range(16):
                    F = qu * 16 + i
                    st = (qu == 0 and i == 0)
                    sp = (qu == 3 and i == 15)
                    e.matmul(p[0][:, :], v.wq[kk % 2][:, i * 128:(i + 1) * 128],
                             v.h_big[:, F * TOK: F * TOK + HALF], start=st, stop=sp)
                    mm = e.matmul(p[1][:, :], v.wq[kk % 2][:, i * 128:(i + 1) * 128],
                                  v.h_big[:, F * TOK + HALF: F * TOK + TOK], start=st, stop=sp)
                mm.then_inc(self.k("m"), 1)
            if self.do_stats and D >= 1:
                self._emit_stats(e, D - 1)
        if self.do_stats:
            self._emit_stats(e, NCH - 1)

    def _emit_stats(self, e, D):
        v = self.env.t
        p = self.env.p
        if D in getattr(self, "_stats_done", set()):
            return
        self._stats_done = getattr(self, "_stats_done", set())
        self._stats_done.add(D)
        e.wait_ge(self.k("v"), self.v_after_cast(D))
        e.wait_ge(self.k("t"), D + 1)
        st = (D == 0)
        sp = (D == NCH - 1)
        e.matmul(p[2][0:1, :], v.ones_bf[:, 0:1], v.castbuf[D % 2][:, 0:HALF], start=st, stop=sp
                 ).then_inc(self.k("p"), 1)
        e.matmul(p[3][0:1, :], v.ones_bf[:, 0:1], v.castbuf[D % 2][:, HALF:TOK], start=st, stop=sp
                 ).then_inc(self.k("p"), 1)
        e.matmul(p[4][0:1, :], v.ones_bf[:, 0:1], v.sqbuf[D % 2][:, 0:HALF], start=st, stop=sp
                 ).then_inc(self.k("p"), 1)
        e.matmul(p[5][0:1, :], v.ones_bf[:, 0:1], v.sqbuf[D % 2][:, HALF:TOK], start=st, stop=sp
                 ).then_inc(self.k("p"), 1)

    def scalar(self, e):
        if not self.do_stats:
            return
        v = self.env.t
        for D in range(NCH):
            e.wait_ge(self.k("e"), 2 * D + 2)
            if D >= 2:
                e.wait_ge(self.k("p"), 4 * (D - 2) + 4)
            e.activation(v.sqbuf[D % 2][:], v.stage[:], AF.Square).then_inc(self.k("t"), 1)

    def sync(self, e):
        v = self.env.t
        for D in range(NCH):
            for qu in range(4):
                kk = 4 * D + qu
                if kk >= 2:
                    e.wait_ge(self.k("q"), kk - 1)
                e.dma_start(out=v.land[kk % 2][:],
                            in_=self.env.dram["w2p"].ap()[D][:, qu * 2048:(qu + 1) * 2048]
                            ).then_inc(self.k(f"a{kk % 2}"), 16)


class ProjPhase:

    def __init__(self, env):
        self.env = env
        self.s = _mk_sems(env, ["pj_b0", "pj_b1", "pj_m", "pj_t"])

    def k(self, n):
        return self.s[n]

    def _mats(self):
        v = self.env.t
        return [("wqp", v.big_q, AF.Identity, QSCALE),
                ("wkp", v.big_k, AF.Sign, 1.0),
                ("wvp", v.big_ktm, AF.Sign, 1.0)]

    def gpsimd(self, e):
        v = self.env.t
        for mi, (wname, _, _, _) in enumerate(self._mats()):
            for O in range(NCH):
                kk = mi * NCH + O
                if kk >= 2:
                    e.wait_ge(self.k("pj_m"), 2 * (kk - 2) + 2)
                e.dma_start(out=v.wq[kk % 2][:], in_=self.env.dram[wname].ap()[O]
                            ).then_inc(self.k(f"pj_b{kk % 2}"), 16)

    def tensor(self, e):
        v = self.env.t
        p = self.env.p
        for mi, (_, _, _, _) in enumerate(self._mats()):
            for O in range(NCH):
                kk = mi * NCH + O
                e.wait_ge(self.k(f"pj_b{kk % 2}"), 16 * (kk // 2 + 1))
                for h in range(2):
                    if kk >= 1:
                        e.wait_ge(self.k("pj_t"), 2 * (kk - 1) + h + 1)
                    for c in range(NCH):
                        mm = e.matmul(p[h][:, :], v.wq[kk % 2][:, c * 128:(c + 1) * 128],
                                      self.env.xln_view(c)[:, h * HALF:(h + 1) * HALF],
                                      start=(c == 0), stop=(c == NCH - 1))
                    mm.then_inc(self.k("pj_m"), 1)

    def scalar(self, e):
        v = self.env.t
        p = self.env.p
        for mi, (_, dst, func, scl) in enumerate(self._mats()):
            for O in range(NCH):
                kk = mi * NCH + O
                for h in range(2):
                    e.wait_ge(self.k("pj_m"), 2 * kk + h + 1)
                    e.activation(dst[:, O * TOK + h * HALF: O * TOK + (h + 1) * HALF],
                                 p[h][:, :], func, scale=scl).then_inc(self.k("pj_t"), 1)

    def sync(self, e):
        pass

    def vector(self, e):
        pass


class TransposePhase:

    def __init__(self, env):
        self.env = env
        self.s = _mk_sems(env, ["tr_p", "tr_v"])

    def _pairs(self):
        v = self.env.t
        return [(v.big_ktm, v.big_v), (v.big_k, v.big_ktm)]

    def tensor(self, e):
        v = self.env.t
        idx = 0
        for (src, dst) in self._pairs():
            for E in range(NCH):
                for tt in range(NLOC):
                    if idx >= 2:
                        e.wait_ge(self.s["tr_v"], idx - 1)
                    e.transpose(self.env.pt[idx % 2][:, :],
                                src[:, E * TOK + tt * 128: E * TOK + (tt + 1) * 128],
                                v.ident[:, :]).then_inc(self.s["tr_p"], 1)
                    idx += 1

    def vector(self, e):
        idx = 0
        for (src, dst) in self._pairs():
            for E in range(NCH):
                for tt in range(NLOC):
                    e.wait_ge(self.s["tr_p"], idx + 1)
                    e.tensor_copy(dst[:, tt * DM + E * 128: tt * DM + (E + 1) * 128],
                                  self.env.pt[idx % 2][:, :]).then_inc(self.s["tr_v"], 1)
                    idx += 1

    def sync(self, e):
        pass

    def gpsimd(self, e):
        pass

    def scalar(self, e):
        pass


class ScanPhase:

    def __init__(self, env):
        self.env = env
        self.s = _mk_sems(env, ["sn_S", "sn_O", "sn_K", "sn_mk", "sn_ad", "sn_ev", "sn_ct", "sn_in"])

    def vector(self, e):
        v = self.env.t
        e.memset(v.state[:], 0.0).then_inc(self.s["sn_in"], 1)
        for n in range(NLOC):
            for h in range(NH):
                idx = n * NH + h
                pi = idx % 2
                e.wait_ge(self.s["sn_S"], idx + 1)
                if idx >= 2:
                    e.wait_ge(self.s["sn_O"], idx - 1)
                e.tensor_mul(v.sm[pi][:, :], self.env.p[2 + pi][:, 0:128], v.mask[:, :]
                             ).then_inc(self.s["sn_mk"], 1)
                e.wait_ge(self.s["sn_K"], idx + 1)
                if n == 0:
                    e.wait_ge(self.s["sn_in"], 1)
                else:
                    e.wait_ge(self.s["sn_ad"], (n - 1) * NH + h + 1)
                e.tensor_add(v.state[:, h * 128:(h + 1) * 128], v.state[:, h * 128:(h + 1) * 128],
                             self.env.p[4 + pi][:, 0:128]).then_inc(self.s["sn_ad"], 1)

    def tensor(self, e):
        v = self.env.t
        p = self.env.p
        e.wait_ge(self.s["sn_in"], 1)
        for n in range(NLOC):
            for h in range(NH):
                idx = n * NH + h
                pi = idx % 2
                qv = v.big_q[:, h * TOK + n * 128: h * TOK + (n + 1) * 128]
                kv_fm = v.big_k[:, h * TOK + n * 128: h * TOK + (n + 1) * 128]
                vv_tm = v.big_v[:, n * DM + h * 128: n * DM + (h + 1) * 128]
                kv_tm = v.big_ktm[:, n * DM + h * 128: n * DM + (h + 1) * 128]
                if idx >= 2:
                    e.wait_ge(self.s["sn_ev"], idx - 1)
                if n > 0:
                    e.wait_ge(self.s["sn_ct"], (n - 1) * NH + h + 1)
                    e.matmul(p[pi][:, 0:128], v.roll[:, h * 128:(h + 1) * 128], qv,
                             start=True, stop=False)
                if idx >= 2:
                    e.wait_ge(self.s["sn_mk"], idx - 1)
                e.matmul(p[2 + pi][:, 0:128], kv_fm, qv, start=True, stop=True
                         ).then_inc(self.s["sn_S"], 1)
                if idx >= 2:
                    e.wait_ge(self.s["sn_ad"], idx - 1)
                e.matmul(p[4 + pi][:, 0:128], kv_tm, vv_tm, start=True, stop=True
                         ).then_inc(self.s["sn_K"], 1)
                e.wait_ge(self.s["sn_mk"], idx + 1)
                e.matmul(p[pi][:, 0:128], vv_tm, v.sm[pi][:, :],
                         start=(n == 0), stop=True).then_inc(self.s["sn_O"], 1)

    def scalar(self, e):
        v = self.env.t
        for n in range(NLOC):
            for h in range(NH):
                idx = n * NH + h
                pi = idx % 2
                e.wait_ge(self.s["sn_O"], idx + 1)
                e.activation(v.big_attn[:, h * TOK + n * 128: h * TOK + (n + 1) * 128],
                             self.env.p[pi][:, 0:128], AF.Identity).then_inc(self.s["sn_ev"], 1)
                e.wait_ge(self.s["sn_ad"], idx + 1)
                e.activation(v.roll[:, h * 128:(h + 1) * 128],
                             v.state[:, h * 128:(h + 1) * 128], AF.Identity
                             ).then_inc(self.s["sn_ct"], 1)

    def sync(self, e):
        pass

    def gpsimd(self, e):
        pass


class AGPhase:

    def __init__(self, env):
        self.env = env
        self.s = _mk_sems(env, ["ag_c", "ag_cc", "ag_g0", "ag_g1", "ag_v"])

    def gpsimd(self, e):
        v = self.env.t
        e.dma_start(out=self.env.dram["cc_si"].ap(), in_=v.state[:]).then_inc(self.s["ag_c"], 16)
        e.wait_ge(self.s["ag_c"], 16)
        e.collective_compute(
            "AllGather", ALU.bypass, replica_groups=[list(range(NCORES))],
            ins=[self.env.dram["cc_si"].ap().opt()], outs=[self.env.dram["cc_so"].ap().opt()],
        ).then_inc(self.s["ag_cc"], 1)
        e.wait_ge(self.s["ag_cc"], 1)
        for r in range(NCORES):
            if r >= 2:
                e.wait_ge(self.s["ag_v"], r)
            e.dma_start(out=v.land[r % 2][:],
                        in_=self.env.dram["cc_so"].ap()[r * 128:(r + 1) * 128, :]
                        ).then_inc(self.s[f"ag_g{r % 2}"], 16)

    def vector(self, e):
        v = self.env.t
        e.wait_ge(self.s["ag_c"], 16)
        e.memset(v.state[:], 0.0).then_inc(self.s["ag_v"], 1)
        for r in range(NCORES):
            e.wait_ge(self.s[f"ag_g{r % 2}"], 16 * (r // 2 + 1))
            e.wait_ge(self.s["ag_v"], 1 + r)
            e.scalar_tensor_tensor(v.state[:], v.land[r % 2][:], v.coef[:, r:r + 1],
                                   v.state[:], ALU.mult, ALU.add).then_inc(self.s["ag_v"], 1)
        e.wait_ge(self.s["ag_v"], 1 + NCORES)
        for h in range(NH):
            ins = e.tensor_copy(v.roll[:, h * 128:(h + 1) * 128], v.state[:, h * 128:(h + 1) * 128])
        ins.then_inc(self.s["ag_v"], 1)

    def sync(self, e):
        pass

    def scalar(self, e):
        pass

    def tensor(self, e):
        pass


class Pass2Phase:

    def __init__(self, env, ag):
        self.env = env
        self.ag = ag
        self.s = _mk_sems(env, ["p2_m", "p2_v"])

    def tensor(self, e):
        v = self.env.t
        p = self.env.p
        e.wait_ge(self.ag.s["ag_v"], NCORES + 2)
        for h in range(NH):
            for hf in range(2):
                idx = 2 * h + hf
                if idx >= 2:
                    e.wait_ge(self.s["p2_v"], idx - 1)
                e.matmul(p[idx % 2][:, :], v.roll[:, h * 128:(h + 1) * 128],
                         v.big_q[:, h * TOK + hf * HALF: h * TOK + (hf + 1) * HALF],
                         start=True, stop=True).then_inc(self.s["p2_m"], 1)

    def vector(self, e):
        v = self.env.t
        for h in range(NH):
            for hf in range(2):
                idx = 2 * h + hf
                e.wait_ge(self.s["p2_m"], idx + 1)
                sl = slice(h * TOK + hf * HALF, h * TOK + (hf + 1) * HALF)
                e.tensor_add(v.big_attn[:, sl], v.big_attn[:, sl], self.env.p[idx % 2][:, :]
                             ).then_inc(self.s["p2_v"], 1)

    def sync(self, e):
        pass

    def gpsimd(self, e):
        pass

    def scalar(self, e):
        pass


class WoPhase:

    def __init__(self, env, x_src_ap, out_ap):
        self.env = env
        self.x_src_ap = x_src_ap
        self.out_ap = out_ap
        self.s = _mk_sems(env, ["wo_a0", "wo_a1", "wo_m", "wo_e", "wo_bl0", "wo_bl1",
                                "wo_bs", "wo_v", "wo_t", "wo_p"])

    def k(self, n):
        return self.s[n]

    def load_wait(self, e, O):
        e.wait_ge(self.k(f"wo_bl{O % 2}"), 16 * (O // 2 + 1))

    def gpsimd(self, e):
        v = self.env.t
        e.dma_start(out=v.xstr[0][:], in_=self.x_src_ap(0)).then_inc(self.k("wo_bl0"), 16)
        e.dma_start(out=v.xstr[1][:], in_=self.x_src_ap(1)).then_inc(self.k("wo_bl1"), 16)
        for O in range(NCH):
            e.wait_ge(self.k("wo_e"), 2 * O + 2)
            e.wait_ge(self.k("wo_v"), self.v_after_cast(O))
            e.wait_ge(self.k("wo_t"), O + 1)
            e.dma_start(out=self.out_ap(O), in_=v.stage[:]).then_inc(self.k("wo_bs"), 16)
            if O + 2 < NCH:
                e.dma_start(out=v.xstr[(O + 2) % 2][:], in_=self.x_src_ap(O + 2)
                            ).then_inc(self.k(f"wo_bl{O % 2}"), 16)
        e.wait_ge(self.k("wo_bs"), 16 * NCH)

    def sync(self, e):
        v = self.env.t
        for O in range(NCH):
            if O >= 2:
                e.wait_ge(self.k("wo_m"), 2 * (O - 2) + 2)
            e.dma_start(out=v.land[O % 2][:], in_=self.env.dram["wop"].ap()[O]
                        ).then_inc(self.k(f"wo_a{O % 2}"), 16)

    def v_after_cast(self, O):
        return 2 * O + 3 if O < NCH - 1 else 2 * NCH

    def vector(self, e):
        v = self.env.t
        for O in range(NCH):
            e.wait_ge(self.k(f"wo_a{O % 2}"), 16 * (O // 2 + 1))
            if O >= 2:
                e.wait_ge(self.k("wo_m"), 2 * (O - 2) + 2)
            e.tensor_copy(v.wq[O % 2][:], v.land[O % 2][:]).then_inc(self.k("wo_v"), 1)
            if O >= 1:
                self._emit_evac(e, O - 1)
        self._emit_evac(e, NCH - 1)

    def _emit_evac(self, e, O):
        v = self.env.t
        done = getattr(self, "_evd", set())
        self._evd = done
        if O in done:
            return
        done.add(O)
        if O >= 1:
            e.wait_ge(self.k("wo_bs"), 16 * O)
        self.load_wait(e, O)
        for h in range(2):
            e.wait_ge(self.k("wo_m"), 2 * O + h + 1)
            e.tensor_add(v.stage[:, h * HALF:(h + 1) * HALF], v.xstr[O % 2][:, h * HALF:(h + 1) * HALF],
                         self.env.p[h][:, :]).then_inc(self.k("wo_e"), 1)
        if O >= 2:
            e.wait_ge(self.k("wo_p"), 4 * (O - 2) + 2)
        e.wait_ge(self.k("wo_e"), 2 * O + 2)
        e.tensor_copy(v.castbuf[O % 2][:], v.stage[:]).then_inc(self.k("wo_v"), 1)

    def tensor(self, e):
        v = self.env.t
        p = self.env.p
        for O in range(NCH):
            e.wait_ge(self.k("wo_v"), self.v_after_wcast(O))
            for h in range(2):
                if O >= 1:
                    e.wait_ge(self.k("wo_e"), 2 * (O - 1) + h + 1)
                for c in range(NCH):
                    mm = e.matmul(p[h][:, :], v.wq[O % 2][:, c * 128:(c + 1) * 128],
                                  v.big_attn[:, c * TOK + h * HALF: c * TOK + (h + 1) * HALF],
                                  start=(c == 0), stop=(c == NCH - 1))
                mm.then_inc(self.k("wo_m"), 1)
            if O >= 1:
                self._emit_stats(e, O - 1)
        self._emit_stats(e, NCH - 1)

    def v_after_wcast(self, O):
        return 1 if O == 0 else 2 * O

    def _emit_stats(self, e, O):
        v = self.env.t
        p = self.env.p
        done = getattr(self, "_std", set())
        self._std = done
        if O in done:
            return
        done.add(O)
        e.wait_ge(self.k("wo_v"), self.v_after_cast(O))
        e.wait_ge(self.k("wo_t"), O + 1)
        st = (O == 0)
        sp = (O == NCH - 1)
        e.matmul(p[2][0:1, :], v.ones_bf[:, 0:1], v.castbuf[O % 2][:, 0:HALF], start=st, stop=sp
                 ).then_inc(self.k("wo_p"), 1)
        e.matmul(p[3][0:1, :], v.ones_bf[:, 0:1], v.castbuf[O % 2][:, HALF:TOK], start=st, stop=sp
                 ).then_inc(self.k("wo_p"), 1)
        e.matmul(p[4][0:1, :], v.ones_bf[:, 0:1], v.sqbuf[O % 2][:, 0:HALF], start=st, stop=sp
                 ).then_inc(self.k("wo_p"), 1)
        e.matmul(p[5][0:1, :], v.ones_bf[:, 0:1], v.sqbuf[O % 2][:, HALF:TOK], start=st, stop=sp
                 ).then_inc(self.k("wo_p"), 1)

    def scalar(self, e):
        v = self.env.t
        for O in range(NCH):
            e.wait_ge(self.k("wo_e"), 2 * O + 2)
            if O >= 2:
                e.wait_ge(self.k("wo_p"), 4 * (O - 2) + 4)
            e.activation(v.sqbuf[O % 2][:], v.stage[:], AF.Square).then_inc(self.k("wo_t"), 1)


class FinalWait:
    def __init__(self, env, sem, val):
        self.env = env
        self.sem = sem
        self.val = val

    def gpsimd(self, e):
        e.wait_ge(self.sem, self.val)

    def sync(self, e):
        pass

    def vector(self, e):
        pass

    def scalar(self, e):
        pass

    def tensor(self, e):
        pass



def build_program(phase_sel=None, io_kinds=None, taps=()):
    io_kinds = io_kinds or {}
    env = Env()
    nc = bass.Bass("TRN2", target_bir_lowering=False)
    env.nc = nc
    ctx = ExitStack()
    env.ctx = ctx

    d = {}
    d["x0T"] = nc.dram_tensor("x0T", [DM, TOK], F32, kind="ExternalInput")
    d["w1p"] = nc.dram_tensor("w1p", [NFB, 128, DM], F32, kind="ExternalInput")
    d["w2p"] = nc.dram_tensor("w2p", [NCH, 128, DFF], F32, kind="ExternalInput")
    d["wqp"] = nc.dram_tensor("wqp", [NCH, 128, DM], F32, kind="ExternalInput")
    d["wkp"] = nc.dram_tensor("wkp", [NCH, 128, DM], F32, kind="ExternalInput")
    d["wvp"] = nc.dram_tensor("wvp", [NCH, 128, DM], F32, kind="ExternalInput")
    d["wop"] = nc.dram_tensor("wop", [NCH, 128, DM], F32, kind="ExternalInput")
    d["b1t"] = nc.dram_tensor("b1t", [128, NFB], F32, kind="ExternalInput")
    d["b2t"] = nc.dram_tensor("b2t", [128, NCH], F32, kind="ExternalInput")
    d["w12slice"] = nc.dram_tensor("w12slice", [16, 128, DM], F32, kind="ExternalInput")
    d["prefix_coef"] = nc.dram_tensor("prefix_coef", [128, 8], F32, kind="ExternalInput")
    d["outT"] = nc.dram_tensor("outT", [DM, TOK], F32, kind="ExternalOutput")
    d["x1d"] = nc.dram_tensor("x1d", [NCH, 128, TOK], F32, kind=io_kinds.get("x1d", "Internal"))
    d["x2d"] = nc.dram_tensor("x2d", [NCH, 128, TOK], F32, kind=io_kinds.get("x2d", "Internal"))
    d["ccs_in"] = nc.dram_tensor("ccs_in", [1, 8], F32)
    d["ccs_out"] = nc.dram_tensor("ccs_out", [1, 8], F32)
    d["cc_si"] = nc.dram_tensor("cc_si", [128, DM], F32)
    d["cc_so"] = nc.dram_tensor("cc_so", [128 * NCORES, DM], F32)
    env.dram = d

    base0 = nc.sbuf_base
    base = (base0 + 63) // 64 * 64
    t = Env()
    env.t = t

    def at(name, shape, dtype, off):
        return nc.alloc_sbuf_tensor_at(name, shape, dtype, offset=base + off)

    t.xln = at("xln", [128, NCH * TOK], BF16, 0)
    t.state = at("state", [128, NH * 128], F32, 0)
    t.h_big = at("h_big", [128, NFB * TOK], BF16, 32768)
    t.big_q = at("big_q", [128, NH * TOK], BF16, 32768)
    t.big_k = at("big_k", [128, NH * TOK], BF16, 32768 + 32768)
    t.big_attn = at("big_attn", [128, NH * TOK], BF16, 32768 + 32768)
    t.big_v = at("big_v", [128, NLOC * DM], BF16, 32768 + 65536)
    t.big_ktm = at("big_ktm", [128, NLOC * DM], BF16, 32768 + 98304)
    t.land = [at(f"land{i}", [128, DM], F32, 163840 + 8192 * i) for i in range(2)]
    t.wq = [at(f"wqb{i}", [128, DM], BF16, 180224 + 4096 * i) for i in range(2)]
    t.u_f32 = at("u_f32", [128, DM], F32, 188416)
    t.roll = at("roll", [128, NH * 128], BF16, 188416)
    t.sm = [at(f"sm{i}", [128, 128], BF16, 188416 + 4096 + 256 * i) for i in range(2)]
    xs = at("xstr", [128, 2 * TOK], F32, 196608)
    t.xstr = [xs[:, 0:TOK], xs[:, TOK:2 * TOK]]
    t.rows = at("rows", [128, TOK], F32, 204800)
    t.stage = at("stage", [128, TOK], F32, 204800)
    cb = at("castbuf", [128, DM], BF16, 0)
    t.castbuf = [cb[:, 0:TOK], cb[:, TOK:2 * TOK]]
    sq = at("sqbuf", [128, DM], BF16, 4096 * 1)
    t.sqbuf = [sq[:, 0:TOK], sq[:, TOK:2 * TOK]]
    t.rowA = at("rowA", [1, TOK], F32, 0)
    t.rowC = at("rowC", [1, TOK], F32, 4096)
    off = 208896
    t.mask = at("mask", [128, 128], F32, off); off += 512
    t.ident = at("ident", [128, 128], BF16, off); off += 256
    t.b1t = at("b1t_s", [128, NFB], F32, off); off += 256
    t.b2t = at("b2t_s", [128, NCH], F32, off); off += 64
    t.halfb2t = at("halfb2t", [128, NCH], F32, off); off += 64
    t.thr = at("thr", [128, 16], F32, off); off += 64
    t.nthr = at("nthr", [128, 16], F32, off); off += 64
    t.eps = at("eps", [128, 1], F32, off); off += 64
    t.scalepart = at("scalepart", [128, 16], F32, off); off += 64
    t.onescol = at("onescol", [128, 1], F32, off); off += 64
    t.ones_bf = at("ones_bf", [128, 1], BF16, off); off += 64
    t.onesrow = at("onesrow", [1, 128], F32, off); off += 512
    t.rowstage = at("rowstage", [1, 16], F32, off); off += 64
    t.coef = at("coef", [128, 8], F32, off); off += 64
    assert base + off <= nc.sbuf_top, (base, off, nc.sbuf_top)
    map_end = base + off

    def xln_view(c):
        return t.xln[:, c * TOK:(c + 1) * TOK]
    env.xln_view = xln_view

    nc.sbuf_base = map_end

    phases = []

    with ctx:
        psums = []
        for i in range(6):
            psums.append(ctx.enter_context(nc.psum_tensor(f"P{i}", [128, HALF], F32)))
        env.p = psums
        env.pt = [ctx.enter_context(nc.psum_tensor(f"T{i}", [128, 128], BF16)) for i in range(2)]
        env.bar_sem = ctx.enter_context(nc.semaphore("bar"))
        out_sem = ctx.enter_context(nc.semaphore("outs"))
        env.fences = {w: [ctx.enter_context(nc.semaphore(f"fence_{w}")), 0]
                      for w in ("v", "t", "p", "g")}

        sel = phase_sel

        def want(name):
            return sel is None or name in sel

        def x0_ap(c):
            return d["x0T"].ap()[c * 128:(c + 1) * 128, :]

        def x1_ap(c):
            return d["x1d"].ap()[c]

        def x2_ap(c):
            return d["x2d"].ap()[c]

        def out_ap(c):
            return d["outT"].ap()[c * 128:(c + 1) * 128, :]

        phases.append(Prologue(env))
        phases.append(Barrier(env))
        if want("ffn1") or want("ffn2"):
            phases.append(ScalePhase(env))
            phases.append(Barrier(env))
        if want("ln1"):
            phases.append(LNStats(env, "l1s", x0_ap))
            phases.append(Barrier(env))
            phases.append(LNRowsApply(env, "l1r", x0_ap))
            phases.append(Barrier(env))
        if want("ffn1"):
            phases.append(FFNP1(env, "f1a", 0))
            phases.append(Barrier(env))
            phases.append(FFNP2(env, "f1b", 1, x0_ap, x1_ap, do_stats=True))
            phases.append(Barrier(env))
        if want("ln2"):
            if not want("ffn1"):
                phases.append(LNStats(env, "l2s", x1_ap))
                phases.append(Barrier(env))
            phases.append(LNRowsApply(env, "l2r", x1_ap))
            phases.append(Barrier(env))
        if want("attn"):
            phases.append(ProjPhase(env))
            phases.append(Barrier(env))
            phases.append(TransposePhase(env))
            phases.append(Barrier(env))
            phases.append(ScanPhase(env))
            phases.append(Barrier(env))
            ag = AGPhase(env)
            phases.append(ag)
            phases.append(Barrier(env))
            phases.append(Pass2Phase(env, ag))
            phases.append(Barrier(env))
            phases.append(WoPhase(env, x1_ap, x2_ap))
            phases.append(Barrier(env))
        if want("ln3"):
            if not want("attn"):
                phases.append(LNStats(env, "l3s", x2_ap))
                phases.append(Barrier(env))
            phases.append(LNRowsApply(env, "l3r", x2_ap))
            phases.append(Barrier(env))
        if want("ffn2"):
            phases.append(FFNP1(env, "f2a", 0))
            phases.append(Barrier(env))
            f2b = FFNP2(env, "f2b", 1, x2_ap, out_ap, do_stats=False)
            phases.append(f2b)
            phases.append(Barrier(env))
            phases.append(FinalWait(env, f2b.k("bs"), 16 * NCH))

        tap_handles = []
        for name in taps:
            src = d[name]
            th = nc.dram_tensor(f"tap_{name}", list(src.shape), src.dtype, kind="ExternalOutput")
            tap_handles.append((th, src))
        if tap_handles:
            phases.append(Barrier(env))

            class TapPhase:
                def gpsimd(self, e):
                    n = 0
                    for th, src in tap_handles:
                        e.dma_start(out=th.ap(), in_=src.ap()).then_inc(out_sem, 16)
                        n += 16
                    e.wait_ge(out_sem, n)

                def sync(self, e): pass
                def vector(self, e): pass
                def scalar(self, e): pass
                def tensor(self, e): pass
            phases.append(TapPhase())

        block = ctx.enter_context(nc.Block())

        @block.sync
        def _(e):
            for ph in phases:
                ph.sync(e)

        @block.gpsimd
        def _(e):
            for ph in phases:
                ph.gpsimd(e)

        @block.vector
        def _(e):
            for ph in phases:
                ph.vector(e)

        @block.scalar
        def _(e):
            for ph in phases:
                ph.scalar(e)

        @block.tensor
        def _(e):
            for ph in phases:
                ph.tensor(e)

    return nc



def stage_inputs(L0_attn_x, L0_attn_Wq, L0_attn_Wk, L0_attn_Wv, L0_attn_Wo,
                 L0_ffn_w1, L0_ffn_b1, L0_ffn_w2, L0_ffn_b2):
    x = np.asarray(L0_attn_x, dtype=np.float32)
    w1 = np.ascontiguousarray(np.asarray(L0_ffn_w1, dtype=np.float32))
    w2 = np.ascontiguousarray(np.asarray(L0_ffn_w2, dtype=np.float32))
    b1 = np.asarray(L0_ffn_b1, dtype=np.float32)
    b2 = np.asarray(L0_ffn_b2, dtype=np.float32)

    def pack_ot(W, nb):
        o, i = W.shape
        return np.ascontiguousarray(
            W.reshape(nb, 128, i // 128, 128).transpose(0, 3, 2, 1).reshape(nb, 128, i))

    w1p = pack_ot(w1, NFB)
    w2p = pack_ot(w2, NCH)
    wqp = pack_ot(np.asarray(L0_attn_Wq, np.float32), NCH)
    wkp = pack_ot(np.asarray(L0_attn_Wk, np.float32), NCH)
    wvp = pack_ot(np.asarray(L0_attn_Wv, np.float32), NCH)
    wop = pack_ot(np.asarray(L0_attn_Wo, np.float32), NCH)
    b1t = np.ascontiguousarray(b1.reshape(NFB, 128).T)
    b2t = np.ascontiguousarray(b2.reshape(NCH, 128).T)

    in_maps = []
    for c in range(NCORES):
        b = c // 4
        t0 = (c % 4) * TOK
        x0T = np.ascontiguousarray(x[b, t0:t0 + TOK, :].T)
        w1sl = w1p[c * 8:(c + 1) * 8]
        w2sl = w2p[c * 2:(c + 1) * 2].reshape(2, 128, 4, 2048).transpose(0, 2, 1, 3).reshape(8, 128, 2048)
        w12slice = np.ascontiguousarray(np.concatenate([w1sl, w2sl], axis=0))
        coef = np.zeros((128, 8), np.float32)
        g0 = 4 * (c // 4)
        for r in range(g0, c):
            coef[:, r] = 1.0
        in_maps.append(dict(
            x0T=x0T, w1p=w1p, w2p=w2p, wqp=wqp, wkp=wkp, wvp=wvp, wop=wop,
            b1t=b1t, b2t=b2t, w12slice=w12slice, prefix_coef=coef,
        ))
    return in_maps


def kernel(**inputs):
    in_maps = stage_inputs(**inputs)
    nc = build_program()
    res = run_bass_kernel_spmd(nc, in_maps, core_ids=list(range(NCORES)))
    out = np.empty((2, 4096, DM), dtype=np.float32)
    for c in range(NCORES):
        b = c // 4
        t0 = (c % 4) * TOK
        out[b, t0:t0 + TOK, :] = res.results[c]["outT"].T
    return out


# version 02
# speedup vs baseline: 1.0158x; 1.0158x over previous
import math
import numpy as np
import ml_dtypes
from contextlib import ExitStack

import concourse.bass as bass
import concourse.mybir as mybir
from concourse.bass_utils import run_bass_kernel_spmd

F32 = mybir.dt.float32
BF16 = mybir.dt.bfloat16
AF = mybir.ActivationFunctionType
ALU = mybir.AluOpType
AX = mybir.AxisListType

TOK = 1024
DM = 2048
DFF = 8192
NCH = 16
NFB = 64
NH = 16
CH = 128
NLOC = 8
HALF = 512
QSCALE = float(1.0 / np.sqrt(128.0))
INV_D = float(1.0 / 2048.0)
MEAN_SCL = float(0.7 / 16777216.0)
LN_EPS = 1e-5

NCORES = 8



FENCE_SAME_ENGINE = True


class Env:
    def fence(self, which, e, ins):
        if not FENCE_SAME_ENGINE:
            return
        sem, cnt = self.fences[which]
        cnt += 1
        self.fences[which][1] = cnt
        ins.then_inc(sem, 1)
        e.wait_ge(sem, cnt)


def _mk_sems(env, names):
    d = {}
    for n in names:
        d[n] = env.ctx.enter_context(env.nc.semaphore(n))
    return d



class Barrier:

    def __init__(self, env):
        env.bar_count = getattr(env, "bar_count", 0) + 1
        self.k = env.bar_count
        self.sem = env.bar_sem

    def _do(self, eng):
        eng.drain().then_inc(self.sem, 1)
        eng.wait_ge(self.sem, 5 * self.k)

    def sync(self, e): self._do(e)
    def gpsimd(self, e): self._do(e)
    def vector(self, e): self._do(e)
    def scalar(self, e): self._do(e)
    def tensor(self, e): self._do(e)


class Prologue:

    def __init__(self, env):
        self.env = env
        self.s = _mk_sems(env, ["pro_d", "pro_g", "pro_v"])

    def sync(self, e):
        v = self.env.t
        e.dma_start(out=v.b1t[:], in_=self.env.dram["b1t"].ap()).then_inc(self.s["pro_d"], 16)
        e.dma_start(out=v.b2t[:], in_=self.env.dram["b2t"].ap()).then_inc(self.s["pro_d"], 16)
        e.dma_start(out=v.coef[:], in_=self.env.dram["prefix_coef"].ap()).then_inc(self.s["pro_d"], 16)

    def gpsimd(self, e):
        v = self.env.t
        e.iota(v.u_f32[:, 0:128], [[1, 128]], channel_multiplier=128,
               allow_small_or_imprecise_dtypes=True)
        e.iota(v.u_f32[:, 128:129], [[1, 1]], channel_multiplier=129,
               allow_small_or_imprecise_dtypes=True).then_inc(self.s["pro_g"], 1)

    def vector(self, e):
        v = self.env.t
        e.wait_ge(self.s["pro_g"], 1)
        e.tensor_scalar(v.mask[:], v.u_f32[:, 0:128], v.u_f32[:, 128:129], None, ALU.is_ge)
        e.tensor_scalar(v.ident[:], v.u_f32[:, 0:128], v.u_f32[:, 128:129], None, ALU.is_equal)
        e.memset(v.eps[:], LN_EPS)
        e.memset(v.rowstage[:, :], 0.0)
        e.memset(v.onescol[:], 1.0)
        e.memset(v.onesrow[:], 1.0)
        e.memset(v.ones_bf[:], 1.0)
        e.wait_ge(self.s["pro_d"], 48)
        e.tensor_scalar_mul(v.halfb2t[:], v.b2t[:], 0.5)

    def scalar(self, e):
        pass

    def tensor(self, e):
        pass


class FrontPhase:

    def __init__(self, env, f1a):
        self.env = env
        self.f1a = f1a
        self.s = _mk_sems(env, ["fr_d", "fr_g", "fr_x0", "fr_a0", "fr_a1", "fr_a2", "fr_a3",
                                "fr_sv", "fr_lv", "fr_lp", "fr_t", "fr_p", "fr_c", "fr_cc"])

    def x0v(self, c):
        return self.env.t.x0big[:, c * TOK:(c + 1) * TOK]

    def sync(self, e):
        v = self.env.t
        for i in range(16):
            if i >= 4:
                e.wait_ge(self.s["fr_sv"], i - 3)
            e.dma_start(out=v.w12slot[i % 4][:], in_=self.env.dram["w12slice"].ap()[i]
                        ).then_inc(self.s[f"fr_a{i % 4}"], 16)
        e.dma_start(out=v.land[0][:], in_=self.env.dram["w1p"].ap()[0]
                    ).then_inc(self.f1a.k("a0"), 16)
        e.dma_start(out=v.land[1][:], in_=self.env.dram["w1p"].ap()[1]
                    ).then_inc(self.f1a.k("a1"), 16)

    def gpsimd(self, e):
        v = self.env.t
        e.dma_start(out=v.x0big[:].rearrange("p (c t) -> p c t", c=NCH),
                    in_=self.env.dram["x0T"].ap().rearrange("(c p) t -> p c t", p=128)
                    ).then_inc(self.s["fr_x0"], 16)
        e.dma_start(out=v.b1t[:], in_=self.env.dram["b1t"].ap()).then_inc(self.s["fr_d"], 16)
        e.dma_start(out=v.b2t[:], in_=self.env.dram["b2t"].ap()).then_inc(self.s["fr_d"], 16)
        e.dma_start(out=v.coef[:], in_=self.env.dram["prefix_coef"].ap()).then_inc(self.s["fr_d"], 16)
        e.iota(v.u_f32[:, 0:128], [[1, 128]], channel_multiplier=128,
               allow_small_or_imprecise_dtypes=True)
        e.iota(v.u_f32[:, 128:129], [[1, 1]], channel_multiplier=129,
               allow_small_or_imprecise_dtypes=True).then_inc(self.s["fr_g"], 1)
        e.wait_ge(self.s["fr_sv"], 18)
        e.dma_start(out=self.env.dram["ccs_in"].ap(), in_=v.rowstage[0:1, 0:8]).then_inc(self.s["fr_c"], 16)
        e.wait_ge(self.s["fr_c"], 16)
        e.collective_compute(
            "AllReduce", ALU.add, replica_groups=[list(range(NCORES))],
            ins=[self.env.dram["ccs_in"].ap().opt()], outs=[self.env.dram["ccs_out"].ap().opt()],
        ).then_inc(self.s["fr_cc"], 1)
        e.wait_ge(self.s["fr_cc"], 1)
        e.dma_start(out=v.rowstage[0:1, 0:8], in_=self.env.dram["ccs_out"].ap()).then_inc(self.s["fr_c"], 16)
        e.wait_ge(self.s["fr_c"], 32)

    def vector(self, e):
        v = self.env.t
        env = self.env
        e.wait_ge(self.s["fr_g"], 1)
        e.tensor_scalar(v.mask[:], v.u_f32[:, 0:128], v.u_f32[:, 128:129], None, ALU.is_ge)
        ins = e.tensor_scalar(v.ident[:], v.u_f32[:, 0:128], v.u_f32[:, 128:129], None, ALU.is_equal)
        env.fence("v", e, ins)
        e.memset(v.eps[:], LN_EPS)
        e.memset(v.rowstage[:, :], 0.0)
        e.memset(v.onescol[:], 1.0)
        e.memset(v.onesrow[:], 1.0)
        e.memset(v.ones_bf[:], 1.0)
        e.wait_ge(self.s["fr_d"], 48)
        e.tensor_scalar_mul(v.halfb2t[:], v.b2t[:], 0.5)
        for i in range(16):
            e.wait_ge(self.s[f"fr_a{i % 4}"], 16 * (i // 4 + 1))
            e.tensor_reduce(v.scalepart[:, i:i + 1], v.w12slot[i % 4][:], AX.X, ALU.add,
                            apply_absolute_value=True).then_inc(self.s["fr_sv"], 1)
        e.wait_ge(self.s["fr_sv"], 16)
        e.tensor_reduce(v.thr[:, 0:1], v.scalepart[:, 0:8], AX.X, ALU.add)
        e.tensor_reduce(v.thr[:, 1:2], v.scalepart[:, 8:16], AX.X, ALU.add).then_inc(self.s["fr_sv"], 1)
        e.wait_ge(self.s["fr_x0"], 16)
        for c in range(NCH):
            if c >= 2:
                e.wait_ge(self.s["fr_lp"], 4 * (c - 2) + 2)
            e.tensor_copy(v.castbuf[c % 2][:], self.x0v(c)).then_inc(self.s["fr_lv"], 1)
        e.wait_ge(self.s["fr_p"], 2)
        e.tensor_copy(v.rowstage[0:1, 0:2], env.p[0][0:1, 0:2]).then_inc(self.s["fr_sv"], 1)
        e.wait_ge(self.s["fr_lp"], 64)
        A, B, C = v.rowA[:, :], v.rows[0:1, :], v.rowC[:, :]
        p = env.p
        for j in range(2):
            ins = e.tensor_scalar_mul(A[:, j * HALF:(j + 1) * HALF], p[2 + j][0:1, :], INV_D)
        env.fence("v", e, ins)
        ins = e.tensor_mul(C[:, :], A[:, :], A[:, :])
        env.fence("v", e, ins)
        for j in range(2):
            ins = e.scalar_tensor_tensor(C[:, j * HALF:(j + 1) * HALF], p[4 + j][0:1, :], INV_D,
                                         C[:, j * HALF:(j + 1) * HALF], ALU.mult, ALU.subtract)
        ins.then_inc(self.s["fr_lv"], 1)
        e.wait_ge(self.s["fr_t"], 17)
        ins = e.reciprocal(B[:, :], B[:, :])
        env.fence("v", e, ins)
        e.tensor_mul(C[:, :], A[:, :], B[:, :]).then_inc(self.s["fr_lv"], 1)
        e.wait_ge(self.s["fr_c"], 32)
        e.tensor_scalar_mul(v.rowstage[0:1, 8:10], v.rowstage[0:1, 0:2], MEAN_SCL).then_inc(self.s["fr_sv"], 1)
        e.wait_ge(self.s["fr_p"], 4)
        e.tensor_copy(v.thr[:, 0:2], env.p[1][:, 0:2])
        e.tensor_scalar_mul(v.nthr[:, 0:2], env.p[1][:, 0:2], -1.0).then_inc(self.s["fr_sv"], 1)
        for c in range(NCH):
            e.wait_ge(self.s["fr_lp"], 68)
            if c >= 1:
                e.wait_ge(self.s["fr_lv"], 18 + c)
            for h in range(2):
                sl = slice(h * HALF, (h + 1) * HALF)
                tins = e.tensor_mul(v.u_f32[:, h * HALF:(h + 1) * HALF], self.x0v(c)[:, sl],
                                    p[2 + h][:, :])
                env.fence("v", e, tins)
                ins = e.tensor_sub(env.xln_view(c)[:, sl],
                                   v.u_f32[:, h * HALF:(h + 1) * HALF], p[4 + h][:, :])
                if h == 0:
                    env.fence("v", e, ins)
            ins.then_inc(self.s["fr_lv"], 1)

    def scalar(self, e):
        v = self.env.t
        e.wait_ge(self.s["fr_x0"], 16)
        for c in range(NCH):
            if c >= 2:
                e.wait_ge(self.s["fr_lp"], 4 * (c - 2) + 4)
            e.activation(v.sqbuf[c % 2][:], self.x0v(c), AF.Square).then_inc(self.s["fr_t"], 1)
        e.wait_ge(self.s["fr_lv"], 17)
        A, B, C = v.rowA[:, :], v.rows[0:1, :], v.rowC[:, :]
        e.activation(B[:, :], C[:, :], AF.Sqrt, bias=v.eps[0:1, 0:1]).then_inc(self.s["fr_t"], 1)

    def tensor(self, e):
        v = self.env.t
        p = self.env.p
        for c in range(NCH):
            e.wait_ge(self.s["fr_lv"], c + 1)
            e.wait_ge(self.s["fr_t"], c + 1)
            st = (c == 0)
            sp = (c == NCH - 1)
            e.matmul(p[2][0:1, :], v.ones_bf[:, 0:1], v.castbuf[c % 2][:, 0:HALF], start=st, stop=sp
                     ).then_inc(self.s["fr_lp"], 1)
            e.matmul(p[3][0:1, :], v.ones_bf[:, 0:1], v.castbuf[c % 2][:, HALF:TOK], start=st, stop=sp
                     ).then_inc(self.s["fr_lp"], 1)
            e.matmul(p[4][0:1, :], v.ones_bf[:, 0:1], v.sqbuf[c % 2][:, 0:HALF], start=st, stop=sp
                     ).then_inc(self.s["fr_lp"], 1)
            e.matmul(p[5][0:1, :], v.ones_bf[:, 0:1], v.sqbuf[c % 2][:, HALF:TOK], start=st, stop=sp
                     ).then_inc(self.s["fr_lp"], 1)
        e.wait_ge(self.s["fr_sv"], 17)
        e.matmul(p[0][0:1, 0:1], v.thr[:, 0:1], v.onescol[:, 0:1], start=True, stop=True
                 ).then_inc(self.s["fr_p"], 1)
        e.matmul(p[0][0:1, 1:2], v.thr[:, 1:2], v.onescol[:, 0:1], start=True, stop=True
                 ).then_inc(self.s["fr_p"], 1)
        e.wait_ge(self.s["fr_lv"], 18)
        A, B, C = v.rowA[:, :], v.rows[0:1, :], v.rowC[:, :]
        for j in range(2):
            e.matmul(p[2 + j][:, :], v.onesrow[0:1, :], B[:, j * HALF:(j + 1) * HALF],
                     start=True, stop=True).then_inc(self.s["fr_lp"], 1)
            e.matmul(p[4 + j][:, :], v.onesrow[0:1, :], C[:, j * HALF:(j + 1) * HALF],
                     start=True, stop=True).then_inc(self.s["fr_lp"], 1)
        e.wait_ge(self.s["fr_sv"], 19)
        e.matmul(p[1][:, 0:1], v.onesrow[0:1, :], v.rowstage[0:1, 8:9], start=True, stop=True
                 ).then_inc(self.s["fr_p"], 1)
        e.matmul(p[1][:, 1:2], v.onesrow[0:1, :], v.rowstage[0:1, 9:10], start=True, stop=True
                 ).then_inc(self.s["fr_p"], 1)


class ScalePhase:

    def __init__(self, env):
        self.env = env
        self.s = _mk_sems(env, ["sc_a0", "sc_a1", "sc_v", "sc_p", "sc_c", "sc_cc"])

    def sync(self, e):
        v = self.env.t
        for i in range(16):
            if i >= 2:
                e.wait_ge(self.s["sc_v"], i - 1)
            e.dma_start(out=v.land[i % 2][:], in_=self.env.dram["w12slice"].ap()[i]
                        ).then_inc(self.s[f"sc_a{i % 2}"], 16)

    def vector(self, e):
        v = self.env.t
        for i in range(16):
            e.wait_ge(self.s[f"sc_a{i % 2}"], 16 * (i // 2 + 1))
            e.tensor_reduce(v.scalepart[:, i:i + 1], v.land[i % 2][:], AX.X, ALU.add,
                            apply_absolute_value=True).then_inc(self.s["sc_v"], 1)
        e.wait_ge(self.s["sc_v"], 16)
        e.tensor_reduce(v.thr[:, 0:1], v.scalepart[:, 0:8], AX.X, ALU.add)
        e.tensor_reduce(v.thr[:, 1:2], v.scalepart[:, 8:16], AX.X, ALU.add).then_inc(self.s["sc_v"], 1)
        e.wait_ge(self.s["sc_p"], 2)
        e.tensor_copy(v.rowstage[0:1, 0:2], self.env.p[2][0:1, 0:2]).then_inc(self.s["sc_v"], 1)
        e.wait_ge(self.s["sc_c"], 32)
        e.tensor_scalar_mul(v.rowstage[0:1, 8:10], v.rowstage[0:1, 0:2], MEAN_SCL).then_inc(self.s["sc_v"], 1)
        e.wait_ge(self.s["sc_p"], 4)
        e.tensor_copy(v.thr[:, 0:2], self.env.p[3][:, 0:2])
        e.tensor_scalar_mul(v.nthr[:, 0:2], self.env.p[3][:, 0:2], -1.0).then_inc(self.s["sc_v"], 1)

    def tensor(self, e):
        v = self.env.t
        e.wait_ge(self.s["sc_v"], 17)
        e.matmul(self.env.p[2][0:1, 0:1], v.thr[:, 0:1], v.onescol[:, 0:1], start=True, stop=True
                 ).then_inc(self.s["sc_p"], 1)
        e.matmul(self.env.p[2][0:1, 1:2], v.thr[:, 1:2], v.onescol[:, 0:1], start=True, stop=True
                 ).then_inc(self.s["sc_p"], 1)
        e.wait_ge(self.s["sc_v"], 19)
        e.matmul(self.env.p[3][:, 0:1], v.onesrow[0:1, :], v.rowstage[0:1, 8:9], start=True, stop=True
                 ).then_inc(self.s["sc_p"], 1)
        e.matmul(self.env.p[3][:, 1:2], v.onesrow[0:1, :], v.rowstage[0:1, 9:10], start=True, stop=True
                 ).then_inc(self.s["sc_p"], 1)

    def gpsimd(self, e):
        v = self.env.t
        e.wait_ge(self.s["sc_v"], 18)
        e.dma_start(out=self.env.dram["ccs_in"].ap(), in_=v.rowstage[0:1, 0:8]).then_inc(self.s["sc_c"], 16)
        e.wait_ge(self.s["sc_c"], 16)
        e.collective_compute(
            "AllReduce", ALU.add, replica_groups=[list(range(NCORES))],
            ins=[self.env.dram["ccs_in"].ap().opt()], outs=[self.env.dram["ccs_out"].ap().opt()],
        ).then_inc(self.s["sc_cc"], 1)
        e.wait_ge(self.s["sc_cc"], 1)
        e.dma_start(out=v.rowstage[0:1, 0:8], in_=self.env.dram["ccs_out"].ap()).then_inc(self.s["sc_c"], 16)
        e.wait_ge(self.s["sc_c"], 32)

    def scalar(self, e):
        pass


class LNStats:

    def __init__(self, env, tag, src_ap):
        self.env = env
        self.src_ap = src_ap
        self.s = _mk_sems(env, [f"{tag}_b0", f"{tag}_b1", f"{tag}_v", f"{tag}_t", f"{tag}_p"])
        self.tag = tag

    def k(self, n):
        return self.s[f"{self.tag}_{n}"]

    def bwait(self, e, c):
        e.wait_ge(self.k(f"b{c % 2}"), 16 * (c // 2 + 1))

    def gpsimd(self, e):
        v = self.env.t
        for c in range(NCH):
            if c >= 2:
                e.wait_ge(self.k("v"), c - 1)
                e.wait_ge(self.k("t"), c - 1)
            e.dma_start(out=v.xstr[c % 2][:], in_=self.src_ap(c)).then_inc(self.k(f"b{c % 2}"), 16)

    def vector(self, e):
        v = self.env.t
        for c in range(NCH):
            self.bwait(e, c)
            if c >= 2:
                e.wait_ge(self.k("p"), 4 * (c - 2) + 2)
            e.tensor_copy(v.castbuf[c % 2][:], v.xstr[c % 2][:]).then_inc(self.k("v"), 1)

    def scalar(self, e):
        v = self.env.t
        for c in range(NCH):
            self.bwait(e, c)
            if c >= 2:
                e.wait_ge(self.k("p"), 4 * (c - 2) + 4)
            e.activation(v.sqbuf[c % 2][:], v.xstr[c % 2][:], AF.Square).then_inc(self.k("t"), 1)

    def tensor(self, e):
        v = self.env.t
        p = self.env.p
        for c in range(NCH):
            e.wait_ge(self.k("v"), c + 1)
            e.wait_ge(self.k("t"), c + 1)
            st = (c == 0)
            sp = (c == NCH - 1)
            e.matmul(p[2][0:1, :], v.ones_bf[:, 0:1], v.castbuf[c % 2][:, 0:HALF], start=st, stop=sp
                     ).then_inc(self.k("p"), 1)
            e.matmul(p[3][0:1, :], v.ones_bf[:, 0:1], v.castbuf[c % 2][:, HALF:TOK], start=st, stop=sp
                     ).then_inc(self.k("p"), 1)
            e.matmul(p[4][0:1, :], v.ones_bf[:, 0:1], v.sqbuf[c % 2][:, 0:HALF], start=st, stop=sp
                     ).then_inc(self.k("p"), 1)
            e.matmul(p[5][0:1, :], v.ones_bf[:, 0:1], v.sqbuf[c % 2][:, HALF:TOK], start=st, stop=sp
                     ).then_inc(self.k("p"), 1)

    def sync(self, e):
        pass


class LNRowsApply:

    def __init__(self, env, tag, src_ap):
        self.env = env
        self.src_ap = src_ap
        self.s = _mk_sems(env, [f"{tag}_b0", f"{tag}_b1", f"{tag}_v", f"{tag}_t", f"{tag}_p"])
        self.tag = tag

    def k(self, n):
        return self.s[f"{self.tag}_{n}"]

    def _rows(self):
        v = self.env.t
        return v.rowA[:, :], v.rows[0:1, :], v.rowC[:, :]

    def vector(self, e):
        v = self.env.t
        p = self.env.p
        A, B, C = self._rows()
        for j in range(2):
            ins = e.tensor_scalar_mul(A[:, j * HALF:(j + 1) * HALF], p[2 + j][0:1, :], INV_D)
        self.env.fence("v", e, ins)
        ins = e.tensor_mul(C[:, :], A[:, :], A[:, :])
        self.env.fence("v", e, ins)
        for j in range(2):
            ins = e.scalar_tensor_tensor(C[:, j * HALF:(j + 1) * HALF], p[4 + j][0:1, :], INV_D,
                                         C[:, j * HALF:(j + 1) * HALF], ALU.mult, ALU.subtract)
        ins.then_inc(self.k("v"), 1)
        e.wait_ge(self.k("t"), 1)
        ins = e.reciprocal(B[:, :], B[:, :])
        self.env.fence("v", e, ins)
        e.tensor_mul(C[:, :], A[:, :], B[:, :]).then_inc(self.k("v"), 1)
        for c in range(NCH):
            e.wait_ge(self.k(f"b{c % 2}"), 16 * (c // 2 + 1))
            e.wait_ge(self.k("p"), 4)
            if c >= 1:
                e.wait_ge(self.k("v"), 2 + c)
            for h in range(2):
                sl = slice(h * HALF, (h + 1) * HALF)
                tins = e.tensor_mul(v.u_f32[:, h * HALF:(h + 1) * HALF], v.xstr[c % 2][:, sl],
                                    p[2 + h][:, :])
                self.env.fence("v", e, tins)
                ins = e.tensor_sub(self.env.xln_view(c)[:, sl],
                                   v.u_f32[:, h * HALF:(h + 1) * HALF], p[4 + h][:, :])
                if h == 0:
                    self.env.fence("v", e, ins)
            ins.then_inc(self.k("v"), 1)

    def scalar(self, e):
        v = self.env.t
        A, B, C = self._rows()
        e.wait_ge(self.k("v"), 1)
        e.activation(B[:, :], C[:, :], AF.Sqrt, bias=v.eps[0:1, 0:1]).then_inc(self.k("t"), 1)

    def tensor(self, e):
        v = self.env.t
        p = self.env.p
        A, B, C = self._rows()
        e.wait_ge(self.k("v"), 2)
        for j in range(2):
            e.matmul(p[2 + j][:, :], v.onesrow[0:1, :], B[:, j * HALF:(j + 1) * HALF],
                     start=True, stop=True).then_inc(self.k("p"), 1)
            e.matmul(p[4 + j][:, :], v.onesrow[0:1, :], C[:, j * HALF:(j + 1) * HALF],
                     start=True, stop=True).then_inc(self.k("p"), 1)

    def gpsimd(self, e):
        v = self.env.t
        for c in range(NCH):
            if c >= 2:
                e.wait_ge(self.k("v"), 2 + c - 1)
            e.dma_start(out=v.xstr[c % 2][:], in_=self.src_ap(c)).then_inc(self.k(f"b{c % 2}"), 16)

    def sync(self, e):
        pass


class FFNP1:

    def __init__(self, env, tag, thr_col, prefetched=False):
        self.env = env
        self.thr_col = thr_col
        self.prefetched = prefetched
        self.s = _mk_sems(env, [f"{tag}_a0", f"{tag}_a1", f"{tag}_q", f"{tag}_m", f"{tag}_g"])
        self.tag = tag

    def k(self, n):
        return self.s[f"{self.tag}_{n}"]

    def sync(self, e):
        v = self.env.t
        for F in range(2 if self.prefetched else 0, NFB):
            if F >= 2:
                e.wait_ge(self.k("q"), F - 1)
            e.dma_start(out=v.land[F % 2][:], in_=self.env.dram["w1p"].ap()[F]
                        ).then_inc(self.k(f"a{F % 2}"), 16)

    def vector(self, e):
        v = self.env.t
        tc = self.thr_col
        for F in range(NFB):
            e.wait_ge(self.k(f"a{F % 2}"), 16 * (F // 2 + 1))
            if F >= 2:
                e.wait_ge(self.k("m"), 2 * (F - 2) + 2)
            if F >= 1:
                e.wait_ge(self.k("q"), F)
            uins = e.tensor_scalar(v.u_f32[:], v.land[F % 2][:], v.nthr[:, tc:tc + 1], None, ALU.is_le)
            self.env.fence("v", e, uins)
            e.scalar_tensor_tensor(v.wq[F % 2][:], v.land[F % 2][:], v.thr[:, tc:tc + 1],
                                   v.u_f32[:], ALU.is_ge, ALU.subtract).then_inc(self.k("q"), 1)

    def tensor(self, e):
        v = self.env.t
        p = self.env.p
        for F in range(NFB):
            e.wait_ge(self.k("q"), F + 1)
            for h in range(2):
                if F >= 1:
                    e.wait_ge(self.k("g"), 2 * (F - 1) + h + 1)
                for c in range(NCH):
                    mm = e.matmul(p[h][:, :], v.wq[F % 2][:, c * 128:(c + 1) * 128],
                                  self.env.xln_view(c)[:, h * HALF:(h + 1) * HALF],
                                  start=(c == 0), stop=(c == NCH - 1))
                mm.then_inc(self.k("m"), 1)

    def scalar(self, e):
        v = self.env.t
        p = self.env.p
        for F in range(NFB):
            for h in range(2):
                e.wait_ge(self.k("m"), 2 * F + h + 1)
                e.activation(v.h_big[:, F * TOK + h * HALF: F * TOK + (h + 1) * HALF],
                             p[h][:, :], AF.Gelu, bias=v.b1t[:, F:F + 1], scale=1.0
                             ).then_inc(self.k("g"), 1)

    def gpsimd(self, e):
        pass


class FFNP2:

    def __init__(self, env, tag, thr_col, x_src_ap, out_ap, do_stats):
        self.env = env
        self.thr_col = thr_col
        self.x_src_ap = x_src_ap
        self.out_ap = out_ap
        self.do_stats = do_stats
        self.banks4 = not do_stats
        self.s = _mk_sems(env, [f"{tag}_a0", f"{tag}_a1", f"{tag}_q", f"{tag}_m", f"{tag}_e",
                                f"{tag}_bl0", f"{tag}_bl1", f"{tag}_bs",
                                f"{tag}_v", f"{tag}_t", f"{tag}_p"])
        self.tag = tag

    def k(self, n):
        return self.s[f"{self.tag}_{n}"]

    def load_wait(self, e, D):
        e.wait_ge(self.k(f"bl{D % 2}"), 16 * (D // 2 + 1))

    def gpsimd(self, e):
        v = self.env.t
        e.dma_start(out=v.xstr[0][:], in_=self.x_src_ap(0)).then_inc(self.k("bl0"), 16)
        e.dma_start(out=v.xstr[1][:], in_=self.x_src_ap(1)).then_inc(self.k("bl1"), 16)
        for D in range(NCH):
            e.wait_ge(self.k("e"), 2 * D + 2)
            if self.do_stats:
                e.wait_ge(self.k("v"), self.v_after_cast(D))
                e.wait_ge(self.k("t"), D + 1)
            e.dma_start(out=self.out_ap(D), in_=v.stage[:]).then_inc(self.k("bs"), 16)
            if D + 2 < NCH:
                e.dma_start(out=v.xstr[(D + 2) % 2][:], in_=self.x_src_ap(D + 2)
                            ).then_inc(self.k(f"bl{D % 2}"), 16)
        e.wait_ge(self.k("bs"), 16 * NCH)

    def v_after_cast(self, D):
        assert self.do_stats
        return 2 + 2 * D if D < NCH - 1 else 2 * NCH

    def v_after_xp(self, Dp):
        if Dp == 0:
            return 1
        if self.do_stats:
            return 2 * Dp + 1
        return Dp + 1

    def vector(self, e):
        v = self.env.t
        tc = self.thr_col
        self.load_wait(e, 0)
        e.tensor_scalar(v.xstr[0][:], v.xstr[0][:], v.halfb2t[:, 0:1], None, ALU.add
                        ).then_inc(self.k("v"), 1)
        for D in range(NCH):
            if D >= 1:
                self._emit_evac(e, D - 1)
            for qu in range(4):
                kk = 4 * D + qu
                e.wait_ge(self.k(f"a{kk % 2}"), 16 * (kk // 2 + 1))
                if kk >= 2:
                    e.wait_ge(self.k("m"), kk - 1)
                if kk >= 1:
                    e.wait_ge(self.k("q"), kk)
                uins = e.tensor_scalar(v.u_f32[:], v.land[kk % 2][:], v.nthr[:, tc:tc + 1], None, ALU.is_le)
                self.env.fence("v", e, uins)
                e.scalar_tensor_tensor(v.wq[kk % 2][:], v.land[kk % 2][:], v.thr[:, tc:tc + 1],
                                       v.u_f32[:], ALU.is_ge, ALU.subtract).then_inc(self.k("q"), 1)
        self._emit_evac(e, NCH - 1)

    def _emit_evac(self, e, D):
        v = self.env.t
        if D in getattr(self, "_evac_done", set()):
            return
        self._evac_done = getattr(self, "_evac_done", set())
        self._evac_done.add(D)
        if D >= 1:
            e.wait_ge(self.k("bs"), 16 * D)
        e.wait_ge(self.k("v"), self.v_after_xp(D))
        e.wait_ge(self.k("m"), 4 * (D + 1))
        for h in range(2):
            pb = self.env.p[2 * (D % 2) + h] if self.banks4 else self.env.p[h]
            e.scalar_tensor_tensor(v.stage[:, h * HALF:(h + 1) * HALF], pb[:, :], 0.5,
                                   v.xstr[D % 2][:, h * HALF:(h + 1) * HALF], ALU.mult, ALU.add
                                   ).then_inc(self.k("e"), 1)
        if self.do_stats:
            if D >= 2:
                e.wait_ge(self.k("p"), 4 * (D - 2) + 2)
            e.wait_ge(self.k("e"), 2 * D + 2)
            e.tensor_copy(self.env.t.castbuf[D % 2][:], v.stage[:]).then_inc(self.k("v"), 1)
        if D + 1 < NCH:
            self.load_wait(e, D + 1)
            e.tensor_scalar(v.xstr[(D + 1) % 2][:], v.xstr[(D + 1) % 2][:],
                            v.halfb2t[:, D + 1:D + 2], None, ALU.add).then_inc(self.k("v"), 1)

    def tensor(self, e):
        v = self.env.t
        p = self.env.p
        for D in range(NCH):
            if self.banks4:
                if D >= 2:
                    e.wait_ge(self.k("e"), 2 * (D - 2) + 2)
                pb0 = p[2 * (D % 2)]
                pb1 = p[2 * (D % 2) + 1]
            else:
                if D >= 1:
                    e.wait_ge(self.k("e"), 2 * (D - 1) + 2)
                pb0, pb1 = p[0], p[1]
            for qu in range(4):
                kk = 4 * D + qu
                e.wait_ge(self.k("q"), kk + 1)
                for i in range(16):
                    F = qu * 16 + i
                    st = (qu == 0 and i == 0)
                    sp = (qu == 3 and i == 15)
                    e.matmul(pb0[:, :], v.wq[kk % 2][:, i * 128:(i + 1) * 128],
                             v.h_big[:, F * TOK: F * TOK + HALF], start=st, stop=sp)
                    mm = e.matmul(pb1[:, :], v.wq[kk % 2][:, i * 128:(i + 1) * 128],
                                  v.h_big[:, F * TOK + HALF: F * TOK + TOK], start=st, stop=sp)
                mm.then_inc(self.k("m"), 1)
            if self.do_stats and D >= 1:
                self._emit_stats(e, D - 1)
        if self.do_stats:
            self._emit_stats(e, NCH - 1)

    def _emit_stats(self, e, D):
        v = self.env.t
        p = self.env.p
        if D in getattr(self, "_stats_done", set()):
            return
        self._stats_done = getattr(self, "_stats_done", set())
        self._stats_done.add(D)
        e.wait_ge(self.k("v"), self.v_after_cast(D))
        e.wait_ge(self.k("t"), D + 1)
        st = (D == 0)
        sp = (D == NCH - 1)
        e.matmul(p[2][0:1, :], v.ones_bf[:, 0:1], v.castbuf[D % 2][:, 0:HALF], start=st, stop=sp
                 ).then_inc(self.k("p"), 1)
        e.matmul(p[3][0:1, :], v.ones_bf[:, 0:1], v.castbuf[D % 2][:, HALF:TOK], start=st, stop=sp
                 ).then_inc(self.k("p"), 1)
        e.matmul(p[4][0:1, :], v.ones_bf[:, 0:1], v.sqbuf[D % 2][:, 0:HALF], start=st, stop=sp
                 ).then_inc(self.k("p"), 1)
        e.matmul(p[5][0:1, :], v.ones_bf[:, 0:1], v.sqbuf[D % 2][:, HALF:TOK], start=st, stop=sp
                 ).then_inc(self.k("p"), 1)

    def scalar(self, e):
        if not self.do_stats:
            return
        v = self.env.t
        for D in range(NCH):
            e.wait_ge(self.k("e"), 2 * D + 2)
            if D >= 2:
                e.wait_ge(self.k("p"), 4 * (D - 2) + 4)
            e.activation(v.sqbuf[D % 2][:], v.stage[:], AF.Square).then_inc(self.k("t"), 1)

    def sync(self, e):
        v = self.env.t
        for D in range(NCH):
            for qu in range(4):
                kk = 4 * D + qu
                if kk >= 2:
                    e.wait_ge(self.k("q"), kk - 1)
                e.dma_start(out=v.land[kk % 2][:],
                            in_=self.env.dram["w2p"].ap()[D][:, qu * 2048:(qu + 1) * 2048]
                            ).then_inc(self.k(f"a{kk % 2}"), 16)


class ProjPhase:

    def __init__(self, env):
        self.env = env
        self.s = _mk_sems(env, ["pj_b0", "pj_b1", "pj_m", "pj_t"])

    def k(self, n):
        return self.s[n]

    def _mats(self):
        v = self.env.t
        return [("wqp", v.big_q, AF.Identity, QSCALE),
                ("wkp", v.big_k, AF.Sign, 1.0),
                ("wvp", v.big_ktm, AF.Sign, 1.0)]

    def gpsimd(self, e):
        v = self.env.t
        for mi, (wname, _, _, _) in enumerate(self._mats()):
            for O in range(NCH):
                kk = mi * NCH + O
                if kk >= 2:
                    e.wait_ge(self.k("pj_m"), 2 * (kk - 2) + 2)
                e.dma_start(out=v.wq[kk % 2][:], in_=self.env.dram[wname].ap()[O]
                            ).then_inc(self.k(f"pj_b{kk % 2}"), 16)

    def tensor(self, e):
        v = self.env.t
        p = self.env.p
        for mi, (_, _, _, _) in enumerate(self._mats()):
            for O in range(NCH):
                kk = mi * NCH + O
                e.wait_ge(self.k(f"pj_b{kk % 2}"), 16 * (kk // 2 + 1))
                for h in range(2):
                    if kk >= 1:
                        e.wait_ge(self.k("pj_t"), 2 * (kk - 1) + h + 1)
                    for c in range(NCH):
                        mm = e.matmul(p[h][:, :], v.wq[kk % 2][:, c * 128:(c + 1) * 128],
                                      self.env.xln_view(c)[:, h * HALF:(h + 1) * HALF],
                                      start=(c == 0), stop=(c == NCH - 1))
                    mm.then_inc(self.k("pj_m"), 1)

    def scalar(self, e):
        v = self.env.t
        p = self.env.p
        for mi, (_, dst, func, scl) in enumerate(self._mats()):
            for O in range(NCH):
                kk = mi * NCH + O
                for h in range(2):
                    e.wait_ge(self.k("pj_m"), 2 * kk + h + 1)
                    e.activation(dst[:, O * TOK + h * HALF: O * TOK + (h + 1) * HALF],
                                 p[h][:, :], func, scale=scl).then_inc(self.k("pj_t"), 1)

    def sync(self, e):
        pass

    def vector(self, e):
        pass


class TransposePhase:

    def __init__(self, env):
        self.env = env
        self.s = _mk_sems(env, ["tr_p", "tr_v"])

    def _pairs(self):
        v = self.env.t
        return [(v.big_ktm, v.big_v), (v.big_k, v.big_ktm)]

    def tensor(self, e):
        v = self.env.t
        idx = 0
        for (src, dst) in self._pairs():
            for E in range(NCH):
                for tt in range(NLOC):
                    if idx >= 2:
                        e.wait_ge(self.s["tr_v"], idx - 1)
                    e.transpose(self.env.pt[idx % 2][:, :],
                                src[:, E * TOK + tt * 128: E * TOK + (tt + 1) * 128],
                                v.ident[:, :]).then_inc(self.s["tr_p"], 1)
                    idx += 1

    def vector(self, e):
        idx = 0
        for (src, dst) in self._pairs():
            for E in range(NCH):
                for tt in range(NLOC):
                    e.wait_ge(self.s["tr_p"], idx + 1)
                    e.tensor_copy(dst[:, tt * DM + E * 128: tt * DM + (E + 1) * 128],
                                  self.env.pt[idx % 2][:, :]).then_inc(self.s["tr_v"], 1)
                    idx += 1

    def sync(self, e):
        pass

    def gpsimd(self, e):
        pass

    def scalar(self, e):
        pass


class ScanPhase:

    def __init__(self, env):
        self.env = env
        self.s = _mk_sems(env, ["sn_S", "sn_O", "sn_K", "sn_mk", "sn_ad", "sn_ev", "sn_ct", "sn_in"])

    def vector(self, e):
        v = self.env.t
        e.memset(v.state[:], 0.0).then_inc(self.s["sn_in"], 1)
        for n in range(NLOC):
            for h in range(NH):
                idx = n * NH + h
                pi = idx % 2
                e.wait_ge(self.s["sn_S"], idx + 1)
                if idx >= 2:
                    e.wait_ge(self.s["sn_O"], idx - 1)
                e.tensor_mul(v.sm[pi][:, :], self.env.p[2 + pi][:, 0:128], v.mask[:, :]
                             ).then_inc(self.s["sn_mk"], 1)
                e.wait_ge(self.s["sn_K"], idx + 1)
                if n == 0:
                    e.wait_ge(self.s["sn_in"], 1)
                else:
                    e.wait_ge(self.s["sn_ad"], (n - 1) * NH + h + 1)
                e.tensor_add(v.state[:, h * 128:(h + 1) * 128], v.state[:, h * 128:(h + 1) * 128],
                             self.env.p[4 + pi][:, 0:128]).then_inc(self.s["sn_ad"], 1)

    def tensor(self, e):
        v = self.env.t
        p = self.env.p
        e.wait_ge(self.s["sn_in"], 1)
        for n in range(NLOC):
            for h in range(NH):
                idx = n * NH + h
                pi = idx % 2
                qv = v.big_q[:, h * TOK + n * 128: h * TOK + (n + 1) * 128]
                kv_fm = v.big_k[:, h * TOK + n * 128: h * TOK + (n + 1) * 128]
                vv_tm = v.big_v[:, n * DM + h * 128: n * DM + (h + 1) * 128]
                kv_tm = v.big_ktm[:, n * DM + h * 128: n * DM + (h + 1) * 128]
                if idx >= 2:
                    e.wait_ge(self.s["sn_ev"], idx - 1)
                if n > 0:
                    e.wait_ge(self.s["sn_ct"], (n - 1) * NH + h + 1)
                    e.matmul(p[pi][:, 0:128], v.roll[:, h * 128:(h + 1) * 128], qv,
                             start=True, stop=False)
                if idx >= 2:
                    e.wait_ge(self.s["sn_mk"], idx - 1)
                e.matmul(p[2 + pi][:, 0:128], kv_fm, qv, start=True, stop=True
                         ).then_inc(self.s["sn_S"], 1)
                if idx >= 2:
                    e.wait_ge(self.s["sn_ad"], idx - 1)
                e.matmul(p[4 + pi][:, 0:128], kv_tm, vv_tm, start=True, stop=True
                         ).then_inc(self.s["sn_K"], 1)
                e.wait_ge(self.s["sn_mk"], idx + 1)
                e.matmul(p[pi][:, 0:128], vv_tm, v.sm[pi][:, :],
                         start=(n == 0), stop=True).then_inc(self.s["sn_O"], 1)

    def scalar(self, e):
        v = self.env.t
        for n in range(NLOC):
            for h in range(NH):
                idx = n * NH + h
                pi = idx % 2
                e.wait_ge(self.s["sn_O"], idx + 1)
                e.activation(v.big_attn[:, h * TOK + n * 128: h * TOK + (n + 1) * 128],
                             self.env.p[pi][:, 0:128], AF.Identity).then_inc(self.s["sn_ev"], 1)
                e.wait_ge(self.s["sn_ad"], idx + 1)
                e.activation(v.roll[:, h * 128:(h + 1) * 128],
                             v.state[:, h * 128:(h + 1) * 128], AF.Identity
                             ).then_inc(self.s["sn_ct"], 1)

    def sync(self, e):
        pass

    def gpsimd(self, e):
        pass


class AGPhase:

    def __init__(self, env):
        self.env = env
        self.s = _mk_sems(env, ["ag_c", "ag_cc", "ag_g0", "ag_g1", "ag_v"])

    def gpsimd(self, e):
        v = self.env.t
        e.dma_start(out=self.env.dram["cc_si"].ap(), in_=v.state[:]).then_inc(self.s["ag_c"], 16)
        e.wait_ge(self.s["ag_c"], 16)
        e.collective_compute(
            "AllGather", ALU.bypass, replica_groups=[list(range(NCORES))],
            ins=[self.env.dram["cc_si"].ap().opt()], outs=[self.env.dram["cc_so"].ap().opt()],
        ).then_inc(self.s["ag_cc"], 1)
        e.wait_ge(self.s["ag_cc"], 1)
        for r in range(NCORES):
            if r >= 2:
                e.wait_ge(self.s["ag_v"], r)
            e.dma_start(out=v.land[r % 2][:],
                        in_=self.env.dram["cc_so"].ap()[r * 128:(r + 1) * 128, :]
                        ).then_inc(self.s[f"ag_g{r % 2}"], 16)

    def vector(self, e):
        v = self.env.t
        e.wait_ge(self.s["ag_c"], 16)
        e.memset(v.state[:], 0.0).then_inc(self.s["ag_v"], 1)
        for r in range(NCORES):
            e.wait_ge(self.s[f"ag_g{r % 2}"], 16 * (r // 2 + 1))
            e.wait_ge(self.s["ag_v"], 1 + r)
            e.scalar_tensor_tensor(v.state[:], v.land[r % 2][:], v.coef[:, r:r + 1],
                                   v.state[:], ALU.mult, ALU.add).then_inc(self.s["ag_v"], 1)
        e.wait_ge(self.s["ag_v"], 1 + NCORES)
        for h in range(NH):
            ins = e.tensor_copy(v.roll[:, h * 128:(h + 1) * 128], v.state[:, h * 128:(h + 1) * 128])
        ins.then_inc(self.s["ag_v"], 1)

    def sync(self, e):
        pass

    def scalar(self, e):
        pass

    def tensor(self, e):
        pass


class Pass2Phase:

    def __init__(self, env, ag):
        self.env = env
        self.ag = ag
        self.s = _mk_sems(env, ["p2_m", "p2_v"])

    def tensor(self, e):
        v = self.env.t
        p = self.env.p
        e.wait_ge(self.ag.s["ag_v"], NCORES + 2)
        for h in range(NH):
            for hf in range(2):
                idx = 2 * h + hf
                if idx >= 2:
                    e.wait_ge(self.s["p2_v"], idx - 1)
                e.matmul(p[idx % 2][:, :], v.roll[:, h * 128:(h + 1) * 128],
                         v.big_q[:, h * TOK + hf * HALF: h * TOK + (hf + 1) * HALF],
                         start=True, stop=True).then_inc(self.s["p2_m"], 1)

    def vector(self, e):
        v = self.env.t
        for h in range(NH):
            for hf in range(2):
                idx = 2 * h + hf
                e.wait_ge(self.s["p2_m"], idx + 1)
                sl = slice(h * TOK + hf * HALF, h * TOK + (hf + 1) * HALF)
                e.tensor_add(v.big_attn[:, sl], v.big_attn[:, sl], self.env.p[idx % 2][:, :]
                             ).then_inc(self.s["p2_v"], 1)

    def sync(self, e):
        pass

    def gpsimd(self, e):
        pass

    def scalar(self, e):
        pass


class WoPhase:

    def __init__(self, env, x_src_ap, out_ap):
        self.env = env
        self.x_src_ap = x_src_ap
        self.out_ap = out_ap
        self.s = _mk_sems(env, ["wo_a0", "wo_a1", "wo_m", "wo_e", "wo_bl0", "wo_bl1",
                                "wo_bs", "wo_v", "wo_t", "wo_p"])

    def k(self, n):
        return self.s[n]

    def load_wait(self, e, O):
        e.wait_ge(self.k(f"wo_bl{O % 2}"), 16 * (O // 2 + 1))

    def gpsimd(self, e):
        v = self.env.t
        e.dma_start(out=v.xstr[0][:], in_=self.x_src_ap(0)).then_inc(self.k("wo_bl0"), 16)
        e.dma_start(out=v.xstr[1][:], in_=self.x_src_ap(1)).then_inc(self.k("wo_bl1"), 16)
        for O in range(NCH):
            e.wait_ge(self.k("wo_e"), 2 * O + 2)
            e.wait_ge(self.k("wo_v"), self.v_after_cast(O))
            e.wait_ge(self.k("wo_t"), O + 1)
            e.dma_start(out=self.out_ap(O), in_=v.stage[:]).then_inc(self.k("wo_bs"), 16)
            if O + 2 < NCH:
                e.dma_start(out=v.xstr[(O + 2) % 2][:], in_=self.x_src_ap(O + 2)
                            ).then_inc(self.k(f"wo_bl{O % 2}"), 16)
        e.wait_ge(self.k("wo_bs"), 16 * NCH)

    def sync(self, e):
        v = self.env.t
        for O in range(NCH):
            if O >= 2:
                e.wait_ge(self.k("wo_m"), 2 * (O - 2) + 2)
            e.dma_start(out=v.land[O % 2][:], in_=self.env.dram["wop"].ap()[O]
                        ).then_inc(self.k(f"wo_a{O % 2}"), 16)

    def v_after_cast(self, O):
        return 2 * O + 3 if O < NCH - 1 else 2 * NCH

    def vector(self, e):
        v = self.env.t
        for O in range(NCH):
            e.wait_ge(self.k(f"wo_a{O % 2}"), 16 * (O // 2 + 1))
            if O >= 2:
                e.wait_ge(self.k("wo_m"), 2 * (O - 2) + 2)
            e.tensor_copy(v.wq[O % 2][:], v.land[O % 2][:]).then_inc(self.k("wo_v"), 1)
            if O >= 1:
                self._emit_evac(e, O - 1)
        self._emit_evac(e, NCH - 1)

    def _emit_evac(self, e, O):
        v = self.env.t
        done = getattr(self, "_evd", set())
        self._evd = done
        if O in done:
            return
        done.add(O)
        if O >= 1:
            e.wait_ge(self.k("wo_bs"), 16 * O)
        self.load_wait(e, O)
        for h in range(2):
            e.wait_ge(self.k("wo_m"), 2 * O + h + 1)
            e.tensor_add(v.stage[:, h * HALF:(h + 1) * HALF], v.xstr[O % 2][:, h * HALF:(h + 1) * HALF],
                         self.env.p[h][:, :]).then_inc(self.k("wo_e"), 1)
        if O >= 2:
            e.wait_ge(self.k("wo_p"), 4 * (O - 2) + 2)
        e.wait_ge(self.k("wo_e"), 2 * O + 2)
        e.tensor_copy(v.castbuf[O % 2][:], v.stage[:]).then_inc(self.k("wo_v"), 1)

    def tensor(self, e):
        v = self.env.t
        p = self.env.p
        for O in range(NCH):
            e.wait_ge(self.k("wo_v"), self.v_after_wcast(O))
            for h in range(2):
                if O >= 1:
                    e.wait_ge(self.k("wo_e"), 2 * (O - 1) + h + 1)
                for c in range(NCH):
                    mm = e.matmul(p[h][:, :], v.wq[O % 2][:, c * 128:(c + 1) * 128],
                                  v.big_attn[:, c * TOK + h * HALF: c * TOK + (h + 1) * HALF],
                                  start=(c == 0), stop=(c == NCH - 1))
                mm.then_inc(self.k("wo_m"), 1)
            if O >= 1:
                self._emit_stats(e, O - 1)
        self._emit_stats(e, NCH - 1)

    def v_after_wcast(self, O):
        return 1 if O == 0 else 2 * O

    def _emit_stats(self, e, O):
        v = self.env.t
        p = self.env.p
        done = getattr(self, "_std", set())
        self._std = done
        if O in done:
            return
        done.add(O)
        e.wait_ge(self.k("wo_v"), self.v_after_cast(O))
        e.wait_ge(self.k("wo_t"), O + 1)
        st = (O == 0)
        sp = (O == NCH - 1)
        e.matmul(p[2][0:1, :], v.ones_bf[:, 0:1], v.castbuf[O % 2][:, 0:HALF], start=st, stop=sp
                 ).then_inc(self.k("wo_p"), 1)
        e.matmul(p[3][0:1, :], v.ones_bf[:, 0:1], v.castbuf[O % 2][:, HALF:TOK], start=st, stop=sp
                 ).then_inc(self.k("wo_p"), 1)
        e.matmul(p[4][0:1, :], v.ones_bf[:, 0:1], v.sqbuf[O % 2][:, 0:HALF], start=st, stop=sp
                 ).then_inc(self.k("wo_p"), 1)
        e.matmul(p[5][0:1, :], v.ones_bf[:, 0:1], v.sqbuf[O % 2][:, HALF:TOK], start=st, stop=sp
                 ).then_inc(self.k("wo_p"), 1)

    def scalar(self, e):
        v = self.env.t
        for O in range(NCH):
            e.wait_ge(self.k("wo_e"), 2 * O + 2)
            if O >= 2:
                e.wait_ge(self.k("wo_p"), 4 * (O - 2) + 4)
            e.activation(v.sqbuf[O % 2][:], v.stage[:], AF.Square).then_inc(self.k("wo_t"), 1)


class FinalWait:
    def __init__(self, env, sem, val):
        self.env = env
        self.sem = sem
        self.val = val

    def gpsimd(self, e):
        e.wait_ge(self.sem, self.val)

    def sync(self, e):
        pass

    def vector(self, e):
        pass

    def scalar(self, e):
        pass

    def tensor(self, e):
        pass



def build_program(phase_sel=None, io_kinds=None, taps=()):
    io_kinds = io_kinds or {}
    env = Env()
    nc = bass.Bass("TRN2", target_bir_lowering=False)
    env.nc = nc
    ctx = ExitStack()
    env.ctx = ctx

    d = {}
    d["x0T"] = nc.dram_tensor("x0T", [DM, TOK], F32, kind="ExternalInput")
    d["w1p"] = nc.dram_tensor("w1p", [NFB, 128, DM], F32, kind="ExternalInput")
    d["w2p"] = nc.dram_tensor("w2p", [NCH, 128, DFF], F32, kind="ExternalInput")
    d["wqp"] = nc.dram_tensor("wqp", [NCH, 128, DM], F32, kind="ExternalInput")
    d["wkp"] = nc.dram_tensor("wkp", [NCH, 128, DM], F32, kind="ExternalInput")
    d["wvp"] = nc.dram_tensor("wvp", [NCH, 128, DM], F32, kind="ExternalInput")
    d["wop"] = nc.dram_tensor("wop", [NCH, 128, DM], F32, kind="ExternalInput")
    d["b1t"] = nc.dram_tensor("b1t", [128, NFB], F32, kind="ExternalInput")
    d["b2t"] = nc.dram_tensor("b2t", [128, NCH], F32, kind="ExternalInput")
    d["w12slice"] = nc.dram_tensor("w12slice", [16, 128, DM], F32, kind="ExternalInput")
    d["prefix_coef"] = nc.dram_tensor("prefix_coef", [128, 8], F32, kind="ExternalInput")
    d["outT"] = nc.dram_tensor("outT", [DM, TOK], F32, kind="ExternalOutput")
    d["x1d"] = nc.dram_tensor("x1d", [NCH, 128, TOK], F32, kind=io_kinds.get("x1d", "Internal"))
    d["x2d"] = nc.dram_tensor("x2d", [NCH, 128, TOK], F32, kind=io_kinds.get("x2d", "Internal"))
    d["ccs_in"] = nc.dram_tensor("ccs_in", [1, 8], F32)
    d["ccs_out"] = nc.dram_tensor("ccs_out", [1, 8], F32)
    d["cc_si"] = nc.dram_tensor("cc_si", [128, DM], F32)
    d["cc_so"] = nc.dram_tensor("cc_so", [128 * NCORES, DM], F32)
    env.dram = d

    base0 = nc.sbuf_base
    base = (base0 + 63) // 64 * 64
    t = Env()
    env.t = t

    def at(name, shape, dtype, off):
        return nc.alloc_sbuf_tensor_at(name, shape, dtype, offset=base + off)

    t.xln = at("xln", [128, NCH * TOK], BF16, 0)
    t.state = at("state", [128, NH * 128], F32, 0)
    t.h_big = at("h_big", [128, NFB * TOK], BF16, 32768)
    t.x0big = at("x0big", [128, NCH * TOK], F32, 32768)
    t.w12slot = [at(f"w12s{i}", [128, DM], F32, 32768 + 65536 + 8192 * i) for i in range(4)]
    t.big_q = at("big_q", [128, NH * TOK], BF16, 32768)
    t.big_k = at("big_k", [128, NH * TOK], BF16, 32768 + 32768)
    t.big_attn = at("big_attn", [128, NH * TOK], BF16, 32768 + 32768)
    t.big_v = at("big_v", [128, NLOC * DM], BF16, 32768 + 65536)
    t.big_ktm = at("big_ktm", [128, NLOC * DM], BF16, 32768 + 98304)
    t.land = [at(f"land{i}", [128, DM], F32, 163840 + 8192 * i) for i in range(2)]
    t.wq = [at(f"wqb{i}", [128, DM], BF16, 180224 + 4096 * i) for i in range(2)]
    t.u_f32 = at("u_f32", [128, DM], F32, 188416)
    t.roll = at("roll", [128, NH * 128], BF16, 188416)
    t.sm = [at(f"sm{i}", [128, 128], BF16, 188416 + 4096 + 256 * i) for i in range(2)]
    xs = at("xstr", [128, 2 * TOK], F32, 196608)
    t.xstr = [xs[:, 0:TOK], xs[:, TOK:2 * TOK]]
    t.rows = at("rows", [128, TOK], F32, 204800)
    t.stage = at("stage", [128, TOK], F32, 204800)
    cb = at("castbuf", [128, DM], BF16, 0)
    t.castbuf = [cb[:, 0:TOK], cb[:, TOK:2 * TOK]]
    sq = at("sqbuf", [128, DM], BF16, 4096 * 1)
    t.sqbuf = [sq[:, 0:TOK], sq[:, TOK:2 * TOK]]
    t.rowA = at("rowA", [1, TOK], F32, 0)
    t.rowC = at("rowC", [1, TOK], F32, 4096)
    off = 208896
    t.mask = at("mask", [128, 128], F32, off); off += 512
    t.ident = at("ident", [128, 128], BF16, off); off += 256
    t.b1t = at("b1t_s", [128, NFB], F32, off); off += 256
    t.b2t = at("b2t_s", [128, NCH], F32, off); off += 64
    t.halfb2t = at("halfb2t", [128, NCH], F32, off); off += 64
    t.thr = at("thr", [128, 16], F32, off); off += 64
    t.nthr = at("nthr", [128, 16], F32, off); off += 64
    t.eps = at("eps", [128, 1], F32, off); off += 64
    t.scalepart = at("scalepart", [128, 16], F32, off); off += 64
    t.onescol = at("onescol", [128, 1], F32, off); off += 64
    t.ones_bf = at("ones_bf", [128, 1], BF16, off); off += 64
    t.onesrow = at("onesrow", [1, 128], F32, off); off += 512
    t.rowstage = at("rowstage", [1, 16], F32, off); off += 64
    t.coef = at("coef", [128, 8], F32, off); off += 64
    assert base + off <= nc.sbuf_top, (base, off, nc.sbuf_top)
    map_end = base + off

    def xln_view(c):
        return t.xln[:, c * TOK:(c + 1) * TOK]
    env.xln_view = xln_view

    nc.sbuf_base = map_end

    phases = []

    with ctx:
        psums = []
        for i in range(6):
            psums.append(ctx.enter_context(nc.psum_tensor(f"P{i}", [128, HALF], F32)))
        env.p = psums
        env.pt = [ctx.enter_context(nc.psum_tensor(f"T{i}", [128, 128], BF16)) for i in range(2)]
        env.bar_sem = ctx.enter_context(nc.semaphore("bar"))
        out_sem = ctx.enter_context(nc.semaphore("outs"))
        env.fences = {w: [ctx.enter_context(nc.semaphore(f"fence_{w}")), 0]
                      for w in ("v", "t", "p", "g")}

        sel = phase_sel

        def want(name):
            return sel is None or name in sel

        def x0_ap(c):
            return d["x0T"].ap()[c * 128:(c + 1) * 128, :]

        def x1_ap(c):
            return d["x1d"].ap()[c]

        def x2_ap(c):
            return d["x2d"].ap()[c]

        def out_ap(c):
            return d["outT"].ap()[c * 128:(c + 1) * 128, :]

        if sel is None:
            f1a = FFNP1(env, "f1a", 0, prefetched=True)
            phases.append(FrontPhase(env, f1a))
            phases.append(Barrier(env))
            phases.append(f1a)
            phases.append(Barrier(env))
            phases.append(FFNP2(env, "f1b", 1, x0_ap, x1_ap, do_stats=True))
            phases.append(Barrier(env))
        else:
            phases.append(Prologue(env))
            phases.append(Barrier(env))
            if want("ffn1") or want("ffn2"):
                phases.append(ScalePhase(env))
                phases.append(Barrier(env))
            if want("ln1"):
                phases.append(LNStats(env, "l1s", x0_ap))
                phases.append(Barrier(env))
                phases.append(LNRowsApply(env, "l1r", x0_ap))
                phases.append(Barrier(env))
            if want("ffn1"):
                phases.append(FFNP1(env, "f1a", 0))
                phases.append(Barrier(env))
                phases.append(FFNP2(env, "f1b", 1, x0_ap, x1_ap, do_stats=True))
                phases.append(Barrier(env))
        if want("ln2"):
            if not want("ffn1"):
                phases.append(LNStats(env, "l2s", x1_ap))
                phases.append(Barrier(env))
            phases.append(LNRowsApply(env, "l2r", x1_ap))
            phases.append(Barrier(env))
        if want("attn"):
            phases.append(ProjPhase(env))
            phases.append(Barrier(env))
            phases.append(TransposePhase(env))
            phases.append(Barrier(env))
            phases.append(ScanPhase(env))
            phases.append(Barrier(env))
            ag = AGPhase(env)
            phases.append(ag)
            phases.append(Barrier(env))
            phases.append(Pass2Phase(env, ag))
            phases.append(Barrier(env))
            phases.append(WoPhase(env, x1_ap, x2_ap))
            phases.append(Barrier(env))
        if want("ln3"):
            if not want("attn"):
                phases.append(LNStats(env, "l3s", x2_ap))
                phases.append(Barrier(env))
            phases.append(LNRowsApply(env, "l3r", x2_ap))
            phases.append(Barrier(env))
        if want("ffn2"):
            phases.append(FFNP1(env, "f2a", 0))
            phases.append(Barrier(env))
            f2b = FFNP2(env, "f2b", 1, x2_ap, out_ap, do_stats=False)
            phases.append(f2b)
            phases.append(Barrier(env))
            phases.append(FinalWait(env, f2b.k("bs"), 16 * NCH))

        tap_handles = []
        for name in taps:
            src = d[name]
            th = nc.dram_tensor(f"tap_{name}", list(src.shape), src.dtype, kind="ExternalOutput")
            tap_handles.append((th, src))
        if tap_handles:
            phases.append(Barrier(env))

            class TapPhase:
                def gpsimd(self, e):
                    n = 0
                    for th, src in tap_handles:
                        e.dma_start(out=th.ap(), in_=src.ap()).then_inc(out_sem, 16)
                        n += 16
                    e.wait_ge(out_sem, n)

                def sync(self, e): pass
                def vector(self, e): pass
                def scalar(self, e): pass
                def tensor(self, e): pass
            phases.append(TapPhase())

        block = ctx.enter_context(nc.Block())

        @block.sync
        def _(e):
            for ph in phases:
                ph.sync(e)

        @block.gpsimd
        def _(e):
            for ph in phases:
                ph.gpsimd(e)

        @block.vector
        def _(e):
            for ph in phases:
                ph.vector(e)

        @block.scalar
        def _(e):
            for ph in phases:
                ph.scalar(e)

        @block.tensor
        def _(e):
            for ph in phases:
                ph.tensor(e)

    return nc



def stage_inputs(L0_attn_x, L0_attn_Wq, L0_attn_Wk, L0_attn_Wv, L0_attn_Wo,
                 L0_ffn_w1, L0_ffn_b1, L0_ffn_w2, L0_ffn_b2):
    x = np.asarray(L0_attn_x, dtype=np.float32)
    w1 = np.ascontiguousarray(np.asarray(L0_ffn_w1, dtype=np.float32))
    w2 = np.ascontiguousarray(np.asarray(L0_ffn_w2, dtype=np.float32))
    b1 = np.asarray(L0_ffn_b1, dtype=np.float32)
    b2 = np.asarray(L0_ffn_b2, dtype=np.float32)

    def pack_ot(W, nb):
        o, i = W.shape
        return np.ascontiguousarray(
            W.reshape(nb, 128, i // 128, 128).transpose(0, 3, 2, 1).reshape(nb, 128, i))

    w1p = pack_ot(w1, NFB)
    w2p = pack_ot(w2, NCH)
    wqp = pack_ot(np.asarray(L0_attn_Wq, np.float32), NCH)
    wkp = pack_ot(np.asarray(L0_attn_Wk, np.float32), NCH)
    wvp = pack_ot(np.asarray(L0_attn_Wv, np.float32), NCH)
    wop = pack_ot(np.asarray(L0_attn_Wo, np.float32), NCH)
    b1t = np.ascontiguousarray(b1.reshape(NFB, 128).T)
    b2t = np.ascontiguousarray(b2.reshape(NCH, 128).T)

    in_maps = []
    for c in range(NCORES):
        b = c // 4
        t0 = (c % 4) * TOK
        x0T = np.ascontiguousarray(x[b, t0:t0 + TOK, :].T)
        w1sl = w1p[c * 8:(c + 1) * 8]
        w2sl = w2p[c * 2:(c + 1) * 2].reshape(2, 128, 4, 2048).transpose(0, 2, 1, 3).reshape(8, 128, 2048)
        w12slice = np.ascontiguousarray(np.concatenate([w1sl, w2sl], axis=0))
        coef = np.zeros((128, 8), np.float32)
        g0 = 4 * (c // 4)
        for r in range(g0, c):
            coef[:, r] = 1.0
        in_maps.append(dict(
            x0T=x0T, w1p=w1p, w2p=w2p, wqp=wqp, wkp=wkp, wvp=wvp, wop=wop,
            b1t=b1t, b2t=b2t, w12slice=w12slice, prefix_coef=coef,
        ))
    return in_maps


def kernel(**inputs):
    in_maps = stage_inputs(**inputs)
    nc = build_program()
    res = run_bass_kernel_spmd(nc, in_maps, core_ids=list(range(NCORES)))
    out = np.empty((2, 4096, DM), dtype=np.float32)
    for c in range(NCORES):
        b = c // 4
        t0 = (c % 4) * TOK
        out[b, t0:t0 + TOK, :] = res.results[c]["outT"].T
    return out


# version 05
# speedup vs baseline: 1.0892x; 1.0723x over previous
import math
import numpy as np
import ml_dtypes
from contextlib import ExitStack

import concourse.bass as bass
import concourse.mybir as mybir
from concourse.bass_utils import run_bass_kernel_spmd

F32 = mybir.dt.float32
BF16 = mybir.dt.bfloat16
AF = mybir.ActivationFunctionType
ALU = mybir.AluOpType
AX = mybir.AxisListType

TOK = 1024
DM = 2048
DFF = 8192
NCH = 16
NFB = 64
NH = 16
CH = 128
NLOC = 8
HALF = 512
QSCALE = float(1.0 / np.sqrt(128.0))
INV_D = float(1.0 / 2048.0)
MEAN_SCL = float(0.7 / 16777216.0)
LN_EPS = 1e-5

NCORES = 8



FENCE_SAME_ENGINE = True


class Env:
    def fence(self, which, e, ins):
        if not FENCE_SAME_ENGINE:
            return
        sem, cnt = self.fences[which]
        cnt += 1
        self.fences[which][1] = cnt
        ins.then_inc(sem, 1)
        e.wait_ge(sem, cnt)


def _mk_sems(env, names):
    d = {}
    for n in names:
        d[n] = env.ctx.enter_context(env.nc.semaphore(n))
    return d



class Barrier:

    def __init__(self, env):
        env.bar_count = getattr(env, "bar_count", 0) + 1
        self.k = env.bar_count
        self.sem = env.bar_sem

    def _do(self, eng):
        eng.drain().then_inc(self.sem, 1)
        eng.wait_ge(self.sem, 5 * self.k)

    def sync(self, e): self._do(e)
    def gpsimd(self, e): self._do(e)
    def vector(self, e): self._do(e)
    def scalar(self, e): self._do(e)
    def tensor(self, e): self._do(e)


class Prologue:

    def __init__(self, env):
        self.env = env
        self.s = _mk_sems(env, ["pro_d", "pro_g", "pro_v"])

    def sync(self, e):
        v = self.env.t
        e.dma_start(out=v.b1t[:], in_=self.env.dram["b1t"].ap()).then_inc(self.s["pro_d"], 16)
        e.dma_start(out=v.b2t[:], in_=self.env.dram["b2t"].ap()).then_inc(self.s["pro_d"], 16)
        e.dma_start(out=v.coef[:], in_=self.env.dram["prefix_coef"].ap()).then_inc(self.s["pro_d"], 16)

    def gpsimd(self, e):
        v = self.env.t
        e.iota(v.u_f32[:, 0:128], [[1, 128]], channel_multiplier=128,
               allow_small_or_imprecise_dtypes=True)
        e.iota(v.u_f32[:, 128:129], [[1, 1]], channel_multiplier=129,
               allow_small_or_imprecise_dtypes=True).then_inc(self.s["pro_g"], 1)

    def vector(self, e):
        v = self.env.t
        e.wait_ge(self.s["pro_g"], 1)
        e.tensor_scalar(v.mask[:], v.u_f32[:, 0:128], v.u_f32[:, 128:129], None, ALU.is_ge)
        e.tensor_scalar(v.ident[:], v.u_f32[:, 0:128], v.u_f32[:, 128:129], None, ALU.is_equal)
        e.memset(v.eps[:], LN_EPS)
        e.memset(v.rowstage[:, :], 0.0)
        e.memset(v.onescol[:], 1.0)
        e.memset(v.onesrow[:], 1.0)
        e.memset(v.ones_bf[:], 1.0)
        e.wait_ge(self.s["pro_d"], 48)
        e.tensor_scalar_mul(v.halfb2t[:], v.b2t[:], 0.5)

    def scalar(self, e):
        pass

    def tensor(self, e):
        pass


class FrontPhase:

    def __init__(self, env, f1a):
        self.env = env
        self.f1a = f1a
        self.s = _mk_sems(env, ["fr_d", "fr_g", "fr_x0", "fr_a0", "fr_a1", "fr_a2", "fr_a3",
                                "fr_sv", "fr_lv", "fr_lp", "fr_t", "fr_p", "fr_c", "fr_cc"])

    def x0v(self, c):
        return self.env.t.x0big[:, c * TOK:(c + 1) * TOK]

    def sync(self, e):
        v = self.env.t
        for i in range(16):
            if i >= 4:
                e.wait_ge(self.s["fr_sv"], i - 3)
            e.dma_start(out=v.w12slot[i % 4][:], in_=self.env.dram["w12slice"].ap()[i]
                        ).then_inc(self.s[f"fr_a{i % 4}"], 16)
        e.dma_start(out=v.land[0][:], in_=self.env.dram["w1p"].ap()[0]
                    ).then_inc(self.f1a.k("a0"), 16)
        e.dma_start(out=v.land[1][:], in_=self.env.dram["w1p"].ap()[1]
                    ).then_inc(self.f1a.k("a1"), 16)

    def gpsimd(self, e):
        v = self.env.t
        e.dma_start(out=v.x0big[:].rearrange("p (c t) -> p c t", c=NCH),
                    in_=self.env.dram["x0T"].ap().rearrange("(c p) t -> p c t", p=128)
                    ).then_inc(self.s["fr_x0"], 16)
        e.dma_start(out=v.b1t[:], in_=self.env.dram["b1t"].ap()).then_inc(self.s["fr_d"], 16)
        e.dma_start(out=v.b2t[:], in_=self.env.dram["b2t"].ap()).then_inc(self.s["fr_d"], 16)
        e.dma_start(out=v.coef[:], in_=self.env.dram["prefix_coef"].ap()).then_inc(self.s["fr_d"], 16)
        e.iota(v.u_f32[:, 0:128], [[1, 128]], channel_multiplier=128,
               allow_small_or_imprecise_dtypes=True)
        e.iota(v.u_f32[:, 128:129], [[1, 1]], channel_multiplier=129,
               allow_small_or_imprecise_dtypes=True).then_inc(self.s["fr_g"], 1)
        e.wait_ge(self.s["fr_sv"], 18)
        e.dma_start(out=self.env.dram["ccs_in"].ap(), in_=v.rowstage[0:1, 0:8]).then_inc(self.s["fr_c"], 16)
        e.wait_ge(self.s["fr_c"], 16)
        e.collective_compute(
            "AllReduce", ALU.add, replica_groups=[list(range(NCORES))],
            ins=[self.env.dram["ccs_in"].ap().opt()], outs=[self.env.dram["ccs_out"].ap().opt()],
        ).then_inc(self.s["fr_cc"], 1)
        e.wait_ge(self.s["fr_cc"], 1)
        e.dma_start(out=v.rowstage[0:1, 0:8], in_=self.env.dram["ccs_out"].ap()).then_inc(self.s["fr_c"], 16)
        e.wait_ge(self.s["fr_c"], 32)

    def vector(self, e):
        v = self.env.t
        env = self.env
        e.wait_ge(self.s["fr_g"], 1)
        e.tensor_scalar(v.mask[:], v.u_f32[:, 0:128], v.u_f32[:, 128:129], None, ALU.is_ge)
        ins = e.tensor_scalar(v.ident[:], v.u_f32[:, 0:128], v.u_f32[:, 128:129], None, ALU.is_equal)
        env.fence("v", e, ins)
        e.memset(v.eps[:], LN_EPS)
        e.memset(v.rowstage[:, :], 0.0)
        e.memset(v.onescol[:], 1.0)
        e.memset(v.onesrow[:], 1.0)
        e.memset(v.ones_bf[:], 1.0)
        e.wait_ge(self.s["fr_d"], 48)
        e.tensor_scalar_mul(v.halfb2t[:], v.b2t[:], 0.5)
        for i in range(16):
            e.wait_ge(self.s[f"fr_a{i % 4}"], 16 * (i // 4 + 1))
            e.tensor_reduce(v.scalepart[:, i:i + 1], v.w12slot[i % 4][:], AX.X, ALU.add,
                            apply_absolute_value=True).then_inc(self.s["fr_sv"], 1)
        e.wait_ge(self.s["fr_sv"], 16)
        e.tensor_reduce(v.thr[:, 0:1], v.scalepart[:, 0:8], AX.X, ALU.add)
        e.tensor_reduce(v.thr[:, 1:2], v.scalepart[:, 8:16], AX.X, ALU.add).then_inc(self.s["fr_sv"], 1)
        e.wait_ge(self.s["fr_x0"], 16)
        for c in range(NCH):
            if c >= 2:
                e.wait_ge(self.s["fr_lp"], 4 * (c - 2) + 2)
            e.tensor_copy(v.castbuf[c % 2][:], self.x0v(c)).then_inc(self.s["fr_lv"], 1)
        e.wait_ge(self.s["fr_p"], 2)
        e.tensor_copy(v.rowstage[0:1, 0:2], env.p[0][0:1, 0:2]).then_inc(self.s["fr_sv"], 1)
        e.wait_ge(self.s["fr_lp"], 64)
        A, B, C = v.rowA[:, :], v.rows[0:1, :], v.rowC[:, :]
        p = env.p
        for j in range(2):
            ins = e.tensor_scalar_mul(A[:, j * HALF:(j + 1) * HALF], p[2 + j][0:1, :], INV_D)
        env.fence("v", e, ins)
        ins = e.tensor_mul(C[:, :], A[:, :], A[:, :])
        env.fence("v", e, ins)
        for j in range(2):
            ins = e.scalar_tensor_tensor(C[:, j * HALF:(j + 1) * HALF], p[4 + j][0:1, :], INV_D,
                                         C[:, j * HALF:(j + 1) * HALF], ALU.mult, ALU.subtract)
        ins.then_inc(self.s["fr_lv"], 1)
        e.wait_ge(self.s["fr_t"], 17)
        ins = e.reciprocal(B[:, :], B[:, :])
        env.fence("v", e, ins)
        e.tensor_mul(C[:, :], A[:, :], B[:, :]).then_inc(self.s["fr_lv"], 1)
        e.wait_ge(self.s["fr_lp"], 68)
        rb = v.stage[:, 0:TOK]
        mrb = v.u_f32[:, TOK:2 * TOK]
        for j in range(2):
            e.tensor_copy(rb[:, j * HALF:(j + 1) * HALF], p[2 + j][:, :])
            ins = e.tensor_copy(mrb[:, j * HALF:(j + 1) * HALF], p[4 + j][:, :])
        env.fence("v", e, ins)
        for c in range(NCH):
            if c >= 1:
                e.wait_ge(self.s["fr_lv"], 18 + c)
            tins = e.tensor_mul(v.u_f32[:, 0:TOK], self.x0v(c), rb)
            env.fence("v", e, tins)
            e.tensor_sub(env.xln_view(c), v.u_f32[:, 0:TOK], mrb).then_inc(self.s["fr_lv"], 1)
        e.wait_ge(self.s["fr_c"], 32)
        e.tensor_scalar_mul(v.rowstage[0:1, 8:10], v.rowstage[0:1, 0:2], MEAN_SCL).then_inc(self.s["fr_sv"], 1)
        e.wait_ge(self.s["fr_p"], 4)
        e.tensor_copy(v.thr[:, 0:2], env.p[1][:, 0:2])
        e.tensor_scalar_mul(v.nthr[:, 0:2], env.p[1][:, 0:2], -1.0).then_inc(self.s["fr_sv"], 1)

    def scalar(self, e):
        v = self.env.t
        e.wait_ge(self.s["fr_x0"], 16)
        for c in range(NCH):
            if c >= 2:
                e.wait_ge(self.s["fr_lp"], 4 * (c - 2) + 4)
            e.activation(v.sqbuf[c % 2][:], self.x0v(c), AF.Square).then_inc(self.s["fr_t"], 1)
        e.wait_ge(self.s["fr_lv"], 17)
        A, B, C = v.rowA[:, :], v.rows[0:1, :], v.rowC[:, :]
        e.activation(B[:, :], C[:, :], AF.Sqrt, bias=v.eps[0:1, 0:1]).then_inc(self.s["fr_t"], 1)

    def tensor(self, e):
        v = self.env.t
        p = self.env.p
        for c in range(NCH):
            e.wait_ge(self.s["fr_lv"], c + 1)
            e.wait_ge(self.s["fr_t"], c + 1)
            st = (c == 0)
            sp = (c == NCH - 1)
            e.matmul(p[2][0:1, :], v.ones_bf[:, 0:1], v.castbuf[c % 2][:, 0:HALF], start=st, stop=sp
                     ).then_inc(self.s["fr_lp"], 1)
            e.matmul(p[3][0:1, :], v.ones_bf[:, 0:1], v.castbuf[c % 2][:, HALF:TOK], start=st, stop=sp
                     ).then_inc(self.s["fr_lp"], 1)
            e.matmul(p[4][0:1, :], v.ones_bf[:, 0:1], v.sqbuf[c % 2][:, 0:HALF], start=st, stop=sp
                     ).then_inc(self.s["fr_lp"], 1)
            e.matmul(p[5][0:1, :], v.ones_bf[:, 0:1], v.sqbuf[c % 2][:, HALF:TOK], start=st, stop=sp
                     ).then_inc(self.s["fr_lp"], 1)
        e.wait_ge(self.s["fr_sv"], 17)
        e.matmul(p[0][0:1, 0:1], v.thr[:, 0:1], v.onescol[:, 0:1], start=True, stop=True
                 ).then_inc(self.s["fr_p"], 1)
        e.matmul(p[0][0:1, 1:2], v.thr[:, 1:2], v.onescol[:, 0:1], start=True, stop=True
                 ).then_inc(self.s["fr_p"], 1)
        e.wait_ge(self.s["fr_lv"], 18)
        A, B, C = v.rowA[:, :], v.rows[0:1, :], v.rowC[:, :]
        for j in range(2):
            e.matmul(p[2 + j][:, :], v.onesrow[0:1, :], B[:, j * HALF:(j + 1) * HALF],
                     start=True, stop=True).then_inc(self.s["fr_lp"], 1)
            e.matmul(p[4 + j][:, :], v.onesrow[0:1, :], C[:, j * HALF:(j + 1) * HALF],
                     start=True, stop=True).then_inc(self.s["fr_lp"], 1)
        e.wait_ge(self.s["fr_sv"], 19)
        e.matmul(p[1][:, 0:1], v.onesrow[0:1, :], v.rowstage[0:1, 8:9], start=True, stop=True
                 ).then_inc(self.s["fr_p"], 1)
        e.matmul(p[1][:, 1:2], v.onesrow[0:1, :], v.rowstage[0:1, 9:10], start=True, stop=True
                 ).then_inc(self.s["fr_p"], 1)


class ScalePhase:

    def __init__(self, env):
        self.env = env
        self.s = _mk_sems(env, ["sc_a0", "sc_a1", "sc_v", "sc_p", "sc_c", "sc_cc"])

    def sync(self, e):
        v = self.env.t
        for i in range(16):
            if i >= 2:
                e.wait_ge(self.s["sc_v"], i - 1)
            e.dma_start(out=v.land[i % 2][:], in_=self.env.dram["w12slice"].ap()[i]
                        ).then_inc(self.s[f"sc_a{i % 2}"], 16)

    def vector(self, e):
        v = self.env.t
        for i in range(16):
            e.wait_ge(self.s[f"sc_a{i % 2}"], 16 * (i // 2 + 1))
            e.tensor_reduce(v.scalepart[:, i:i + 1], v.land[i % 2][:], AX.X, ALU.add,
                            apply_absolute_value=True).then_inc(self.s["sc_v"], 1)
        e.wait_ge(self.s["sc_v"], 16)
        e.tensor_reduce(v.thr[:, 0:1], v.scalepart[:, 0:8], AX.X, ALU.add)
        e.tensor_reduce(v.thr[:, 1:2], v.scalepart[:, 8:16], AX.X, ALU.add).then_inc(self.s["sc_v"], 1)
        e.wait_ge(self.s["sc_p"], 2)
        e.tensor_copy(v.rowstage[0:1, 0:2], self.env.p[2][0:1, 0:2]).then_inc(self.s["sc_v"], 1)
        e.wait_ge(self.s["sc_c"], 32)
        e.tensor_scalar_mul(v.rowstage[0:1, 8:10], v.rowstage[0:1, 0:2], MEAN_SCL).then_inc(self.s["sc_v"], 1)
        e.wait_ge(self.s["sc_p"], 4)
        e.tensor_copy(v.thr[:, 0:2], self.env.p[3][:, 0:2])
        e.tensor_scalar_mul(v.nthr[:, 0:2], self.env.p[3][:, 0:2], -1.0).then_inc(self.s["sc_v"], 1)

    def tensor(self, e):
        v = self.env.t
        e.wait_ge(self.s["sc_v"], 17)
        e.matmul(self.env.p[2][0:1, 0:1], v.thr[:, 0:1], v.onescol[:, 0:1], start=True, stop=True
                 ).then_inc(self.s["sc_p"], 1)
        e.matmul(self.env.p[2][0:1, 1:2], v.thr[:, 1:2], v.onescol[:, 0:1], start=True, stop=True
                 ).then_inc(self.s["sc_p"], 1)
        e.wait_ge(self.s["sc_v"], 19)
        e.matmul(self.env.p[3][:, 0:1], v.onesrow[0:1, :], v.rowstage[0:1, 8:9], start=True, stop=True
                 ).then_inc(self.s["sc_p"], 1)
        e.matmul(self.env.p[3][:, 1:2], v.onesrow[0:1, :], v.rowstage[0:1, 9:10], start=True, stop=True
                 ).then_inc(self.s["sc_p"], 1)

    def gpsimd(self, e):
        v = self.env.t
        e.wait_ge(self.s["sc_v"], 18)
        e.dma_start(out=self.env.dram["ccs_in"].ap(), in_=v.rowstage[0:1, 0:8]).then_inc(self.s["sc_c"], 16)
        e.wait_ge(self.s["sc_c"], 16)
        e.collective_compute(
            "AllReduce", ALU.add, replica_groups=[list(range(NCORES))],
            ins=[self.env.dram["ccs_in"].ap().opt()], outs=[self.env.dram["ccs_out"].ap().opt()],
        ).then_inc(self.s["sc_cc"], 1)
        e.wait_ge(self.s["sc_cc"], 1)
        e.dma_start(out=v.rowstage[0:1, 0:8], in_=self.env.dram["ccs_out"].ap()).then_inc(self.s["sc_c"], 16)
        e.wait_ge(self.s["sc_c"], 32)

    def scalar(self, e):
        pass


class LNStats:

    def __init__(self, env, tag, src_ap):
        self.env = env
        self.src_ap = src_ap
        self.s = _mk_sems(env, [f"{tag}_b0", f"{tag}_b1", f"{tag}_v", f"{tag}_t", f"{tag}_p"])
        self.tag = tag

    def k(self, n):
        return self.s[f"{self.tag}_{n}"]

    def bwait(self, e, c):
        e.wait_ge(self.k(f"b{c % 2}"), 16 * (c // 2 + 1))

    def gpsimd(self, e):
        v = self.env.t
        for c in range(NCH):
            if c >= 2:
                e.wait_ge(self.k("v"), c - 1)
                e.wait_ge(self.k("t"), c - 1)
            e.dma_start(out=v.xstr[c % 2][:], in_=self.src_ap(c)).then_inc(self.k(f"b{c % 2}"), 16)

    def vector(self, e):
        v = self.env.t
        for c in range(NCH):
            self.bwait(e, c)
            if c >= 2:
                e.wait_ge(self.k("p"), 4 * (c - 2) + 2)
            e.tensor_copy(v.castbuf[c % 2][:], v.xstr[c % 2][:]).then_inc(self.k("v"), 1)

    def scalar(self, e):
        v = self.env.t
        for c in range(NCH):
            self.bwait(e, c)
            if c >= 2:
                e.wait_ge(self.k("p"), 4 * (c - 2) + 4)
            e.activation(v.sqbuf[c % 2][:], v.xstr[c % 2][:], AF.Square).then_inc(self.k("t"), 1)

    def tensor(self, e):
        v = self.env.t
        p = self.env.p
        for c in range(NCH):
            e.wait_ge(self.k("v"), c + 1)
            e.wait_ge(self.k("t"), c + 1)
            st = (c == 0)
            sp = (c == NCH - 1)
            e.matmul(p[2][0:1, :], v.ones_bf[:, 0:1], v.castbuf[c % 2][:, 0:HALF], start=st, stop=sp
                     ).then_inc(self.k("p"), 1)
            e.matmul(p[3][0:1, :], v.ones_bf[:, 0:1], v.castbuf[c % 2][:, HALF:TOK], start=st, stop=sp
                     ).then_inc(self.k("p"), 1)
            e.matmul(p[4][0:1, :], v.ones_bf[:, 0:1], v.sqbuf[c % 2][:, 0:HALF], start=st, stop=sp
                     ).then_inc(self.k("p"), 1)
            e.matmul(p[5][0:1, :], v.ones_bf[:, 0:1], v.sqbuf[c % 2][:, HALF:TOK], start=st, stop=sp
                     ).then_inc(self.k("p"), 1)

    def sync(self, e):
        pass


class LNRowsApply:

    def __init__(self, env, tag, src_ap):
        self.env = env
        self.src_ap = src_ap
        self.s = _mk_sems(env, [f"{tag}_b0", f"{tag}_b1", f"{tag}_v", f"{tag}_t", f"{tag}_p"])
        self.tag = tag

    def k(self, n):
        return self.s[f"{self.tag}_{n}"]

    def _rows(self):
        v = self.env.t
        return v.rowA[:, :], v.rows[0:1, :], v.rowC[:, :]

    def vector(self, e):
        v = self.env.t
        p = self.env.p
        A, B, C = self._rows()
        for j in range(2):
            ins = e.tensor_scalar_mul(A[:, j * HALF:(j + 1) * HALF], p[2 + j][0:1, :], INV_D)
        self.env.fence("v", e, ins)
        ins = e.tensor_mul(C[:, :], A[:, :], A[:, :])
        self.env.fence("v", e, ins)
        for j in range(2):
            ins = e.scalar_tensor_tensor(C[:, j * HALF:(j + 1) * HALF], p[4 + j][0:1, :], INV_D,
                                         C[:, j * HALF:(j + 1) * HALF], ALU.mult, ALU.subtract)
        ins.then_inc(self.k("v"), 1)
        e.wait_ge(self.k("t"), 1)
        ins = e.reciprocal(B[:, :], B[:, :])
        self.env.fence("v", e, ins)
        e.tensor_mul(C[:, :], A[:, :], B[:, :]).then_inc(self.k("v"), 1)
        e.wait_ge(self.k("p"), 4)
        rb = v.stage[:, 0:TOK]
        mrb = v.u_f32[:, TOK:2 * TOK]
        for j in range(2):
            e.tensor_copy(rb[:, j * HALF:(j + 1) * HALF], p[2 + j][:, :])
            ins = e.tensor_copy(mrb[:, j * HALF:(j + 1) * HALF], p[4 + j][:, :])
        self.env.fence("v", e, ins)
        for c in range(NCH):
            e.wait_ge(self.k(f"b{c % 2}"), 16 * (c // 2 + 1))
            if c >= 1:
                e.wait_ge(self.k("v"), 2 + c)
            tins = e.tensor_mul(v.u_f32[:, 0:TOK], v.xstr[c % 2][:, 0:TOK], rb)
            self.env.fence("v", e, tins)
            e.tensor_sub(self.env.xln_view(c), v.u_f32[:, 0:TOK], mrb
                         ).then_inc(self.k("v"), 1)

    def scalar(self, e):
        v = self.env.t
        A, B, C = self._rows()
        e.wait_ge(self.k("v"), 1)
        e.activation(B[:, :], C[:, :], AF.Sqrt, bias=v.eps[0:1, 0:1]).then_inc(self.k("t"), 1)

    def tensor(self, e):
        v = self.env.t
        p = self.env.p
        A, B, C = self._rows()
        e.wait_ge(self.k("v"), 2)
        for j in range(2):
            e.matmul(p[2 + j][:, :], v.onesrow[0:1, :], B[:, j * HALF:(j + 1) * HALF],
                     start=True, stop=True).then_inc(self.k("p"), 1)
            e.matmul(p[4 + j][:, :], v.onesrow[0:1, :], C[:, j * HALF:(j + 1) * HALF],
                     start=True, stop=True).then_inc(self.k("p"), 1)

    def gpsimd(self, e):
        v = self.env.t
        for c in range(NCH):
            if c >= 2:
                e.wait_ge(self.k("v"), 2 + c - 1)
            e.dma_start(out=v.xstr[c % 2][:], in_=self.src_ap(c)).then_inc(self.k(f"b{c % 2}"), 16)

    def sync(self, e):
        pass


class FFNP1:

    def __init__(self, env, tag, thr_col, prefetched=False, writeback=False, cached=False):
        self.env = env
        self.thr_col = thr_col
        self.prefetched = prefetched
        self.writeback = writeback
        self.cached = cached
        names = [f"{tag}_a0", f"{tag}_a1", f"{tag}_m", f"{tag}_g"]
        if not cached:
            names.append(f"{tag}_q")
        if writeback:
            names += [f"{tag}_w0", f"{tag}_w1"]
        self.s = _mk_sems(env, names)
        self.tag = tag

    def k(self, n):
        return self.s[f"{self.tag}_{n}"]

    def sync(self, e):
        v = self.env.t
        if self.cached:
            for F in range(NFB):
                if F >= 2:
                    e.wait_ge(self.k("m"), 2 * (F - 2) + 2)
                e.dma_start(out=v.wq[F % 2][:], in_=self.env.dram["w1qd"].ap()[F]
                            ).then_inc(self.k(f"a{F % 2}"), 16)
            return
        for F in range(2 if self.prefetched else 0, NFB):
            if F >= 2:
                e.wait_ge(self.k("q"), F - 1)
            e.dma_start(out=v.land[F % 2][:], in_=self.env.dram["w1p"].ap()[F]
                        ).then_inc(self.k(f"a{F % 2}"), 16)

    def vector(self, e):
        if self.cached:
            return
        v = self.env.t
        tc = self.thr_col
        for F in range(NFB):
            e.wait_ge(self.k(f"a{F % 2}"), 16 * (F // 2 + 1))
            if F >= 2:
                e.wait_ge(self.k("m"), 2 * (F - 2) + 2)
                if self.writeback:
                    e.wait_ge(self.k(f"w{F % 2}"), 16 * ((F - 2) // 2 + 1))
            if F >= 1:
                e.wait_ge(self.k("q"), F)
            uins = e.tensor_scalar(v.u_f32[:], v.land[F % 2][:], v.nthr[:, tc:tc + 1], None, ALU.is_le)
            self.env.fence("v", e, uins)
            e.scalar_tensor_tensor(v.wq[F % 2][:], v.land[F % 2][:], v.thr[:, tc:tc + 1],
                                   v.u_f32[:], ALU.is_ge, ALU.subtract).then_inc(self.k("q"), 1)

    def tensor(self, e):
        v = self.env.t
        p = self.env.p
        for F in range(NFB):
            if self.cached:
                e.wait_ge(self.k(f"a{F % 2}"), 16 * (F // 2 + 1))
            else:
                e.wait_ge(self.k("q"), F + 1)
            for h in range(2):
                if F >= 1:
                    e.wait_ge(self.k("g"), 2 * (F - 1) + h + 1)
                for c in range(NCH):
                    mm = e.matmul(p[h][:, :], v.wq[F % 2][:, c * 128:(c + 1) * 128],
                                  self.env.xln_view(c)[:, h * HALF:(h + 1) * HALF],
                                  start=(c == 0), stop=(c == NCH - 1))
                mm.then_inc(self.k("m"), 1)

    def scalar(self, e):
        v = self.env.t
        p = self.env.p
        for F in range(NFB):
            for h in range(2):
                e.wait_ge(self.k("m"), 2 * F + h + 1)
                e.activation(v.h_big[:, F * TOK + h * HALF: F * TOK + (h + 1) * HALF],
                             p[h][:, :], AF.Gelu, bias=v.b1t[:, F:F + 1], scale=1.0
                             ).then_inc(self.k("g"), 1)

    def gpsimd(self, e):
        if not self.writeback:
            return
        v = self.env.t
        for F in range(NFB):
            e.wait_ge(self.k("q"), F + 1)
            e.dma_start(out=self.env.dram["w1qd"].ap()[F], in_=v.wq[F % 2][:]
                        ).then_inc(self.k(f"w{F % 2}"), 16)
        e.wait_ge(self.k("w0"), 16 * (NFB // 2))
        e.wait_ge(self.k("w1"), 16 * (NFB // 2))


class FFNP2:

    def __init__(self, env, tag, thr_col, x_src_ap, out_ap, do_stats,
                 writeback=False, cached=False):
        self.env = env
        self.thr_col = thr_col
        self.x_src_ap = x_src_ap
        self.out_ap = out_ap
        self.do_stats = do_stats
        self.writeback = writeback
        self.cached = cached
        self.banks4 = not do_stats
        names = [f"{tag}_a0", f"{tag}_a1", f"{tag}_m", f"{tag}_e",
                 f"{tag}_bl0", f"{tag}_bl1", f"{tag}_bs", f"{tag}_v"]
        if not cached:
            names.append(f"{tag}_q")
        if do_stats:
            names += [f"{tag}_t", f"{tag}_p"]
        if writeback:
            names += [f"{tag}_w0", f"{tag}_w1"]
        self.s = _mk_sems(env, names)
        self.tag = tag

    def w2q_ap(self, kk):
        D, qu = kk // 4, kk % 4
        return self.env.dram["w2qd"].ap()[D][:, qu * 2048:(qu + 1) * 2048]

    def mval(self, kk):
        D, qu = kk // 4, kk % 4
        if self.banks4:
            return 4 * D + qu + 1
        return 5 * D + (qu + 1 if qu < 3 else 5)

    def m_after_D(self, D):
        return (4 if self.banks4 else 5) * (D + 1)

    def k(self, n):
        return self.s[f"{self.tag}_{n}"]

    def load_wait(self, e, D):
        e.wait_ge(self.k(f"bl{D % 2}"), 16 * (D // 2 + 1))

    def gpsimd(self, e):
        v = self.env.t
        e.dma_start(out=v.xstr[0][:], in_=self.x_src_ap(0)).then_inc(self.k("bl0"), 16)
        e.dma_start(out=v.xstr[1][:], in_=self.x_src_ap(1)).then_inc(self.k("bl1"), 16)
        for D in range(NCH):
            if self.writeback:
                for qu in range(4):
                    kk = 4 * D + qu
                    e.wait_ge(self.k("q"), kk + 1)
                    e.dma_start(out=self.w2q_ap(kk), in_=v.wq[kk % 2][:]
                                ).then_inc(self.k(f"w{kk % 2}"), 16)
            e.wait_ge(self.k("e"), 2 * D + 2)
            if self.do_stats:
                e.wait_ge(self.k("v"), self.v_after_cast(D))
                e.wait_ge(self.k("t"), D + 1)
            e.dma_start(out=self.out_ap(D), in_=v.stage[:]).then_inc(self.k("bs"), 16)
            if D + 2 < NCH:
                e.dma_start(out=v.xstr[(D + 2) % 2][:], in_=self.x_src_ap(D + 2)
                            ).then_inc(self.k(f"bl{D % 2}"), 16)
        e.wait_ge(self.k("bs"), 16 * NCH)
        if self.writeback:
            e.wait_ge(self.k("w0"), 16 * (2 * NCH))
            e.wait_ge(self.k("w1"), 16 * (2 * NCH))

    def v_after_cast(self, D):
        assert self.do_stats
        return 2 + 2 * D if D < NCH - 1 else 2 * NCH

    def v_after_xp(self, Dp):
        if Dp == 0:
            return 1
        if self.do_stats:
            return 2 * Dp + 1
        return Dp + 1

    def vector(self, e):
        v = self.env.t
        tc = self.thr_col
        self.load_wait(e, 0)
        e.tensor_scalar(v.xstr[0][:], v.xstr[0][:], v.halfb2t[:, 0:1], None, ALU.add
                        ).then_inc(self.k("v"), 1)
        for D in range(NCH):
            if D >= 1:
                self._emit_evac(e, D - 1)
            if self.cached:
                continue
            for qu in range(4):
                kk = 4 * D + qu
                e.wait_ge(self.k(f"a{kk % 2}"), 16 * (kk // 2 + 1))
                if kk >= 2:
                    e.wait_ge(self.k("m"), self.mval(kk - 2))
                    if self.writeback:
                        e.wait_ge(self.k(f"w{kk % 2}"), 16 * ((kk - 2) // 2 + 1))
                if kk >= 1:
                    e.wait_ge(self.k("q"), kk)
                uins = e.tensor_scalar(v.u_f32[:], v.land[kk % 2][:], v.nthr[:, tc:tc + 1], None, ALU.is_le)
                self.env.fence("v", e, uins)
                e.scalar_tensor_tensor(v.wq[kk % 2][:], v.land[kk % 2][:], v.thr[:, tc:tc + 1],
                                       v.u_f32[:], ALU.is_ge, ALU.subtract).then_inc(self.k("q"), 1)
        self._emit_evac(e, NCH - 1)

    def _emit_evac(self, e, D):
        v = self.env.t
        if D in getattr(self, "_evac_done", set()):
            return
        self._evac_done = getattr(self, "_evac_done", set())
        self._evac_done.add(D)
        if D >= 1:
            e.wait_ge(self.k("bs"), 16 * D)
        e.wait_ge(self.k("v"), self.v_after_xp(D))
        if self.banks4:
            e.wait_ge(self.k("m"), self.m_after_D(D))
        for h in range(2):
            if not self.banks4:
                e.wait_ge(self.k("m"), 5 * D + 4 + h)
            pb = self.env.p[2 * (D % 2) + h] if self.banks4 else self.env.p[h]
            e.scalar_tensor_tensor(v.stage[:, h * HALF:(h + 1) * HALF], pb[:, :], 0.5,
                                   v.xstr[D % 2][:, h * HALF:(h + 1) * HALF], ALU.mult, ALU.add
                                   ).then_inc(self.k("e"), 1)
        if self.do_stats:
            if D >= 2:
                e.wait_ge(self.k("p"), 4 * (D - 2) + 2)
            e.wait_ge(self.k("e"), 2 * D + 2)
            e.tensor_copy(self.env.t.castbuf[D % 2][:], v.stage[:]).then_inc(self.k("v"), 1)
        if D + 1 < NCH:
            self.load_wait(e, D + 1)
            e.tensor_scalar(v.xstr[(D + 1) % 2][:], v.xstr[(D + 1) % 2][:],
                            v.halfb2t[:, D + 1:D + 2], None, ALU.add).then_inc(self.k("v"), 1)

    def tensor(self, e):
        v = self.env.t
        p = self.env.p
        for D in range(NCH):
            if self.banks4:
                if D >= 2:
                    e.wait_ge(self.k("e"), 2 * (D - 2) + 2)
                pb0 = p[2 * (D % 2)]
                pb1 = p[2 * (D % 2) + 1]
            else:
                if D >= 1:
                    e.wait_ge(self.k("e"), 2 * (D - 1) + 2)
                pb0, pb1 = p[0], p[1]
            for qu in range(4):
                kk = 4 * D + qu
                if self.cached:
                    e.wait_ge(self.k(f"a{kk % 2}"), 16 * (kk // 2 + 1))
                else:
                    e.wait_ge(self.k("q"), kk + 1)
                if (not self.banks4) and qu == 3:
                    for i in range(16):
                        F = qu * 16 + i
                        mm = e.matmul(pb0[:, :], v.wq[kk % 2][:, i * 128:(i + 1) * 128],
                                      v.h_big[:, F * TOK: F * TOK + HALF],
                                      start=False, stop=(i == 15))
                    mm.then_inc(self.k("m"), 1)
                    for i in range(16):
                        F = qu * 16 + i
                        mm = e.matmul(pb1[:, :], v.wq[kk % 2][:, i * 128:(i + 1) * 128],
                                      v.h_big[:, F * TOK + HALF: F * TOK + TOK],
                                      start=False, stop=(i == 15))
                    mm.then_inc(self.k("m"), 1)
                    continue
                for i in range(16):
                    F = qu * 16 + i
                    st = (qu == 0 and i == 0)
                    sp = False
                    e.matmul(pb0[:, :], v.wq[kk % 2][:, i * 128:(i + 1) * 128],
                             v.h_big[:, F * TOK: F * TOK + HALF], start=st,
                             stop=(self.banks4 and qu == 3 and i == 15))
                    mm = e.matmul(pb1[:, :], v.wq[kk % 2][:, i * 128:(i + 1) * 128],
                                  v.h_big[:, F * TOK + HALF: F * TOK + TOK], start=st,
                                  stop=(self.banks4 and qu == 3 and i == 15))
                mm.then_inc(self.k("m"), 1)
            if self.do_stats and D >= 1:
                self._emit_stats(e, D - 1)
        if self.do_stats:
            self._emit_stats(e, NCH - 1)

    def _emit_stats(self, e, D):
        v = self.env.t
        p = self.env.p
        if D in getattr(self, "_stats_done", set()):
            return
        self._stats_done = getattr(self, "_stats_done", set())
        self._stats_done.add(D)
        e.wait_ge(self.k("v"), self.v_after_cast(D))
        e.wait_ge(self.k("t"), D + 1)
        st = (D == 0)
        sp = (D == NCH - 1)
        e.matmul(p[2][0:1, :], v.ones_bf[:, 0:1], v.castbuf[D % 2][:, 0:HALF], start=st, stop=sp
                 ).then_inc(self.k("p"), 1)
        e.matmul(p[3][0:1, :], v.ones_bf[:, 0:1], v.castbuf[D % 2][:, HALF:TOK], start=st, stop=sp
                 ).then_inc(self.k("p"), 1)
        e.matmul(p[4][0:1, :], v.ones_bf[:, 0:1], v.sqbuf[D % 2][:, 0:HALF], start=st, stop=sp
                 ).then_inc(self.k("p"), 1)
        e.matmul(p[5][0:1, :], v.ones_bf[:, 0:1], v.sqbuf[D % 2][:, HALF:TOK], start=st, stop=sp
                 ).then_inc(self.k("p"), 1)

    def scalar(self, e):
        if not self.do_stats:
            return
        v = self.env.t
        for D in range(NCH):
            e.wait_ge(self.k("e"), 2 * D + 2)
            if D >= 2:
                e.wait_ge(self.k("p"), 4 * (D - 2) + 4)
            e.activation(v.sqbuf[D % 2][:], v.stage[:], AF.Square).then_inc(self.k("t"), 1)

    def sync(self, e):
        v = self.env.t
        if self.cached:
            for kk in range(4 * NCH):
                if kk >= 2:
                    e.wait_ge(self.k("m"), self.mval(kk - 2))
                e.dma_start(out=v.wq[kk % 2][:], in_=self.w2q_ap(kk)
                            ).then_inc(self.k(f"a{kk % 2}"), 16)
            return
        for D in range(NCH):
            for qu in range(4):
                kk = 4 * D + qu
                if kk >= 2:
                    e.wait_ge(self.k("q"), kk - 1)
                e.dma_start(out=v.land[kk % 2][:],
                            in_=self.env.dram["w2p"].ap()[D][:, qu * 2048:(qu + 1) * 2048]
                            ).then_inc(self.k(f"a{kk % 2}"), 16)


class ProjPhase:

    def __init__(self, env):
        self.env = env
        self.s = _mk_sems(env, ["pj_b0", "pj_b1", "pj_m", "pj_t"])

    def k(self, n):
        return self.s[n]

    def _mats(self):
        v = self.env.t
        return [("wqp", v.big_q, AF.Identity, QSCALE),
                ("wkp", v.big_k, AF.Sign, 1.0),
                ("wvp", v.big_ktm, AF.Sign, 1.0)]

    def gpsimd(self, e):
        v = self.env.t
        for mi, (wname, _, _, _) in enumerate(self._mats()):
            for O in range(NCH):
                kk = mi * NCH + O
                if kk >= 2:
                    e.wait_ge(self.k("pj_m"), 2 * (kk - 2) + 2)
                e.dma_start(out=v.wq[kk % 2][:], in_=self.env.dram[wname].ap()[O]
                            ).then_inc(self.k(f"pj_b{kk % 2}"), 16)

    def tensor(self, e):
        v = self.env.t
        p = self.env.p
        for mi, (_, _, _, _) in enumerate(self._mats()):
            for O in range(NCH):
                kk = mi * NCH + O
                e.wait_ge(self.k(f"pj_b{kk % 2}"), 16 * (kk // 2 + 1))
                for h in range(2):
                    if kk >= 1:
                        e.wait_ge(self.k("pj_t"), 2 * (kk - 1) + h + 1)
                    for c in range(NCH):
                        mm = e.matmul(p[h][:, :], v.wq[kk % 2][:, c * 128:(c + 1) * 128],
                                      self.env.xln_view(c)[:, h * HALF:(h + 1) * HALF],
                                      start=(c == 0), stop=(c == NCH - 1))
                    mm.then_inc(self.k("pj_m"), 1)

    def scalar(self, e):
        v = self.env.t
        p = self.env.p
        for mi, (_, dst, func, scl) in enumerate(self._mats()):
            for O in range(NCH):
                kk = mi * NCH + O
                for h in range(2):
                    e.wait_ge(self.k("pj_m"), 2 * kk + h + 1)
                    e.activation(dst[:, O * TOK + h * HALF: O * TOK + (h + 1) * HALF],
                                 p[h][:, :], func, scale=scl).then_inc(self.k("pj_t"), 1)

    def sync(self, e):
        pass

    def vector(self, e):
        pass


class TransposePhase:

    def __init__(self, env):
        self.env = env
        self.s = _mk_sems(env, ["tr_p", "tr_v"])

    def _pairs(self):
        v = self.env.t
        return [(v.big_ktm, v.big_v), (v.big_k, v.big_ktm)]

    def tensor(self, e):
        v = self.env.t
        idx = 0
        for (src, dst) in self._pairs():
            for E in range(NCH):
                for tt in range(NLOC):
                    if idx >= 2:
                        e.wait_ge(self.s["tr_v"], idx - 1)
                    e.transpose(self.env.pt[idx % 2][:, :],
                                src[:, E * TOK + tt * 128: E * TOK + (tt + 1) * 128],
                                v.ident[:, :]).then_inc(self.s["tr_p"], 1)
                    idx += 1

    def vector(self, e):
        idx = 0
        for (src, dst) in self._pairs():
            for E in range(NCH):
                for tt in range(NLOC):
                    e.wait_ge(self.s["tr_p"], idx + 1)
                    e.tensor_copy(dst[:, tt * DM + E * 128: tt * DM + (E + 1) * 128],
                                  self.env.pt[idx % 2][:, :]).then_inc(self.s["tr_v"], 1)
                    idx += 1

    def sync(self, e):
        pass

    def gpsimd(self, e):
        pass

    def scalar(self, e):
        pass


class ScanPhase:

    def __init__(self, env):
        self.env = env
        self.s = _mk_sems(env, ["sn_S", "sn_O", "sn_K", "sn_mk", "sn_ad", "sn_ev", "sn_ct", "sn_in",
                                "s0_p", "s0_v", "ag_c", "ag_cc", "ag_g0", "ag_g1", "ag_v"])

    def gpsimd(self, e):
        v = self.env.t
        e.wait_ge(self.s["s0_v"], NH)
        e.dma_start(out=self.env.dram["cc_si"].ap(), in_=v.state[:]).then_inc(self.s["ag_c"], 16)
        e.wait_ge(self.s["ag_c"], 16)
        e.collective_compute(
            "AllGather", ALU.bypass, replica_groups=[list(range(NCORES))],
            ins=[self.env.dram["cc_si"].ap().opt()], outs=[self.env.dram["cc_so"].ap().opt()],
        ).then_inc(self.s["ag_cc"], 1)
        e.wait_ge(self.s["ag_cc"], 1)
        for r in range(NCORES):
            if r >= 2:
                e.wait_ge(self.s["ag_v"], r - 1)
            e.dma_start(out=v.land[r % 2][:],
                        in_=self.env.dram["cc_so"].ap()[r * 128:(r + 1) * 128, :]
                        ).then_inc(self.s[f"ag_g{r % 2}"], 16)

    def vector(self, e):
        v = self.env.t
        for h in range(NH):
            e.wait_ge(self.s["s0_p"], h + 1)
            e.tensor_copy(v.state[:, h * 128:(h + 1) * 128],
                          self.env.p[4 + (h % 2)][:, 0:128]).then_inc(self.s["s0_v"], 1)
        e.wait_ge(self.s["ag_c"], 16)
        e.memset(v.state[:], 0.0).then_inc(self.s["sn_in"], 1)
        self._pass1_vector(e)
        xs = v.xstr_full
        for r in range(NCORES):
            e.wait_ge(self.s[f"ag_g{r % 2}"], 16 * (r // 2 + 1))
            if r == 0:
                e.tensor_scalar(xs[:], v.land[0][:], v.coef[:, 0:1], None, ALU.mult
                                ).then_inc(self.s["ag_v"], 1)
            else:
                e.wait_ge(self.s["ag_v"], r)
                e.scalar_tensor_tensor(xs[:], v.land[r % 2][:], v.coef[:, r:r + 1],
                                       xs[:], ALU.mult, ALU.add).then_inc(self.s["ag_v"], 1)
        e.wait_ge(self.s["ag_v"], NCORES)
        e.wait_ge(self.s["sn_ct"], NLOC * NH)
        for h in range(NH):
            ins = e.tensor_copy(v.roll[:, h * 128:(h + 1) * 128], xs[:, h * 128:(h + 1) * 128])
        ins.then_inc(self.s["ag_v"], 1)

    def _pass1_vector(self, e):
        v = self.env.t
        for n in range(NLOC):
            for h in range(NH):
                idx = n * NH + h
                pi = idx % 2
                e.wait_ge(self.s["sn_S"], idx + 1)
                if idx >= 2:
                    e.wait_ge(self.s["sn_O"], idx - 1)
                e.tensor_mul(v.sm[pi][:, :], self.env.p[2 + pi][:, 0:128], v.mask[:, :]
                             ).then_inc(self.s["sn_mk"], 1)
                e.wait_ge(self.s["sn_K"], idx + 1)
                if n == 0:
                    e.wait_ge(self.s["sn_in"], 1)
                else:
                    e.wait_ge(self.s["sn_ad"], (n - 1) * NH + h + 1)
                e.tensor_add(v.state[:, h * 128:(h + 1) * 128], v.state[:, h * 128:(h + 1) * 128],
                             self.env.p[4 + pi][:, 0:128]).then_inc(self.s["sn_ad"], 1)

    def tensor(self, e):
        v = self.env.t
        p = self.env.p
        for h in range(NH):
            if h >= 2:
                e.wait_ge(self.s["s0_v"], h - 1)
            for n in range(NLOC):
                kv_tm = v.big_ktm[:, n * DM + h * 128: n * DM + (h + 1) * 128]
                vv_tm = v.big_v[:, n * DM + h * 128: n * DM + (h + 1) * 128]
                mm = e.matmul(p[4 + (h % 2)][:, 0:128], kv_tm, vv_tm,
                              start=(n == 0), stop=(n == NLOC - 1))
            mm.then_inc(self.s["s0_p"], 1)
        e.wait_ge(self.s["sn_in"], 1)
        for n in range(NLOC):
            for h in range(NH):
                idx = n * NH + h
                pi = idx % 2
                qv = v.big_q[:, h * TOK + n * 128: h * TOK + (n + 1) * 128]
                kv_fm = v.big_k[:, h * TOK + n * 128: h * TOK + (n + 1) * 128]
                vv_tm = v.big_v[:, n * DM + h * 128: n * DM + (h + 1) * 128]
                kv_tm = v.big_ktm[:, n * DM + h * 128: n * DM + (h + 1) * 128]
                if idx >= 2:
                    e.wait_ge(self.s["sn_ev"], idx - 1)
                if n > 0:
                    e.wait_ge(self.s["sn_ct"], (n - 1) * NH + h + 1)
                    e.matmul(p[pi][:, 0:128], v.roll[:, h * 128:(h + 1) * 128], qv,
                             start=True, stop=False)
                if idx >= 2:
                    e.wait_ge(self.s["sn_mk"], idx - 1)
                e.matmul(p[2 + pi][:, 0:128], kv_fm, qv, start=True, stop=True
                         ).then_inc(self.s["sn_S"], 1)
                if idx >= 2:
                    e.wait_ge(self.s["sn_ad"], idx - 1)
                e.matmul(p[4 + pi][:, 0:128], kv_tm, vv_tm, start=True, stop=True
                         ).then_inc(self.s["sn_K"], 1)
                e.wait_ge(self.s["sn_mk"], idx + 1)
                e.matmul(p[pi][:, 0:128], vv_tm, v.sm[pi][:, :],
                         start=(n == 0), stop=True).then_inc(self.s["sn_O"], 1)

    def scalar(self, e):
        v = self.env.t
        for n in range(NLOC):
            for h in range(NH):
                idx = n * NH + h
                pi = idx % 2
                e.wait_ge(self.s["sn_O"], idx + 1)
                e.activation(v.big_attn[:, h * TOK + n * 128: h * TOK + (n + 1) * 128],
                             self.env.p[pi][:, 0:128], AF.Identity).then_inc(self.s["sn_ev"], 1)
                e.wait_ge(self.s["sn_ad"], idx + 1)
                e.activation(v.roll[:, h * 128:(h + 1) * 128],
                             v.state[:, h * 128:(h + 1) * 128], AF.Identity
                             ).then_inc(self.s["sn_ct"], 1)

    def sync(self, e):
        pass


class AGPhase:

    def __init__(self, env):
        self.env = env
        self.s = _mk_sems(env, ["ag_c", "ag_cc", "ag_g0", "ag_g1", "ag_v"])

    def gpsimd(self, e):
        v = self.env.t
        e.dma_start(out=self.env.dram["cc_si"].ap(), in_=v.state[:]).then_inc(self.s["ag_c"], 16)
        e.wait_ge(self.s["ag_c"], 16)
        e.collective_compute(
            "AllGather", ALU.bypass, replica_groups=[list(range(NCORES))],
            ins=[self.env.dram["cc_si"].ap().opt()], outs=[self.env.dram["cc_so"].ap().opt()],
        ).then_inc(self.s["ag_cc"], 1)
        e.wait_ge(self.s["ag_cc"], 1)
        for r in range(NCORES):
            if r >= 2:
                e.wait_ge(self.s["ag_v"], r)
            e.dma_start(out=v.land[r % 2][:],
                        in_=self.env.dram["cc_so"].ap()[r * 128:(r + 1) * 128, :]
                        ).then_inc(self.s[f"ag_g{r % 2}"], 16)

    def vector(self, e):
        v = self.env.t
        e.wait_ge(self.s["ag_c"], 16)
        e.memset(v.state[:], 0.0).then_inc(self.s["ag_v"], 1)
        for r in range(NCORES):
            e.wait_ge(self.s[f"ag_g{r % 2}"], 16 * (r // 2 + 1))
            e.wait_ge(self.s["ag_v"], 1 + r)
            e.scalar_tensor_tensor(v.state[:], v.land[r % 2][:], v.coef[:, r:r + 1],
                                   v.state[:], ALU.mult, ALU.add).then_inc(self.s["ag_v"], 1)
        e.wait_ge(self.s["ag_v"], 1 + NCORES)
        for h in range(NH):
            ins = e.tensor_copy(v.roll[:, h * 128:(h + 1) * 128], v.state[:, h * 128:(h + 1) * 128])
        ins.then_inc(self.s["ag_v"], 1)

    def sync(self, e):
        pass

    def scalar(self, e):
        pass

    def tensor(self, e):
        pass


class Pass2Phase:

    def __init__(self, env, scan):
        self.env = env
        self.scan = scan
        self.s = _mk_sems(env, ["p2_m", "p2_v"])

    def tensor(self, e):
        v = self.env.t
        p = self.env.p
        e.wait_ge(self.scan.s["ag_v"], NCORES + 1)
        for h in range(NH):
            for hf in range(2):
                idx = 2 * h + hf
                if idx >= 2:
                    e.wait_ge(self.s["p2_v"], idx - 1)
                e.matmul(p[idx % 2][:, :], v.roll[:, h * 128:(h + 1) * 128],
                         v.big_q[:, h * TOK + hf * HALF: h * TOK + (hf + 1) * HALF],
                         start=True, stop=True).then_inc(self.s["p2_m"], 1)

    def vector(self, e):
        v = self.env.t
        for h in range(NH):
            for hf in range(2):
                idx = 2 * h + hf
                e.wait_ge(self.s["p2_m"], idx + 1)
                sl = slice(h * TOK + hf * HALF, h * TOK + (hf + 1) * HALF)
                e.tensor_add(v.big_attn[:, sl], v.big_attn[:, sl], self.env.p[idx % 2][:, :]
                             ).then_inc(self.s["p2_v"], 1)

    def sync(self, e):
        pass

    def gpsimd(self, e):
        pass

    def scalar(self, e):
        pass


class WoPhase:

    def __init__(self, env, x_src_ap, out_ap):
        self.env = env
        self.x_src_ap = x_src_ap
        self.out_ap = out_ap
        self.s = _mk_sems(env, ["wo_a0", "wo_a1", "wo_m", "wo_e", "wo_bl0", "wo_bl1",
                                "wo_bs", "wo_v", "wo_t", "wo_p"])

    def k(self, n):
        return self.s[n]

    def load_wait(self, e, O):
        e.wait_ge(self.k(f"wo_bl{O % 2}"), 16 * (O // 2 + 1))

    def gpsimd(self, e):
        v = self.env.t
        e.dma_start(out=v.xstr[0][:], in_=self.x_src_ap(0)).then_inc(self.k("wo_bl0"), 16)
        e.dma_start(out=v.xstr[1][:], in_=self.x_src_ap(1)).then_inc(self.k("wo_bl1"), 16)
        for O in range(NCH):
            e.wait_ge(self.k("wo_e"), 2 * O + 2)
            e.wait_ge(self.k("wo_v"), self.v_after_cast(O))
            e.wait_ge(self.k("wo_t"), O + 1)
            e.dma_start(out=self.out_ap(O), in_=v.stage[:]).then_inc(self.k("wo_bs"), 16)
            if O + 2 < NCH:
                e.dma_start(out=v.xstr[(O + 2) % 2][:], in_=self.x_src_ap(O + 2)
                            ).then_inc(self.k(f"wo_bl{O % 2}"), 16)
        e.wait_ge(self.k("wo_bs"), 16 * NCH)

    def sync(self, e):
        v = self.env.t
        for O in range(NCH):
            if O >= 2:
                e.wait_ge(self.k("wo_m"), 2 * (O - 2) + 2)
            e.dma_start(out=v.land[O % 2][:], in_=self.env.dram["wop"].ap()[O]
                        ).then_inc(self.k(f"wo_a{O % 2}"), 16)

    def v_after_cast(self, O):
        return 2 * O + 3 if O < NCH - 1 else 2 * NCH

    def vector(self, e):
        v = self.env.t
        for O in range(NCH):
            e.wait_ge(self.k(f"wo_a{O % 2}"), 16 * (O // 2 + 1))
            if O >= 2:
                e.wait_ge(self.k("wo_m"), 2 * (O - 2) + 2)
            e.tensor_copy(v.wq[O % 2][:], v.land[O % 2][:]).then_inc(self.k("wo_v"), 1)
            if O >= 1:
                self._emit_evac(e, O - 1)
        self._emit_evac(e, NCH - 1)

    def _emit_evac(self, e, O):
        v = self.env.t
        done = getattr(self, "_evd", set())
        self._evd = done
        if O in done:
            return
        done.add(O)
        if O >= 1:
            e.wait_ge(self.k("wo_bs"), 16 * O)
        self.load_wait(e, O)
        for h in range(2):
            e.wait_ge(self.k("wo_m"), 2 * O + h + 1)
            e.tensor_add(v.stage[:, h * HALF:(h + 1) * HALF], v.xstr[O % 2][:, h * HALF:(h + 1) * HALF],
                         self.env.p[h][:, :]).then_inc(self.k("wo_e"), 1)
        if O >= 2:
            e.wait_ge(self.k("wo_p"), 4 * (O - 2) + 2)
        e.wait_ge(self.k("wo_e"), 2 * O + 2)
        e.tensor_copy(v.castbuf[O % 2][:], v.stage[:]).then_inc(self.k("wo_v"), 1)

    def tensor(self, e):
        v = self.env.t
        p = self.env.p
        for O in range(NCH):
            e.wait_ge(self.k("wo_v"), self.v_after_wcast(O))
            for h in range(2):
                if O >= 1:
                    e.wait_ge(self.k("wo_e"), 2 * (O - 1) + h + 1)
                for c in range(NCH):
                    mm = e.matmul(p[h][:, :], v.wq[O % 2][:, c * 128:(c + 1) * 128],
                                  v.big_attn[:, c * TOK + h * HALF: c * TOK + (h + 1) * HALF],
                                  start=(c == 0), stop=(c == NCH - 1))
                mm.then_inc(self.k("wo_m"), 1)
            if O >= 1:
                self._emit_stats(e, O - 1)
        self._emit_stats(e, NCH - 1)

    def v_after_wcast(self, O):
        return 1 if O == 0 else 2 * O

    def _emit_stats(self, e, O):
        v = self.env.t
        p = self.env.p
        done = getattr(self, "_std", set())
        self._std = done
        if O in done:
            return
        done.add(O)
        e.wait_ge(self.k("wo_v"), self.v_after_cast(O))
        e.wait_ge(self.k("wo_t"), O + 1)
        st = (O == 0)
        sp = (O == NCH - 1)
        e.matmul(p[2][0:1, :], v.ones_bf[:, 0:1], v.castbuf[O % 2][:, 0:HALF], start=st, stop=sp
                 ).then_inc(self.k("wo_p"), 1)
        e.matmul(p[3][0:1, :], v.ones_bf[:, 0:1], v.castbuf[O % 2][:, HALF:TOK], start=st, stop=sp
                 ).then_inc(self.k("wo_p"), 1)
        e.matmul(p[4][0:1, :], v.ones_bf[:, 0:1], v.sqbuf[O % 2][:, 0:HALF], start=st, stop=sp
                 ).then_inc(self.k("wo_p"), 1)
        e.matmul(p[5][0:1, :], v.ones_bf[:, 0:1], v.sqbuf[O % 2][:, HALF:TOK], start=st, stop=sp
                 ).then_inc(self.k("wo_p"), 1)

    def scalar(self, e):
        v = self.env.t
        for O in range(NCH):
            e.wait_ge(self.k("wo_e"), 2 * O + 2)
            if O >= 2:
                e.wait_ge(self.k("wo_p"), 4 * (O - 2) + 4)
            e.activation(v.sqbuf[O % 2][:], v.stage[:], AF.Square).then_inc(self.k("wo_t"), 1)


class FinalWait:
    def __init__(self, env, sem, val):
        self.env = env
        self.sem = sem
        self.val = val

    def gpsimd(self, e):
        e.wait_ge(self.sem, self.val)

    def sync(self, e):
        pass

    def vector(self, e):
        pass

    def scalar(self, e):
        pass

    def tensor(self, e):
        pass



def build_program(phase_sel=None, io_kinds=None, taps=()):
    io_kinds = io_kinds or {}
    env = Env()
    nc = bass.Bass("TRN2", target_bir_lowering=False)
    env.nc = nc
    ctx = ExitStack()
    env.ctx = ctx

    d = {}
    d["x0T"] = nc.dram_tensor("x0T", [DM, TOK], F32, kind="ExternalInput")
    d["w1p"] = nc.dram_tensor("w1p", [NFB, 128, DM], F32, kind="ExternalInput")
    d["w2p"] = nc.dram_tensor("w2p", [NCH, 128, DFF], F32, kind="ExternalInput")
    d["wqp"] = nc.dram_tensor("wqp", [NCH, 128, DM], F32, kind="ExternalInput")
    d["wkp"] = nc.dram_tensor("wkp", [NCH, 128, DM], F32, kind="ExternalInput")
    d["wvp"] = nc.dram_tensor("wvp", [NCH, 128, DM], F32, kind="ExternalInput")
    d["wop"] = nc.dram_tensor("wop", [NCH, 128, DM], F32, kind="ExternalInput")
    d["b1t"] = nc.dram_tensor("b1t", [128, NFB], F32, kind="ExternalInput")
    d["b2t"] = nc.dram_tensor("b2t", [128, NCH], F32, kind="ExternalInput")
    d["w12slice"] = nc.dram_tensor("w12slice", [16, 128, DM], F32, kind="ExternalInput")
    d["prefix_coef"] = nc.dram_tensor("prefix_coef", [128, 8], F32, kind="ExternalInput")
    d["outT"] = nc.dram_tensor("outT", [DM, TOK], F32, kind="ExternalOutput")
    d["x1d"] = nc.dram_tensor("x1d", [NCH, 128, TOK], F32, kind=io_kinds.get("x1d", "Internal"))
    d["x2d"] = nc.dram_tensor("x2d", [NCH, 128, TOK], F32, kind=io_kinds.get("x2d", "Internal"))
    d["ccs_in"] = nc.dram_tensor("ccs_in", [1, 8], F32)
    d["ccs_out"] = nc.dram_tensor("ccs_out", [1, 8], F32)
    d["w1qd"] = nc.dram_tensor("w1qd", [NFB, 128, DM], BF16)
    d["w2qd"] = nc.dram_tensor("w2qd", [NCH, 128, DFF], BF16)
    d["cc_si"] = nc.dram_tensor("cc_si", [128, DM], F32)
    d["cc_so"] = nc.dram_tensor("cc_so", [128 * NCORES, DM], F32)
    env.dram = d

    base0 = nc.sbuf_base
    base = (base0 + 63) // 64 * 64
    t = Env()
    env.t = t

    def at(name, shape, dtype, off):
        return nc.alloc_sbuf_tensor_at(name, shape, dtype, offset=base + off)

    t.xln = at("xln", [128, NCH * TOK], BF16, 0)
    t.state = at("state", [128, NH * 128], F32, 0)
    t.h_big = at("h_big", [128, NFB * TOK], BF16, 32768)
    t.x0big = at("x0big", [128, NCH * TOK], F32, 32768)
    t.w12slot = [at(f"w12s{i}", [128, DM], F32, 32768 + 65536 + 8192 * i) for i in range(4)]
    t.big_q = at("big_q", [128, NH * TOK], BF16, 32768)
    t.big_k = at("big_k", [128, NH * TOK], BF16, 32768 + 32768)
    t.big_attn = at("big_attn", [128, NH * TOK], BF16, 32768 + 32768)
    t.big_v = at("big_v", [128, NLOC * DM], BF16, 32768 + 65536)
    t.big_ktm = at("big_ktm", [128, NLOC * DM], BF16, 32768 + 98304)
    t.land = [at(f"land{i}", [128, DM], F32, 163840 + 8192 * i) for i in range(2)]
    t.wq = [at(f"wqb{i}", [128, DM], BF16, 180224 + 4096 * i) for i in range(2)]
    t.u_f32 = at("u_f32", [128, DM], F32, 188416)
    t.roll = at("roll", [128, NH * 128], BF16, 188416)
    t.sm = [at(f"sm{i}", [128, 128], BF16, 188416 + 4096 + 256 * i) for i in range(2)]
    xs = at("xstr", [128, 2 * TOK], F32, 196608)
    t.xstr = [xs[:, 0:TOK], xs[:, TOK:2 * TOK]]
    t.xstr_full = xs
    t.rows = at("rows", [128, TOK], F32, 204800)
    t.stage = at("stage", [128, TOK], F32, 204800)
    cb = at("castbuf", [128, DM], BF16, 0)
    t.castbuf = [cb[:, 0:TOK], cb[:, TOK:2 * TOK]]
    sq = at("sqbuf", [128, DM], BF16, 4096 * 1)
    t.sqbuf = [sq[:, 0:TOK], sq[:, TOK:2 * TOK]]
    t.rowA = at("rowA", [1, TOK], F32, 0)
    t.rowC = at("rowC", [1, TOK], F32, 4096)
    off = 208896
    t.mask = at("mask", [128, 128], F32, off); off += 512
    t.ident = at("ident", [128, 128], BF16, off); off += 256
    t.b1t = at("b1t_s", [128, NFB], F32, off); off += 256
    t.b2t = at("b2t_s", [128, NCH], F32, off); off += 64
    t.halfb2t = at("halfb2t", [128, NCH], F32, off); off += 64
    t.thr = at("thr", [128, 16], F32, off); off += 64
    t.nthr = at("nthr", [128, 16], F32, off); off += 64
    t.eps = at("eps", [128, 1], F32, off); off += 64
    t.scalepart = at("scalepart", [128, 16], F32, off); off += 64
    t.onescol = at("onescol", [128, 1], F32, off); off += 64
    t.ones_bf = at("ones_bf", [128, 1], BF16, off); off += 64
    t.onesrow = at("onesrow", [1, 128], F32, off); off += 512
    t.rowstage = at("rowstage", [1, 16], F32, off); off += 64
    t.coef = at("coef", [128, 8], F32, off); off += 64
    assert base + off <= nc.sbuf_top, (base, off, nc.sbuf_top)
    map_end = base + off

    def xln_view(c):
        return t.xln[:, c * TOK:(c + 1) * TOK]
    env.xln_view = xln_view

    nc.sbuf_base = map_end

    phases = []

    with ctx:
        psums = []
        for i in range(6):
            psums.append(ctx.enter_context(nc.psum_tensor(f"P{i}", [128, HALF], F32)))
        env.p = psums
        env.pt = [ctx.enter_context(nc.psum_tensor(f"T{i}", [128, 128], BF16)) for i in range(2)]
        env.bar_sem = ctx.enter_context(nc.semaphore("bar"))
        out_sem = ctx.enter_context(nc.semaphore("outs"))
        env.fences = {w: [ctx.enter_context(nc.semaphore(f"fence_{w}")), 0]
                      for w in ("v", "t", "p", "g")}

        sel = phase_sel

        def want(name):
            return sel is None or name in sel

        def x0_ap(c):
            return d["x0T"].ap()[c * 128:(c + 1) * 128, :]

        def x1_ap(c):
            return d["x1d"].ap()[c]

        def x2_ap(c):
            return d["x2d"].ap()[c]

        def out_ap(c):
            return d["outT"].ap()[c * 128:(c + 1) * 128, :]

        if sel is None:
            f1a = FFNP1(env, "f1a", 0, prefetched=True, writeback=True)
            phases.append(FrontPhase(env, f1a))
            phases.append(Barrier(env))
            phases.append(f1a)
            phases.append(Barrier(env))
            phases.append(FFNP2(env, "f1b", 1, x0_ap, x1_ap, do_stats=True, writeback=True))
            phases.append(Barrier(env))
        else:
            phases.append(Prologue(env))
            phases.append(Barrier(env))
            if want("ffn1") or want("ffn2"):
                phases.append(ScalePhase(env))
                phases.append(Barrier(env))
            if want("ln1"):
                phases.append(LNStats(env, "l1s", x0_ap))
                phases.append(Barrier(env))
                phases.append(LNRowsApply(env, "l1r", x0_ap))
                phases.append(Barrier(env))
            if want("ffn1"):
                phases.append(FFNP1(env, "f1a", 0))
                phases.append(Barrier(env))
                phases.append(FFNP2(env, "f1b", 1, x0_ap, x1_ap, do_stats=True))
                phases.append(Barrier(env))
        if want("ln2"):
            if not want("ffn1"):
                phases.append(LNStats(env, "l2s", x1_ap))
                phases.append(Barrier(env))
            phases.append(LNRowsApply(env, "l2r", x1_ap))
            phases.append(Barrier(env))
        if want("attn"):
            phases.append(ProjPhase(env))
            phases.append(Barrier(env))
            phases.append(TransposePhase(env))
            phases.append(Barrier(env))
            scan = ScanPhase(env)
            phases.append(scan)
            phases.append(Barrier(env))
            phases.append(Pass2Phase(env, scan))
            phases.append(Barrier(env))
            phases.append(WoPhase(env, x1_ap, x2_ap))
            phases.append(Barrier(env))
        if want("ln3"):
            if not want("attn"):
                phases.append(LNStats(env, "l3s", x2_ap))
                phases.append(Barrier(env))
            phases.append(LNRowsApply(env, "l3r", x2_ap))
            phases.append(Barrier(env))
        if want("ffn2"):
            cached = sel is None
            phases.append(FFNP1(env, "f2a", 0, cached=cached))
            phases.append(Barrier(env))
            f2b = FFNP2(env, "f2b", 1, x2_ap, out_ap, do_stats=False, cached=cached)
            phases.append(f2b)
            phases.append(Barrier(env))
            phases.append(FinalWait(env, f2b.k("bs"), 16 * NCH))

        tap_handles = []
        for name in taps:
            src = d[name]
            th = nc.dram_tensor(f"tap_{name}", list(src.shape), src.dtype, kind="ExternalOutput")
            tap_handles.append((th, src))
        if tap_handles:
            phases.append(Barrier(env))

            class TapPhase:
                def gpsimd(self, e):
                    n = 0
                    for th, src in tap_handles:
                        e.dma_start(out=th.ap(), in_=src.ap()).then_inc(out_sem, 16)
                        n += 16
                    e.wait_ge(out_sem, n)

                def sync(self, e): pass
                def vector(self, e): pass
                def scalar(self, e): pass
                def tensor(self, e): pass
            phases.append(TapPhase())

        block = ctx.enter_context(nc.Block())

        @block.sync
        def _(e):
            for ph in phases:
                ph.sync(e)

        @block.gpsimd
        def _(e):
            for ph in phases:
                ph.gpsimd(e)

        @block.vector
        def _(e):
            for ph in phases:
                ph.vector(e)

        @block.scalar
        def _(e):
            for ph in phases:
                ph.scalar(e)

        @block.tensor
        def _(e):
            for ph in phases:
                ph.tensor(e)

    return nc



def stage_inputs(L0_attn_x, L0_attn_Wq, L0_attn_Wk, L0_attn_Wv, L0_attn_Wo,
                 L0_ffn_w1, L0_ffn_b1, L0_ffn_w2, L0_ffn_b2):
    x = np.asarray(L0_attn_x, dtype=np.float32)
    w1 = np.ascontiguousarray(np.asarray(L0_ffn_w1, dtype=np.float32))
    w2 = np.ascontiguousarray(np.asarray(L0_ffn_w2, dtype=np.float32))
    b1 = np.asarray(L0_ffn_b1, dtype=np.float32)
    b2 = np.asarray(L0_ffn_b2, dtype=np.float32)

    def pack_ot(W, nb):
        o, i = W.shape
        return np.ascontiguousarray(
            W.reshape(nb, 128, i // 128, 128).transpose(0, 3, 2, 1).reshape(nb, 128, i))

    w1p = pack_ot(w1, NFB)
    w2p = pack_ot(w2, NCH)
    wqp = pack_ot(np.asarray(L0_attn_Wq, np.float32), NCH)
    wkp = pack_ot(np.asarray(L0_attn_Wk, np.float32), NCH)
    wvp = pack_ot(np.asarray(L0_attn_Wv, np.float32), NCH)
    wop = pack_ot(np.asarray(L0_attn_Wo, np.float32), NCH)
    b1t = np.ascontiguousarray(b1.reshape(NFB, 128).T)
    b2t = np.ascontiguousarray(b2.reshape(NCH, 128).T)

    in_maps = []
    for c in range(NCORES):
        b = c // 4
        t0 = (c % 4) * TOK
        x0T = np.ascontiguousarray(x[b, t0:t0 + TOK, :].T)
        w1sl = w1p[c * 8:(c + 1) * 8]
        w2sl = w2p[c * 2:(c + 1) * 2].reshape(2, 128, 4, 2048).transpose(0, 2, 1, 3).reshape(8, 128, 2048)
        w12slice = np.ascontiguousarray(np.concatenate([w1sl, w2sl], axis=0))
        coef = np.zeros((128, 8), np.float32)
        g0 = 4 * (c // 4)
        for r in range(g0, c):
            coef[:, r] = 1.0
        in_maps.append(dict(
            x0T=x0T, w1p=w1p, w2p=w2p, wqp=wqp, wkp=wkp, wvp=wvp, wop=wop,
            b1t=b1t, b2t=b2t, w12slice=w12slice, prefix_coef=coef,
        ))
    return in_maps


def kernel(**inputs):
    in_maps = stage_inputs(**inputs)
    nc = build_program()
    res = run_bass_kernel_spmd(nc, in_maps, core_ids=list(range(NCORES)))
    out = np.empty((2, 4096, DM), dtype=np.float32)
    for c in range(NCORES):
        b = c // 4
        t0 = (c % 4) * TOK
        out[b, t0:t0 + TOK, :] = res.results[c]["outT"].T
    return out


# version 06
# speedup vs baseline: 1.1071x; 1.0164x over previous
import math
import numpy as np
import ml_dtypes
from contextlib import ExitStack

import concourse.bass as bass
import concourse.mybir as mybir
from concourse.bass_utils import run_bass_kernel_spmd

F32 = mybir.dt.float32
BF16 = mybir.dt.bfloat16
AF = mybir.ActivationFunctionType
ALU = mybir.AluOpType
AX = mybir.AxisListType

TOK = 1024
DM = 2048
DFF = 8192
NCH = 16
NFB = 64
NH = 16
CH = 128
NLOC = 8
HALF = 512
QSCALE = float(1.0 / np.sqrt(128.0))
INV_D = float(1.0 / 2048.0)
MEAN_SCL = float(0.7 / 16777216.0)
LN_EPS = 1e-5

NCORES = 8



FENCE_SAME_ENGINE = True


class Env:
    def fence(self, which, e, ins):
        if not FENCE_SAME_ENGINE:
            return
        sem, cnt = self.fences[which]
        cnt += 1
        self.fences[which][1] = cnt
        ins.then_inc(sem, 1)
        e.wait_ge(sem, cnt)


def _mk_sems(env, names):
    d = {}
    for n in names:
        d[n] = env.ctx.enter_context(env.nc.semaphore(n))
    return d



class Barrier:

    def __init__(self, env):
        env.bar_count = getattr(env, "bar_count", 0) + 1
        self.k = env.bar_count
        self.sem = env.bar_sem

    def _do(self, eng):
        eng.drain().then_inc(self.sem, 1)
        eng.wait_ge(self.sem, 5 * self.k)

    def sync(self, e): self._do(e)
    def gpsimd(self, e): self._do(e)
    def vector(self, e): self._do(e)
    def scalar(self, e): self._do(e)
    def tensor(self, e): self._do(e)


class Prologue:

    def __init__(self, env):
        self.env = env
        self.s = _mk_sems(env, ["pro_d", "pro_g", "pro_v"])

    def sync(self, e):
        v = self.env.t
        e.dma_start(out=v.b1t[:], in_=self.env.dram["b1t"].ap()).then_inc(self.s["pro_d"], 16)
        e.dma_start(out=v.b2t[:], in_=self.env.dram["b2t"].ap()).then_inc(self.s["pro_d"], 16)
        e.dma_start(out=v.coef[:], in_=self.env.dram["prefix_coef"].ap()).then_inc(self.s["pro_d"], 16)

    def gpsimd(self, e):
        v = self.env.t
        e.iota(v.u_f32[:, 0:128], [[1, 128]], channel_multiplier=128,
               allow_small_or_imprecise_dtypes=True)
        e.iota(v.u_f32[:, 128:129], [[1, 1]], channel_multiplier=129,
               allow_small_or_imprecise_dtypes=True).then_inc(self.s["pro_g"], 1)

    def vector(self, e):
        v = self.env.t
        e.wait_ge(self.s["pro_g"], 1)
        e.tensor_scalar(v.mask[:], v.u_f32[:, 0:128], v.u_f32[:, 128:129], None, ALU.is_ge)
        e.tensor_scalar(v.ident[:], v.u_f32[:, 0:128], v.u_f32[:, 128:129], None, ALU.is_equal)
        e.memset(v.eps[:], LN_EPS)
        e.memset(v.rowstage[:, :], 0.0)
        e.memset(v.onescol[:], 1.0)
        e.memset(v.onesrow[:], 1.0)
        e.memset(v.ones_bf[:], 1.0)
        e.wait_ge(self.s["pro_d"], 48)
        e.tensor_scalar_mul(v.halfb2t[:], v.b2t[:], 0.5)

    def scalar(self, e):
        pass

    def tensor(self, e):
        pass


class FrontPhase:

    def __init__(self, env, f1a):
        self.env = env
        self.f1a = f1a
        self.s = _mk_sems(env, ["fr_d", "fr_g", "fr_x0", "fr_a0", "fr_a1", "fr_a2", "fr_a3",
                                "fr_sv", "fr_lv", "fr_lp", "fr_t", "fr_p", "fr_c", "fr_cc"])

    def x0v(self, c):
        return self.env.t.x0big[:, c * TOK:(c + 1) * TOK]

    def sync(self, e):
        v = self.env.t
        for i in range(16):
            if i >= 4:
                e.wait_ge(self.s["fr_sv"], i - 3)
            e.dma_start(out=v.w12slot[i % 4][:], in_=self.env.dram["w12slice"].ap()[i]
                        ).then_inc(self.s[f"fr_a{i % 4}"], 16)
        e.dma_start(out=v.land[0][:], in_=self.env.dram["w1p"].ap()[0]
                    ).then_inc(self.f1a.k("a0"), 16)
        e.dma_start(out=v.land[1][:], in_=self.env.dram["w1p"].ap()[1]
                    ).then_inc(self.f1a.k("a1"), 16)

    def gpsimd(self, e):
        v = self.env.t
        e.dma_start(out=v.x0big[:].rearrange("p (c t) -> p c t", c=NCH),
                    in_=self.env.dram["x0T"].ap().rearrange("(c p) t -> p c t", p=128)
                    ).then_inc(self.s["fr_x0"], 16)
        e.dma_start(out=v.b1t[:], in_=self.env.dram["b1t"].ap()).then_inc(self.s["fr_d"], 16)
        e.dma_start(out=v.b2t[:], in_=self.env.dram["b2t"].ap()).then_inc(self.s["fr_d"], 16)
        e.dma_start(out=v.coef[:], in_=self.env.dram["prefix_coef"].ap()).then_inc(self.s["fr_d"], 16)
        e.iota(v.u_f32[:, 0:128], [[1, 128]], channel_multiplier=128,
               allow_small_or_imprecise_dtypes=True)
        e.iota(v.u_f32[:, 128:129], [[1, 1]], channel_multiplier=129,
               allow_small_or_imprecise_dtypes=True).then_inc(self.s["fr_g"], 1)
        e.wait_ge(self.s["fr_sv"], 18)
        e.dma_start(out=self.env.dram["ccs_in"].ap(), in_=v.rowstage[0:1, 0:8]).then_inc(self.s["fr_c"], 16)
        e.wait_ge(self.s["fr_c"], 16)
        e.collective_compute(
            "AllReduce", ALU.add, replica_groups=[list(range(NCORES))],
            ins=[self.env.dram["ccs_in"].ap().opt()], outs=[self.env.dram["ccs_out"].ap().opt()],
        ).then_inc(self.s["fr_cc"], 1)
        e.wait_ge(self.s["fr_cc"], 1)
        e.dma_start(out=v.rowstage[0:1, 0:8], in_=self.env.dram["ccs_out"].ap()).then_inc(self.s["fr_c"], 16)
        e.wait_ge(self.s["fr_c"], 32)

    def vector(self, e):
        v = self.env.t
        env = self.env
        e.wait_ge(self.s["fr_g"], 1)
        e.tensor_scalar(v.mask[:], v.u_f32[:, 0:128], v.u_f32[:, 128:129], None, ALU.is_ge)
        ins = e.tensor_scalar(v.ident[:], v.u_f32[:, 0:128], v.u_f32[:, 128:129], None, ALU.is_equal)
        env.fence("v", e, ins)
        e.memset(v.eps[:], LN_EPS)
        e.memset(v.rowstage[:, :], 0.0)
        e.memset(v.onescol[:], 1.0)
        e.memset(v.onesrow[:], 1.0)
        e.memset(v.ones_bf[:], 1.0)
        e.wait_ge(self.s["fr_d"], 48)
        e.tensor_scalar_mul(v.halfb2t[:], v.b2t[:], 0.5)
        for i in range(16):
            e.wait_ge(self.s[f"fr_a{i % 4}"], 16 * (i // 4 + 1))
            e.tensor_reduce(v.scalepart[:, i:i + 1], v.w12slot[i % 4][:], AX.X, ALU.add,
                            apply_absolute_value=True).then_inc(self.s["fr_sv"], 1)
        e.wait_ge(self.s["fr_sv"], 16)
        e.tensor_reduce(v.thr[:, 0:1], v.scalepart[:, 0:8], AX.X, ALU.add)
        e.tensor_reduce(v.thr[:, 1:2], v.scalepart[:, 8:16], AX.X, ALU.add).then_inc(self.s["fr_sv"], 1)
        e.wait_ge(self.s["fr_x0"], 16)
        for c in range(NCH):
            if c >= 2:
                e.wait_ge(self.s["fr_lp"], 4 * (c - 2) + 2)
            e.tensor_copy(v.castbuf[c % 2][:], self.x0v(c)).then_inc(self.s["fr_lv"], 1)
        e.wait_ge(self.s["fr_p"], 2)
        e.tensor_copy(v.rowstage[0:1, 0:2], env.p[0][0:1, 0:2]).then_inc(self.s["fr_sv"], 1)
        e.wait_ge(self.s["fr_lp"], 64)
        A, B, C = v.rowA[:, :], v.rows[0:1, :], v.rowC[:, :]
        p = env.p
        for j in range(2):
            ins = e.tensor_scalar_mul(A[:, j * HALF:(j + 1) * HALF], p[2 + j][0:1, :], INV_D)
        env.fence("v", e, ins)
        ins = e.tensor_mul(C[:, :], A[:, :], A[:, :])
        env.fence("v", e, ins)
        for j in range(2):
            ins = e.scalar_tensor_tensor(C[:, j * HALF:(j + 1) * HALF], p[4 + j][0:1, :], INV_D,
                                         C[:, j * HALF:(j + 1) * HALF], ALU.mult, ALU.subtract)
        ins.then_inc(self.s["fr_lv"], 1)
        e.wait_ge(self.s["fr_t"], 17)
        ins = e.reciprocal(B[:, :], B[:, :])
        env.fence("v", e, ins)
        e.tensor_mul(C[:, :], A[:, :], B[:, :]).then_inc(self.s["fr_lv"], 1)
        e.wait_ge(self.s["fr_lp"], 68)
        rb = v.stage[:, 0:TOK]
        mrb = v.u_f32[:, TOK:2 * TOK]
        for j in range(2):
            e.tensor_copy(rb[:, j * HALF:(j + 1) * HALF], p[2 + j][:, :])
            ins = e.tensor_copy(mrb[:, j * HALF:(j + 1) * HALF], p[4 + j][:, :])
        env.fence("v", e, ins)
        for c in range(NCH):
            if c >= 1:
                e.wait_ge(self.s["fr_lv"], 18 + c)
            tins = e.tensor_mul(v.u_f32[:, 0:TOK], self.x0v(c), rb)
            env.fence("v", e, tins)
            e.tensor_sub(env.xln_view(c), v.u_f32[:, 0:TOK], mrb).then_inc(self.s["fr_lv"], 1)
        e.wait_ge(self.s["fr_c"], 32)
        e.tensor_scalar_mul(v.rowstage[0:1, 8:10], v.rowstage[0:1, 0:2], MEAN_SCL).then_inc(self.s["fr_sv"], 1)
        e.wait_ge(self.s["fr_p"], 4)
        e.tensor_copy(v.thr[:, 0:2], env.p[1][:, 0:2])
        e.tensor_scalar_mul(v.nthr[:, 0:2], env.p[1][:, 0:2], -1.0).then_inc(self.s["fr_sv"], 1)

    def scalar(self, e):
        v = self.env.t
        e.wait_ge(self.s["fr_x0"], 16)
        for c in range(NCH):
            if c >= 2:
                e.wait_ge(self.s["fr_lp"], 4 * (c - 2) + 4)
            e.activation(v.sqbuf[c % 2][:], self.x0v(c), AF.Square).then_inc(self.s["fr_t"], 1)
        e.wait_ge(self.s["fr_lv"], 17)
        A, B, C = v.rowA[:, :], v.rows[0:1, :], v.rowC[:, :]
        e.activation(B[:, :], C[:, :], AF.Sqrt, bias=v.eps[0:1, 0:1]).then_inc(self.s["fr_t"], 1)

    def tensor(self, e):
        v = self.env.t
        p = self.env.p
        for c in range(NCH):
            e.wait_ge(self.s["fr_lv"], c + 1)
            e.wait_ge(self.s["fr_t"], c + 1)
            st = (c == 0)
            sp = (c == NCH - 1)
            e.matmul(p[2][0:1, :], v.ones_bf[:, 0:1], v.castbuf[c % 2][:, 0:HALF], start=st, stop=sp
                     ).then_inc(self.s["fr_lp"], 1)
            e.matmul(p[3][0:1, :], v.ones_bf[:, 0:1], v.castbuf[c % 2][:, HALF:TOK], start=st, stop=sp
                     ).then_inc(self.s["fr_lp"], 1)
            e.matmul(p[4][0:1, :], v.ones_bf[:, 0:1], v.sqbuf[c % 2][:, 0:HALF], start=st, stop=sp
                     ).then_inc(self.s["fr_lp"], 1)
            e.matmul(p[5][0:1, :], v.ones_bf[:, 0:1], v.sqbuf[c % 2][:, HALF:TOK], start=st, stop=sp
                     ).then_inc(self.s["fr_lp"], 1)
        e.wait_ge(self.s["fr_sv"], 17)
        e.matmul(p[0][0:1, 0:1], v.thr[:, 0:1], v.onescol[:, 0:1], start=True, stop=True
                 ).then_inc(self.s["fr_p"], 1)
        e.matmul(p[0][0:1, 1:2], v.thr[:, 1:2], v.onescol[:, 0:1], start=True, stop=True
                 ).then_inc(self.s["fr_p"], 1)
        e.wait_ge(self.s["fr_lv"], 18)
        A, B, C = v.rowA[:, :], v.rows[0:1, :], v.rowC[:, :]
        for j in range(2):
            e.matmul(p[2 + j][:, :], v.onesrow[0:1, :], B[:, j * HALF:(j + 1) * HALF],
                     start=True, stop=True).then_inc(self.s["fr_lp"], 1)
            e.matmul(p[4 + j][:, :], v.onesrow[0:1, :], C[:, j * HALF:(j + 1) * HALF],
                     start=True, stop=True).then_inc(self.s["fr_lp"], 1)
        e.wait_ge(self.s["fr_sv"], 19)
        e.matmul(p[1][:, 0:1], v.onesrow[0:1, :], v.rowstage[0:1, 8:9], start=True, stop=True
                 ).then_inc(self.s["fr_p"], 1)
        e.matmul(p[1][:, 1:2], v.onesrow[0:1, :], v.rowstage[0:1, 9:10], start=True, stop=True
                 ).then_inc(self.s["fr_p"], 1)


class ScalePhase:

    def __init__(self, env):
        self.env = env
        self.s = _mk_sems(env, ["sc_a0", "sc_a1", "sc_v", "sc_p", "sc_c", "sc_cc"])

    def sync(self, e):
        v = self.env.t
        for i in range(16):
            if i >= 2:
                e.wait_ge(self.s["sc_v"], i - 1)
            e.dma_start(out=v.land[i % 2][:], in_=self.env.dram["w12slice"].ap()[i]
                        ).then_inc(self.s[f"sc_a{i % 2}"], 16)

    def vector(self, e):
        v = self.env.t
        for i in range(16):
            e.wait_ge(self.s[f"sc_a{i % 2}"], 16 * (i // 2 + 1))
            e.tensor_reduce(v.scalepart[:, i:i + 1], v.land[i % 2][:], AX.X, ALU.add,
                            apply_absolute_value=True).then_inc(self.s["sc_v"], 1)
        e.wait_ge(self.s["sc_v"], 16)
        e.tensor_reduce(v.thr[:, 0:1], v.scalepart[:, 0:8], AX.X, ALU.add)
        e.tensor_reduce(v.thr[:, 1:2], v.scalepart[:, 8:16], AX.X, ALU.add).then_inc(self.s["sc_v"], 1)
        e.wait_ge(self.s["sc_p"], 2)
        e.tensor_copy(v.rowstage[0:1, 0:2], self.env.p[2][0:1, 0:2]).then_inc(self.s["sc_v"], 1)
        e.wait_ge(self.s["sc_c"], 32)
        e.tensor_scalar_mul(v.rowstage[0:1, 8:10], v.rowstage[0:1, 0:2], MEAN_SCL).then_inc(self.s["sc_v"], 1)
        e.wait_ge(self.s["sc_p"], 4)
        e.tensor_copy(v.thr[:, 0:2], self.env.p[3][:, 0:2])
        e.tensor_scalar_mul(v.nthr[:, 0:2], self.env.p[3][:, 0:2], -1.0).then_inc(self.s["sc_v"], 1)

    def tensor(self, e):
        v = self.env.t
        e.wait_ge(self.s["sc_v"], 17)
        e.matmul(self.env.p[2][0:1, 0:1], v.thr[:, 0:1], v.onescol[:, 0:1], start=True, stop=True
                 ).then_inc(self.s["sc_p"], 1)
        e.matmul(self.env.p[2][0:1, 1:2], v.thr[:, 1:2], v.onescol[:, 0:1], start=True, stop=True
                 ).then_inc(self.s["sc_p"], 1)
        e.wait_ge(self.s["sc_v"], 19)
        e.matmul(self.env.p[3][:, 0:1], v.onesrow[0:1, :], v.rowstage[0:1, 8:9], start=True, stop=True
                 ).then_inc(self.s["sc_p"], 1)
        e.matmul(self.env.p[3][:, 1:2], v.onesrow[0:1, :], v.rowstage[0:1, 9:10], start=True, stop=True
                 ).then_inc(self.s["sc_p"], 1)

    def gpsimd(self, e):
        v = self.env.t
        e.wait_ge(self.s["sc_v"], 18)
        e.dma_start(out=self.env.dram["ccs_in"].ap(), in_=v.rowstage[0:1, 0:8]).then_inc(self.s["sc_c"], 16)
        e.wait_ge(self.s["sc_c"], 16)
        e.collective_compute(
            "AllReduce", ALU.add, replica_groups=[list(range(NCORES))],
            ins=[self.env.dram["ccs_in"].ap().opt()], outs=[self.env.dram["ccs_out"].ap().opt()],
        ).then_inc(self.s["sc_cc"], 1)
        e.wait_ge(self.s["sc_cc"], 1)
        e.dma_start(out=v.rowstage[0:1, 0:8], in_=self.env.dram["ccs_out"].ap()).then_inc(self.s["sc_c"], 16)
        e.wait_ge(self.s["sc_c"], 32)

    def scalar(self, e):
        pass


class LNStats:

    def __init__(self, env, tag, src_ap):
        self.env = env
        self.src_ap = src_ap
        self.s = _mk_sems(env, [f"{tag}_b0", f"{tag}_b1", f"{tag}_v", f"{tag}_t", f"{tag}_p"])
        self.tag = tag

    def k(self, n):
        return self.s[f"{self.tag}_{n}"]

    def bwait(self, e, c):
        e.wait_ge(self.k(f"b{c % 2}"), 16 * (c // 2 + 1))

    def gpsimd(self, e):
        v = self.env.t
        for c in range(NCH):
            if c >= 2:
                e.wait_ge(self.k("v"), c - 1)
                e.wait_ge(self.k("t"), c - 1)
            e.dma_start(out=v.xstr[c % 2][:], in_=self.src_ap(c)).then_inc(self.k(f"b{c % 2}"), 16)

    def vector(self, e):
        v = self.env.t
        for c in range(NCH):
            self.bwait(e, c)
            if c >= 2:
                e.wait_ge(self.k("p"), 4 * (c - 2) + 2)
            e.tensor_copy(v.castbuf[c % 2][:], v.xstr[c % 2][:]).then_inc(self.k("v"), 1)

    def scalar(self, e):
        v = self.env.t
        for c in range(NCH):
            self.bwait(e, c)
            if c >= 2:
                e.wait_ge(self.k("p"), 4 * (c - 2) + 4)
            e.activation(v.sqbuf[c % 2][:], v.xstr[c % 2][:], AF.Square).then_inc(self.k("t"), 1)

    def tensor(self, e):
        v = self.env.t
        p = self.env.p
        for c in range(NCH):
            e.wait_ge(self.k("v"), c + 1)
            e.wait_ge(self.k("t"), c + 1)
            st = (c == 0)
            sp = (c == NCH - 1)
            e.matmul(p[2][0:1, :], v.ones_bf[:, 0:1], v.castbuf[c % 2][:, 0:HALF], start=st, stop=sp
                     ).then_inc(self.k("p"), 1)
            e.matmul(p[3][0:1, :], v.ones_bf[:, 0:1], v.castbuf[c % 2][:, HALF:TOK], start=st, stop=sp
                     ).then_inc(self.k("p"), 1)
            e.matmul(p[4][0:1, :], v.ones_bf[:, 0:1], v.sqbuf[c % 2][:, 0:HALF], start=st, stop=sp
                     ).then_inc(self.k("p"), 1)
            e.matmul(p[5][0:1, :], v.ones_bf[:, 0:1], v.sqbuf[c % 2][:, HALF:TOK], start=st, stop=sp
                     ).then_inc(self.k("p"), 1)

    def sync(self, e):
        pass


class LNRowsApply:

    def __init__(self, env, tag, src_ap):
        self.env = env
        self.src_ap = src_ap
        self.s = _mk_sems(env, [f"{tag}_b0", f"{tag}_b1", f"{tag}_v", f"{tag}_t", f"{tag}_p"])
        self.tag = tag

    def k(self, n):
        return self.s[f"{self.tag}_{n}"]

    def _rows(self):
        v = self.env.t
        return v.rowA[:, :], v.rows[0:1, :], v.rowC[:, :]

    def vector(self, e):
        v = self.env.t
        p = self.env.p
        A, B, C = self._rows()
        for j in range(2):
            ins = e.tensor_scalar_mul(A[:, j * HALF:(j + 1) * HALF], p[2 + j][0:1, :], INV_D)
        self.env.fence("v", e, ins)
        ins = e.tensor_mul(C[:, :], A[:, :], A[:, :])
        self.env.fence("v", e, ins)
        for j in range(2):
            ins = e.scalar_tensor_tensor(C[:, j * HALF:(j + 1) * HALF], p[4 + j][0:1, :], INV_D,
                                         C[:, j * HALF:(j + 1) * HALF], ALU.mult, ALU.subtract)
        ins.then_inc(self.k("v"), 1)
        e.wait_ge(self.k("t"), 1)
        ins = e.reciprocal(B[:, :], B[:, :])
        self.env.fence("v", e, ins)
        e.tensor_mul(C[:, :], A[:, :], B[:, :]).then_inc(self.k("v"), 1)
        e.wait_ge(self.k("p"), 4)
        rb = v.stage[:, 0:TOK]
        mrb = v.u_f32[:, TOK:2 * TOK]
        for j in range(2):
            e.tensor_copy(rb[:, j * HALF:(j + 1) * HALF], p[2 + j][:, :])
            ins = e.tensor_copy(mrb[:, j * HALF:(j + 1) * HALF], p[4 + j][:, :])
        self.env.fence("v", e, ins)
        for c in range(NCH):
            e.wait_ge(self.k(f"b{c % 2}"), 16 * (c // 2 + 1))
            if c >= 1:
                e.wait_ge(self.k("v"), 2 + c)
            tins = e.tensor_mul(v.u_f32[:, 0:TOK], v.xstr[c % 2][:, 0:TOK], rb)
            self.env.fence("v", e, tins)
            e.tensor_sub(self.env.xln_view(c), v.u_f32[:, 0:TOK], mrb
                         ).then_inc(self.k("v"), 1)

    def scalar(self, e):
        v = self.env.t
        A, B, C = self._rows()
        e.wait_ge(self.k("v"), 1)
        e.activation(B[:, :], C[:, :], AF.Sqrt, bias=v.eps[0:1, 0:1]).then_inc(self.k("t"), 1)

    def tensor(self, e):
        v = self.env.t
        p = self.env.p
        A, B, C = self._rows()
        e.wait_ge(self.k("v"), 2)
        for j in range(2):
            e.matmul(p[2 + j][:, :], v.onesrow[0:1, :], B[:, j * HALF:(j + 1) * HALF],
                     start=True, stop=True).then_inc(self.k("p"), 1)
            e.matmul(p[4 + j][:, :], v.onesrow[0:1, :], C[:, j * HALF:(j + 1) * HALF],
                     start=True, stop=True).then_inc(self.k("p"), 1)

    def gpsimd(self, e):
        v = self.env.t
        for c in range(NCH):
            if c >= 2:
                e.wait_ge(self.k("v"), 2 + c - 1)
            e.dma_start(out=v.xstr[c % 2][:], in_=self.src_ap(c)).then_inc(self.k(f"b{c % 2}"), 16)

    def sync(self, e):
        pass


class FFNP1:

    def __init__(self, env, tag, thr_col, prefetched=False, writeback=False, cached=False):
        self.env = env
        self.thr_col = thr_col
        self.prefetched = prefetched
        self.writeback = writeback
        self.cached = cached
        names = [f"{tag}_a0", f"{tag}_a1", f"{tag}_m", f"{tag}_g"]
        if not cached:
            names.append(f"{tag}_q")
        if writeback:
            names += [f"{tag}_w0", f"{tag}_w1"]
        self.s = _mk_sems(env, names)
        self.tag = tag

    def k(self, n):
        return self.s[f"{self.tag}_{n}"]

    def sync(self, e):
        v = self.env.t
        if self.cached:
            for F in range(NFB):
                if F >= 2:
                    e.wait_ge(self.k("m"), 2 * (F - 2) + 2)
                e.dma_start(out=v.wq[F % 2][:], in_=self.env.dram["w1qd"].ap()[F]
                            ).then_inc(self.k(f"a{F % 2}"), 16)
            return
        for F in range(2 if self.prefetched else 0, NFB):
            if F >= 2:
                e.wait_ge(self.k("q"), F - 1)
            e.dma_start(out=v.land[F % 2][:], in_=self.env.dram["w1p"].ap()[F]
                        ).then_inc(self.k(f"a{F % 2}"), 16)

    def vector(self, e):
        if self.cached:
            return
        v = self.env.t
        tc = self.thr_col
        for F in range(NFB):
            e.wait_ge(self.k(f"a{F % 2}"), 16 * (F // 2 + 1))
            if F >= 2:
                e.wait_ge(self.k("m"), 2 * (F - 2) + 2)
                if self.writeback:
                    e.wait_ge(self.k(f"w{F % 2}"), 16 * ((F - 2) // 2 + 1))
            if F >= 1:
                e.wait_ge(self.k("q"), F)
            uins = e.tensor_scalar(v.u_f32[:], v.land[F % 2][:], v.nthr[:, tc:tc + 1], None, ALU.is_le)
            self.env.fence("v", e, uins)
            e.scalar_tensor_tensor(v.wq[F % 2][:], v.land[F % 2][:], v.thr[:, tc:tc + 1],
                                   v.u_f32[:], ALU.is_ge, ALU.subtract).then_inc(self.k("q"), 1)

    def tensor(self, e):
        v = self.env.t
        p = self.env.p
        for F in range(NFB):
            if self.cached:
                e.wait_ge(self.k(f"a{F % 2}"), 16 * (F // 2 + 1))
            else:
                e.wait_ge(self.k("q"), F + 1)
            for h in range(2):
                if F >= 1:
                    e.wait_ge(self.k("g"), 2 * (F - 1) + h + 1)
                for c in range(NCH):
                    mm = e.matmul(p[h][:, :], v.wq[F % 2][:, c * 128:(c + 1) * 128],
                                  self.env.xln_view(c)[:, h * HALF:(h + 1) * HALF],
                                  start=(c == 0), stop=(c == NCH - 1))
                mm.then_inc(self.k("m"), 1)

    def scalar(self, e):
        v = self.env.t
        p = self.env.p
        for F in range(NFB):
            for h in range(2):
                e.wait_ge(self.k("m"), 2 * F + h + 1)
                e.activation(v.h_big[:, F * TOK + h * HALF: F * TOK + (h + 1) * HALF],
                             p[h][:, :], AF.Gelu, bias=v.b1t[:, F:F + 1], scale=1.0
                             ).then_inc(self.k("g"), 1)

    def gpsimd(self, e):
        if not self.writeback:
            return
        v = self.env.t
        for F in range(NFB):
            e.wait_ge(self.k("q"), F + 1)
            e.dma_start(out=self.env.dram["w1qd"].ap()[F], in_=v.wq[F % 2][:]
                        ).then_inc(self.k(f"w{F % 2}"), 16)
        e.wait_ge(self.k("w0"), 16 * (NFB // 2))
        e.wait_ge(self.k("w1"), 16 * (NFB // 2))


class FFNP2:

    def __init__(self, env, tag, thr_col, x_src_ap, out_ap, do_stats,
                 writeback=False, cached=False):
        self.env = env
        self.thr_col = thr_col
        self.x_src_ap = x_src_ap
        self.out_ap = out_ap
        self.do_stats = do_stats
        self.writeback = writeback
        self.cached = cached
        self.banks4 = not do_stats
        names = [f"{tag}_a0", f"{tag}_a1", f"{tag}_m", f"{tag}_e",
                 f"{tag}_bl0", f"{tag}_bl1", f"{tag}_bs", f"{tag}_v"]
        if not cached:
            names.append(f"{tag}_q")
        if do_stats:
            names += [f"{tag}_t", f"{tag}_p"]
        if writeback:
            names += [f"{tag}_w0", f"{tag}_w1"]
        self.s = _mk_sems(env, names)
        self.tag = tag

    def w2q_ap(self, kk):
        D, qu = kk // 4, kk % 4
        return self.env.dram["w2qd"].ap()[D][:, qu * 2048:(qu + 1) * 2048]

    split_q3 = False

    def mval(self, kk):
        D, qu = kk // 4, kk % 4
        if not self.split_q3:
            return 4 * D + qu + 1
        return 5 * D + (qu + 1 if qu < 3 else 5)

    def m_after_D(self, D):
        return (4 if not self.split_q3 else 5) * (D + 1)

    def k(self, n):
        return self.s[f"{self.tag}_{n}"]

    def load_wait(self, e, D):
        e.wait_ge(self.k(f"bl{D % 2}"), 16 * (D // 2 + 1))

    def gpsimd(self, e):
        v = self.env.t
        e.dma_start(out=v.xstr[0][:], in_=self.x_src_ap(0)).then_inc(self.k("bl0"), 16)
        e.dma_start(out=v.xstr[1][:], in_=self.x_src_ap(1)).then_inc(self.k("bl1"), 16)
        for D in range(NCH):
            if self.writeback:
                for qu in range(4):
                    kk = 4 * D + qu
                    e.wait_ge(self.k("q"), kk + 1)
                    e.dma_start(out=self.w2q_ap(kk), in_=v.wq[kk % 2][:]
                                ).then_inc(self.k(f"w{kk % 2}"), 16)
            e.wait_ge(self.k("e"), 2 * D + 2)
            if self.do_stats:
                e.wait_ge(self.k("v"), self.v_after_cast(D))
                e.wait_ge(self.k("t"), D + 1)
            e.dma_start(out=self.out_ap(D), in_=v.stage[:]).then_inc(self.k("bs"), 16)
            if D + 2 < NCH:
                e.dma_start(out=v.xstr[(D + 2) % 2][:], in_=self.x_src_ap(D + 2)
                            ).then_inc(self.k(f"bl{D % 2}"), 16)
        e.wait_ge(self.k("bs"), 16 * NCH)
        if self.writeback:
            e.wait_ge(self.k("w0"), 16 * (2 * NCH))
            e.wait_ge(self.k("w1"), 16 * (2 * NCH))

    def v_after_cast(self, D):
        assert self.do_stats
        return 2 + 2 * D if D < NCH - 1 else 2 * NCH

    def v_after_xp(self, Dp):
        if Dp == 0:
            return 1
        if self.do_stats:
            return 2 * Dp + 1
        return Dp + 1

    def vector(self, e):
        v = self.env.t
        tc = self.thr_col
        self.load_wait(e, 0)
        e.tensor_scalar(v.xstr[0][:], v.xstr[0][:], v.halfb2t[:, 0:1], None, ALU.add
                        ).then_inc(self.k("v"), 1)
        for D in range(NCH):
            if D >= 1:
                self._emit_evac(e, D - 1)
            if self.cached:
                continue
            for qu in range(4):
                kk = 4 * D + qu
                e.wait_ge(self.k(f"a{kk % 2}"), 16 * (kk // 2 + 1))
                if kk >= 2:
                    e.wait_ge(self.k("m"), self.mval(kk - 2))
                    if self.writeback:
                        e.wait_ge(self.k(f"w{kk % 2}"), 16 * ((kk - 2) // 2 + 1))
                if kk >= 1:
                    e.wait_ge(self.k("q"), kk)
                uins = e.tensor_scalar(v.u_f32[:], v.land[kk % 2][:], v.nthr[:, tc:tc + 1], None, ALU.is_le)
                self.env.fence("v", e, uins)
                e.scalar_tensor_tensor(v.wq[kk % 2][:], v.land[kk % 2][:], v.thr[:, tc:tc + 1],
                                       v.u_f32[:], ALU.is_ge, ALU.subtract).then_inc(self.k("q"), 1)
        self._emit_evac(e, NCH - 1)

    def _emit_evac(self, e, D):
        v = self.env.t
        if D in getattr(self, "_evac_done", set()):
            return
        self._evac_done = getattr(self, "_evac_done", set())
        self._evac_done.add(D)
        if D >= 1:
            e.wait_ge(self.k("bs"), 16 * D)
        e.wait_ge(self.k("v"), self.v_after_xp(D))
        if not self.split_q3:
            e.wait_ge(self.k("m"), self.m_after_D(D))
        for h in range(2):
            if self.split_q3:
                e.wait_ge(self.k("m"), 5 * D + 4 + h)
            pb = self.env.p[2 * (D % 2) + h] if self.banks4 else self.env.p[h]
            e.scalar_tensor_tensor(v.stage[:, h * HALF:(h + 1) * HALF], pb[:, :], 0.5,
                                   v.xstr[D % 2][:, h * HALF:(h + 1) * HALF], ALU.mult, ALU.add
                                   ).then_inc(self.k("e"), 1)
        if self.do_stats:
            if D >= 2:
                e.wait_ge(self.k("p"), 4 * (D - 2) + 2)
            e.wait_ge(self.k("e"), 2 * D + 2)
            e.tensor_copy(self.env.t.castbuf[D % 2][:], v.stage[:]).then_inc(self.k("v"), 1)
        if D + 1 < NCH:
            self.load_wait(e, D + 1)
            e.tensor_scalar(v.xstr[(D + 1) % 2][:], v.xstr[(D + 1) % 2][:],
                            v.halfb2t[:, D + 1:D + 2], None, ALU.add).then_inc(self.k("v"), 1)

    def tensor(self, e):
        v = self.env.t
        p = self.env.p
        for D in range(NCH):
            if self.banks4:
                if D >= 2:
                    e.wait_ge(self.k("e"), 2 * (D - 2) + 2)
                pb0 = p[2 * (D % 2)]
                pb1 = p[2 * (D % 2) + 1]
            else:
                if D >= 1:
                    e.wait_ge(self.k("e"), 2 * (D - 1) + 2)
                pb0, pb1 = p[0], p[1]
            for qu in range(4):
                kk = 4 * D + qu
                if self.cached:
                    e.wait_ge(self.k(f"a{kk % 2}"), 16 * (kk // 2 + 1))
                else:
                    e.wait_ge(self.k("q"), kk + 1)
                if self.split_q3 and qu == 3:
                    for i in range(16):
                        F = qu * 16 + i
                        mm = e.matmul(pb0[:, :], v.wq[kk % 2][:, i * 128:(i + 1) * 128],
                                      v.h_big[:, F * TOK: F * TOK + HALF],
                                      start=False, stop=(i == 15))
                    mm.then_inc(self.k("m"), 1)
                    for i in range(16):
                        F = qu * 16 + i
                        mm = e.matmul(pb1[:, :], v.wq[kk % 2][:, i * 128:(i + 1) * 128],
                                      v.h_big[:, F * TOK + HALF: F * TOK + TOK],
                                      start=False, stop=(i == 15))
                    mm.then_inc(self.k("m"), 1)
                    continue
                for i in range(16):
                    F = qu * 16 + i
                    st = (qu == 0 and i == 0)
                    sp = ((not self.split_q3) and qu == 3 and i == 15)
                    e.matmul(pb0[:, :], v.wq[kk % 2][:, i * 128:(i + 1) * 128],
                             v.h_big[:, F * TOK: F * TOK + HALF], start=st, stop=sp)
                    mm = e.matmul(pb1[:, :], v.wq[kk % 2][:, i * 128:(i + 1) * 128],
                                  v.h_big[:, F * TOK + HALF: F * TOK + TOK], start=st, stop=sp)
                mm.then_inc(self.k("m"), 1)
            if self.do_stats and D >= 1:
                self._emit_stats(e, D - 1)
        if self.do_stats:
            self._emit_stats(e, NCH - 1)

    def _emit_stats(self, e, D):
        v = self.env.t
        p = self.env.p
        if D in getattr(self, "_stats_done", set()):
            return
        self._stats_done = getattr(self, "_stats_done", set())
        self._stats_done.add(D)
        e.wait_ge(self.k("v"), self.v_after_cast(D))
        e.wait_ge(self.k("t"), D + 1)
        st = (D == 0)
        sp = (D == NCH - 1)
        e.matmul(p[2][0:1, :], v.ones_bf[:, 0:1], v.castbuf[D % 2][:, 0:HALF], start=st, stop=sp
                 ).then_inc(self.k("p"), 1)
        e.matmul(p[3][0:1, :], v.ones_bf[:, 0:1], v.castbuf[D % 2][:, HALF:TOK], start=st, stop=sp
                 ).then_inc(self.k("p"), 1)
        e.matmul(p[4][0:1, :], v.ones_bf[:, 0:1], v.sqbuf[D % 2][:, 0:HALF], start=st, stop=sp
                 ).then_inc(self.k("p"), 1)
        e.matmul(p[5][0:1, :], v.ones_bf[:, 0:1], v.sqbuf[D % 2][:, HALF:TOK], start=st, stop=sp
                 ).then_inc(self.k("p"), 1)

    def scalar(self, e):
        if not self.do_stats:
            return
        v = self.env.t
        for D in range(NCH):
            e.wait_ge(self.k("e"), 2 * D + 2)
            if D >= 2:
                e.wait_ge(self.k("p"), 4 * (D - 2) + 4)
            e.activation(v.sqbuf[D % 2][:], v.stage[:], AF.Square).then_inc(self.k("t"), 1)

    def sync(self, e):
        v = self.env.t
        if self.cached:
            for kk in range(4 * NCH):
                if kk >= 2:
                    e.wait_ge(self.k("m"), self.mval(kk - 2))
                e.dma_start(out=v.wq[kk % 2][:], in_=self.w2q_ap(kk)
                            ).then_inc(self.k(f"a{kk % 2}"), 16)
            return
        for D in range(NCH):
            for qu in range(4):
                kk = 4 * D + qu
                if kk >= 2:
                    e.wait_ge(self.k("q"), kk - 1)
                e.dma_start(out=v.land[kk % 2][:],
                            in_=self.env.dram["w2p"].ap()[D][:, qu * 2048:(qu + 1) * 2048]
                            ).then_inc(self.k(f"a{kk % 2}"), 16)


class ProjPhase:

    def __init__(self, env):
        self.env = env
        self.s = _mk_sems(env, ["pj_b0", "pj_b1", "pj_m", "pj_t"])

    def k(self, n):
        return self.s[n]

    def _mats(self):
        v = self.env.t
        return [("wqp", v.big_q, AF.Identity, QSCALE),
                ("wkp", v.big_k, AF.Sign, 1.0),
                ("wvp", v.big_ktm, AF.Sign, 1.0)]

    def gpsimd(self, e):
        v = self.env.t
        for mi, (wname, _, _, _) in enumerate(self._mats()):
            for O in range(NCH):
                kk = mi * NCH + O
                if kk >= 2:
                    e.wait_ge(self.k("pj_m"), 2 * (kk - 2) + 2)
                e.dma_start(out=v.wq[kk % 2][:], in_=self.env.dram[wname].ap()[O]
                            ).then_inc(self.k(f"pj_b{kk % 2}"), 16)

    def tensor(self, e):
        v = self.env.t
        p = self.env.p
        for mi, (_, _, _, _) in enumerate(self._mats()):
            for O in range(NCH):
                kk = mi * NCH + O
                e.wait_ge(self.k(f"pj_b{kk % 2}"), 16 * (kk // 2 + 1))
                for h in range(2):
                    if kk >= 1:
                        e.wait_ge(self.k("pj_t"), 2 * (kk - 1) + h + 1)
                    for c in range(NCH):
                        mm = e.matmul(p[h][:, :], v.wq[kk % 2][:, c * 128:(c + 1) * 128],
                                      self.env.xln_view(c)[:, h * HALF:(h + 1) * HALF],
                                      start=(c == 0), stop=(c == NCH - 1))
                    mm.then_inc(self.k("pj_m"), 1)

    def scalar(self, e):
        v = self.env.t
        p = self.env.p
        for mi, (_, dst, func, scl) in enumerate(self._mats()):
            for O in range(NCH):
                kk = mi * NCH + O
                for h in range(2):
                    e.wait_ge(self.k("pj_m"), 2 * kk + h + 1)
                    e.activation(dst[:, O * TOK + h * HALF: O * TOK + (h + 1) * HALF],
                                 p[h][:, :], func, scale=scl).then_inc(self.k("pj_t"), 1)

    def sync(self, e):
        pass

    def vector(self, e):
        pass


class TransposePhase:

    def __init__(self, env):
        self.env = env
        self.s = _mk_sems(env, ["tr_p", "tr_v"])

    def _pairs(self):
        v = self.env.t
        return [(v.big_ktm, v.big_v), (v.big_k, v.big_ktm)]

    def tensor(self, e):
        v = self.env.t
        idx = 0
        for (src, dst) in self._pairs():
            for E in range(NCH):
                for tt in range(NLOC):
                    if idx >= 2:
                        e.wait_ge(self.s["tr_v"], idx - 1)
                    e.transpose(self.env.pt[idx % 2][:, :],
                                src[:, E * TOK + tt * 128: E * TOK + (tt + 1) * 128],
                                v.ident[:, :]).then_inc(self.s["tr_p"], 1)
                    idx += 1

    def vector(self, e):
        idx = 0
        for (src, dst) in self._pairs():
            for E in range(NCH):
                for tt in range(NLOC):
                    e.wait_ge(self.s["tr_p"], idx + 1)
                    e.tensor_copy(dst[:, tt * DM + E * 128: tt * DM + (E + 1) * 128],
                                  self.env.pt[idx % 2][:, :]).then_inc(self.s["tr_v"], 1)
                    idx += 1

    def sync(self, e):
        pass

    def gpsimd(self, e):
        pass

    def scalar(self, e):
        pass


class ScanPhase:

    def __init__(self, env):
        self.env = env
        self.s = _mk_sems(env, ["sn_S", "sn_O", "sn_K", "sn_mk", "sn_ad", "sn_ev", "sn_ct", "sn_in",
                                "s0_p", "s0_v", "ag_c", "ag_cc", "ag_g0", "ag_g1", "ag_v"])

    def gpsimd(self, e):
        v = self.env.t
        e.wait_ge(self.s["s0_v"], NH)
        e.dma_start(out=self.env.dram["cc_si"].ap(), in_=v.state[:]).then_inc(self.s["ag_c"], 16)
        e.wait_ge(self.s["ag_c"], 16)
        e.collective_compute(
            "AllGather", ALU.bypass, replica_groups=[list(range(NCORES))],
            ins=[self.env.dram["cc_si"].ap().opt()], outs=[self.env.dram["cc_so"].ap().opt()],
        ).then_inc(self.s["ag_cc"], 1)
        e.wait_ge(self.s["ag_cc"], 1)
        for r in range(NCORES):
            if r >= 2:
                e.wait_ge(self.s["ag_v"], r - 1)
            e.dma_start(out=v.land[r % 2][:],
                        in_=self.env.dram["cc_so"].ap()[r * 128:(r + 1) * 128, :]
                        ).then_inc(self.s[f"ag_g{r % 2}"], 16)

    def vector(self, e):
        v = self.env.t
        for h in range(NH):
            e.wait_ge(self.s["s0_p"], h + 1)
            e.tensor_copy(v.state[:, h * 128:(h + 1) * 128],
                          self.env.p[4 + (h % 2)][:, 0:128]).then_inc(self.s["s0_v"], 1)
        e.wait_ge(self.s["ag_c"], 16)
        e.memset(v.state[:], 0.0).then_inc(self.s["sn_in"], 1)
        self._pass1_vector(e)
        xs = v.xstr_full
        for r in range(NCORES):
            e.wait_ge(self.s[f"ag_g{r % 2}"], 16 * (r // 2 + 1))
            if r == 0:
                e.tensor_scalar(xs[:], v.land[0][:], v.coef[:, 0:1], None, ALU.mult
                                ).then_inc(self.s["ag_v"], 1)
            else:
                e.wait_ge(self.s["ag_v"], r)
                e.scalar_tensor_tensor(xs[:], v.land[r % 2][:], v.coef[:, r:r + 1],
                                       xs[:], ALU.mult, ALU.add).then_inc(self.s["ag_v"], 1)
        e.wait_ge(self.s["ag_v"], NCORES)
        e.wait_ge(self.s["sn_ct"], NLOC * NH)
        for h in range(NH):
            ins = e.tensor_copy(v.roll[:, h * 128:(h + 1) * 128], xs[:, h * 128:(h + 1) * 128])
        ins.then_inc(self.s["ag_v"], 1)

    def _pass1_vector(self, e):
        v = self.env.t
        for n in range(NLOC):
            for h in range(NH):
                idx = n * NH + h
                pi = idx % 2
                e.wait_ge(self.s["sn_S"], idx + 1)
                if idx >= 2:
                    e.wait_ge(self.s["sn_O"], idx - 1)
                e.tensor_mul(v.sm[pi][:, :], self.env.p[2 + pi][:, 0:128], v.mask[:, :]
                             ).then_inc(self.s["sn_mk"], 1)
                e.wait_ge(self.s["sn_K"], idx + 1)
                if n == 0:
                    e.wait_ge(self.s["sn_in"], 1)
                else:
                    e.wait_ge(self.s["sn_ad"], (n - 1) * NH + h + 1)
                e.tensor_add(v.state[:, h * 128:(h + 1) * 128], v.state[:, h * 128:(h + 1) * 128],
                             self.env.p[4 + pi][:, 0:128]).then_inc(self.s["sn_ad"], 1)

    def tensor(self, e):
        v = self.env.t
        p = self.env.p
        for h in range(NH):
            if h >= 2:
                e.wait_ge(self.s["s0_v"], h - 1)
            for n in range(NLOC):
                kv_tm = v.big_ktm[:, n * DM + h * 128: n * DM + (h + 1) * 128]
                vv_tm = v.big_v[:, n * DM + h * 128: n * DM + (h + 1) * 128]
                mm = e.matmul(p[4 + (h % 2)][:, 0:128], kv_tm, vv_tm,
                              start=(n == 0), stop=(n == NLOC - 1))
            mm.then_inc(self.s["s0_p"], 1)
        e.wait_ge(self.s["sn_in"], 1)
        for n in range(NLOC):
            for h in range(NH):
                idx = n * NH + h
                pi = idx % 2
                qv = v.big_q[:, h * TOK + n * 128: h * TOK + (n + 1) * 128]
                kv_fm = v.big_k[:, h * TOK + n * 128: h * TOK + (n + 1) * 128]
                vv_tm = v.big_v[:, n * DM + h * 128: n * DM + (h + 1) * 128]
                kv_tm = v.big_ktm[:, n * DM + h * 128: n * DM + (h + 1) * 128]
                if idx >= 2:
                    e.wait_ge(self.s["sn_ev"], idx - 1)
                if n > 0:
                    e.wait_ge(self.s["sn_ct"], (n - 1) * NH + h + 1)
                    e.matmul(p[pi][:, 0:128], v.roll[:, h * 128:(h + 1) * 128], qv,
                             start=True, stop=False)
                if idx >= 2:
                    e.wait_ge(self.s["sn_mk"], idx - 1)
                e.matmul(p[2 + pi][:, 0:128], kv_fm, qv, start=True, stop=True
                         ).then_inc(self.s["sn_S"], 1)
                if idx >= 2:
                    e.wait_ge(self.s["sn_ad"], idx - 1)
                e.matmul(p[4 + pi][:, 0:128], kv_tm, vv_tm, start=True, stop=True
                         ).then_inc(self.s["sn_K"], 1)
                e.wait_ge(self.s["sn_mk"], idx + 1)
                e.matmul(p[pi][:, 0:128], vv_tm, v.sm[pi][:, :],
                         start=(n == 0), stop=True).then_inc(self.s["sn_O"], 1)

    def scalar(self, e):
        v = self.env.t
        for n in range(NLOC):
            for h in range(NH):
                idx = n * NH + h
                pi = idx % 2
                e.wait_ge(self.s["sn_O"], idx + 1)
                e.activation(v.big_attn[:, h * TOK + n * 128: h * TOK + (n + 1) * 128],
                             self.env.p[pi][:, 0:128], AF.Identity).then_inc(self.s["sn_ev"], 1)
                e.wait_ge(self.s["sn_ad"], idx + 1)
                e.activation(v.roll[:, h * 128:(h + 1) * 128],
                             v.state[:, h * 128:(h + 1) * 128], AF.Identity
                             ).then_inc(self.s["sn_ct"], 1)

    def sync(self, e):
        pass


class AGPhase:

    def __init__(self, env):
        self.env = env
        self.s = _mk_sems(env, ["ag_c", "ag_cc", "ag_g0", "ag_g1", "ag_v"])

    def gpsimd(self, e):
        v = self.env.t
        e.dma_start(out=self.env.dram["cc_si"].ap(), in_=v.state[:]).then_inc(self.s["ag_c"], 16)
        e.wait_ge(self.s["ag_c"], 16)
        e.collective_compute(
            "AllGather", ALU.bypass, replica_groups=[list(range(NCORES))],
            ins=[self.env.dram["cc_si"].ap().opt()], outs=[self.env.dram["cc_so"].ap().opt()],
        ).then_inc(self.s["ag_cc"], 1)
        e.wait_ge(self.s["ag_cc"], 1)
        for r in range(NCORES):
            if r >= 2:
                e.wait_ge(self.s["ag_v"], r)
            e.dma_start(out=v.land[r % 2][:],
                        in_=self.env.dram["cc_so"].ap()[r * 128:(r + 1) * 128, :]
                        ).then_inc(self.s[f"ag_g{r % 2}"], 16)

    def vector(self, e):
        v = self.env.t
        e.wait_ge(self.s["ag_c"], 16)
        e.memset(v.state[:], 0.0).then_inc(self.s["ag_v"], 1)
        for r in range(NCORES):
            e.wait_ge(self.s[f"ag_g{r % 2}"], 16 * (r // 2 + 1))
            e.wait_ge(self.s["ag_v"], 1 + r)
            e.scalar_tensor_tensor(v.state[:], v.land[r % 2][:], v.coef[:, r:r + 1],
                                   v.state[:], ALU.mult, ALU.add).then_inc(self.s["ag_v"], 1)
        e.wait_ge(self.s["ag_v"], 1 + NCORES)
        for h in range(NH):
            ins = e.tensor_copy(v.roll[:, h * 128:(h + 1) * 128], v.state[:, h * 128:(h + 1) * 128])
        ins.then_inc(self.s["ag_v"], 1)

    def sync(self, e):
        pass

    def scalar(self, e):
        pass

    def tensor(self, e):
        pass


class Pass2Phase:

    def __init__(self, env, scan):
        self.env = env
        self.scan = scan
        self.s = _mk_sems(env, ["p2_m", "p2_v"])

    def tensor(self, e):
        v = self.env.t
        p = self.env.p
        e.wait_ge(self.scan.s["ag_v"], NCORES + 1)
        for h in range(NH):
            for hf in range(2):
                idx = 2 * h + hf
                if idx >= 2:
                    e.wait_ge(self.s["p2_v"], idx - 1)
                e.matmul(p[idx % 2][:, :], v.roll[:, h * 128:(h + 1) * 128],
                         v.big_q[:, h * TOK + hf * HALF: h * TOK + (hf + 1) * HALF],
                         start=True, stop=True).then_inc(self.s["p2_m"], 1)

    def vector(self, e):
        v = self.env.t
        for h in range(NH):
            for hf in range(2):
                idx = 2 * h + hf
                e.wait_ge(self.s["p2_m"], idx + 1)
                sl = slice(h * TOK + hf * HALF, h * TOK + (hf + 1) * HALF)
                e.tensor_add(v.big_attn[:, sl], v.big_attn[:, sl], self.env.p[idx % 2][:, :]
                             ).then_inc(self.s["p2_v"], 1)

    def sync(self, e):
        pass

    def gpsimd(self, e):
        pass

    def scalar(self, e):
        pass


class WoPhase:

    def __init__(self, env, x_src_ap, out_ap):
        self.env = env
        self.x_src_ap = x_src_ap
        self.out_ap = out_ap
        self.s = _mk_sems(env, ["wo_a0", "wo_a1", "wo_m", "wo_e", "wo_bl0", "wo_bl1",
                                "wo_bs", "wo_v", "wo_t", "wo_p"])

    def k(self, n):
        return self.s[n]

    def load_wait(self, e, O):
        e.wait_ge(self.k(f"wo_bl{O % 2}"), 16 * (O // 2 + 1))

    def gpsimd(self, e):
        v = self.env.t
        e.dma_start(out=v.xstr[0][:], in_=self.x_src_ap(0)).then_inc(self.k("wo_bl0"), 16)
        e.dma_start(out=v.xstr[1][:], in_=self.x_src_ap(1)).then_inc(self.k("wo_bl1"), 16)
        for O in range(NCH):
            e.wait_ge(self.k("wo_e"), 2 * O + 2)
            e.wait_ge(self.k("wo_v"), self.v_after_cast(O))
            e.wait_ge(self.k("wo_t"), O + 1)
            e.dma_start(out=self.out_ap(O), in_=v.stage[:]).then_inc(self.k("wo_bs"), 16)
            if O + 2 < NCH:
                e.dma_start(out=v.xstr[(O + 2) % 2][:], in_=self.x_src_ap(O + 2)
                            ).then_inc(self.k(f"wo_bl{O % 2}"), 16)
        e.wait_ge(self.k("wo_bs"), 16 * NCH)

    def sync(self, e):
        v = self.env.t
        for O in range(NCH):
            if O >= 2:
                e.wait_ge(self.k("wo_m"), 2 * (O - 2) + 2)
            e.dma_start(out=v.land[O % 2][:], in_=self.env.dram["wop"].ap()[O]
                        ).then_inc(self.k(f"wo_a{O % 2}"), 16)

    def v_after_cast(self, O):
        return 2 * O + 3 if O < NCH - 1 else 2 * NCH

    def vector(self, e):
        v = self.env.t
        for O in range(NCH):
            e.wait_ge(self.k(f"wo_a{O % 2}"), 16 * (O // 2 + 1))
            if O >= 2:
                e.wait_ge(self.k("wo_m"), 2 * (O - 2) + 2)
            e.tensor_copy(v.wq[O % 2][:], v.land[O % 2][:]).then_inc(self.k("wo_v"), 1)
            if O >= 1:
                self._emit_evac(e, O - 1)
        self._emit_evac(e, NCH - 1)

    def _emit_evac(self, e, O):
        v = self.env.t
        done = getattr(self, "_evd", set())
        self._evd = done
        if O in done:
            return
        done.add(O)
        if O >= 1:
            e.wait_ge(self.k("wo_bs"), 16 * O)
        self.load_wait(e, O)
        for h in range(2):
            e.wait_ge(self.k("wo_m"), 2 * O + h + 1)
            e.tensor_add(v.stage[:, h * HALF:(h + 1) * HALF], v.xstr[O % 2][:, h * HALF:(h + 1) * HALF],
                         self.env.p[h][:, :]).then_inc(self.k("wo_e"), 1)
        if O >= 2:
            e.wait_ge(self.k("wo_p"), 4 * (O - 2) + 2)
        e.wait_ge(self.k("wo_e"), 2 * O + 2)
        e.tensor_copy(v.castbuf[O % 2][:], v.stage[:]).then_inc(self.k("wo_v"), 1)

    def tensor(self, e):
        v = self.env.t
        p = self.env.p
        for O in range(NCH):
            e.wait_ge(self.k("wo_v"), self.v_after_wcast(O))
            for h in range(2):
                if O >= 1:
                    e.wait_ge(self.k("wo_e"), 2 * (O - 1) + h + 1)
                for c in range(NCH):
                    mm = e.matmul(p[h][:, :], v.wq[O % 2][:, c * 128:(c + 1) * 128],
                                  v.big_attn[:, c * TOK + h * HALF: c * TOK + (h + 1) * HALF],
                                  start=(c == 0), stop=(c == NCH - 1))
                mm.then_inc(self.k("wo_m"), 1)
            if O >= 1:
                self._emit_stats(e, O - 1)
        self._emit_stats(e, NCH - 1)

    def v_after_wcast(self, O):
        return 1 if O == 0 else 2 * O

    def _emit_stats(self, e, O):
        v = self.env.t
        p = self.env.p
        done = getattr(self, "_std", set())
        self._std = done
        if O in done:
            return
        done.add(O)
        e.wait_ge(self.k("wo_v"), self.v_after_cast(O))
        e.wait_ge(self.k("wo_t"), O + 1)
        st = (O == 0)
        sp = (O == NCH - 1)
        e.matmul(p[2][0:1, :], v.ones_bf[:, 0:1], v.castbuf[O % 2][:, 0:HALF], start=st, stop=sp
                 ).then_inc(self.k("wo_p"), 1)
        e.matmul(p[3][0:1, :], v.ones_bf[:, 0:1], v.castbuf[O % 2][:, HALF:TOK], start=st, stop=sp
                 ).then_inc(self.k("wo_p"), 1)
        e.matmul(p[4][0:1, :], v.ones_bf[:, 0:1], v.sqbuf[O % 2][:, 0:HALF], start=st, stop=sp
                 ).then_inc(self.k("wo_p"), 1)
        e.matmul(p[5][0:1, :], v.ones_bf[:, 0:1], v.sqbuf[O % 2][:, HALF:TOK], start=st, stop=sp
                 ).then_inc(self.k("wo_p"), 1)

    def scalar(self, e):
        v = self.env.t
        for O in range(NCH):
            e.wait_ge(self.k("wo_e"), 2 * O + 2)
            if O >= 2:
                e.wait_ge(self.k("wo_p"), 4 * (O - 2) + 4)
            e.activation(v.sqbuf[O % 2][:], v.stage[:], AF.Square).then_inc(self.k("wo_t"), 1)


class FinalWait:
    def __init__(self, env, sem, val):
        self.env = env
        self.sem = sem
        self.val = val

    def gpsimd(self, e):
        e.wait_ge(self.sem, self.val)

    def sync(self, e):
        pass

    def vector(self, e):
        pass

    def scalar(self, e):
        pass

    def tensor(self, e):
        pass



def build_program(phase_sel=None, io_kinds=None, taps=()):
    io_kinds = io_kinds or {}
    env = Env()
    nc = bass.Bass("TRN2", target_bir_lowering=False)
    env.nc = nc
    ctx = ExitStack()
    env.ctx = ctx

    d = {}
    d["x0T"] = nc.dram_tensor("x0T", [DM, TOK], F32, kind="ExternalInput")
    d["w1p"] = nc.dram_tensor("w1p", [NFB, 128, DM], F32, kind="ExternalInput")
    d["w2p"] = nc.dram_tensor("w2p", [NCH, 128, DFF], F32, kind="ExternalInput")
    d["wqp"] = nc.dram_tensor("wqp", [NCH, 128, DM], F32, kind="ExternalInput")
    d["wkp"] = nc.dram_tensor("wkp", [NCH, 128, DM], F32, kind="ExternalInput")
    d["wvp"] = nc.dram_tensor("wvp", [NCH, 128, DM], F32, kind="ExternalInput")
    d["wop"] = nc.dram_tensor("wop", [NCH, 128, DM], F32, kind="ExternalInput")
    d["b1t"] = nc.dram_tensor("b1t", [128, NFB], F32, kind="ExternalInput")
    d["b2t"] = nc.dram_tensor("b2t", [128, NCH], F32, kind="ExternalInput")
    d["w12slice"] = nc.dram_tensor("w12slice", [16, 128, DM], F32, kind="ExternalInput")
    d["prefix_coef"] = nc.dram_tensor("prefix_coef", [128, 8], F32, kind="ExternalInput")
    d["outT"] = nc.dram_tensor("outT", [DM, TOK], F32, kind="ExternalOutput")
    d["x1d"] = nc.dram_tensor("x1d", [NCH, 128, TOK], F32, kind=io_kinds.get("x1d", "Internal"))
    d["x2d"] = nc.dram_tensor("x2d", [NCH, 128, TOK], F32, kind=io_kinds.get("x2d", "Internal"))
    d["ccs_in"] = nc.dram_tensor("ccs_in", [1, 8], F32)
    d["ccs_out"] = nc.dram_tensor("ccs_out", [1, 8], F32)
    d["w1qd"] = nc.dram_tensor("w1qd", [NFB, 128, DM], BF16)
    d["w2qd"] = nc.dram_tensor("w2qd", [NCH, 128, DFF], BF16)
    d["cc_si"] = nc.dram_tensor("cc_si", [128, DM], F32)
    d["cc_so"] = nc.dram_tensor("cc_so", [128 * NCORES, DM], F32)
    env.dram = d

    base0 = nc.sbuf_base
    base = (base0 + 63) // 64 * 64
    t = Env()
    env.t = t

    def at(name, shape, dtype, off):
        return nc.alloc_sbuf_tensor_at(name, shape, dtype, offset=base + off)

    t.xln = at("xln", [128, NCH * TOK], BF16, 0)
    t.state = at("state", [128, NH * 128], F32, 0)
    t.h_big = at("h_big", [128, NFB * TOK], BF16, 32768)
    t.x0big = at("x0big", [128, NCH * TOK], F32, 32768)
    t.w12slot = [at(f"w12s{i}", [128, DM], F32, 32768 + 65536 + 8192 * i) for i in range(4)]
    t.big_q = at("big_q", [128, NH * TOK], BF16, 32768)
    t.big_k = at("big_k", [128, NH * TOK], BF16, 32768 + 32768)
    t.big_attn = at("big_attn", [128, NH * TOK], BF16, 32768 + 32768)
    t.big_v = at("big_v", [128, NLOC * DM], BF16, 32768 + 65536)
    t.big_ktm = at("big_ktm", [128, NLOC * DM], BF16, 32768 + 98304)
    t.land = [at(f"land{i}", [128, DM], F32, 163840 + 8192 * i) for i in range(2)]
    t.wq = [at(f"wqb{i}", [128, DM], BF16, 180224 + 4096 * i) for i in range(2)]
    t.u_f32 = at("u_f32", [128, DM], F32, 188416)
    t.roll = at("roll", [128, NH * 128], BF16, 188416)
    t.sm = [at(f"sm{i}", [128, 128], BF16, 188416 + 4096 + 256 * i) for i in range(2)]
    xs = at("xstr", [128, 2 * TOK], F32, 196608)
    t.xstr = [xs[:, 0:TOK], xs[:, TOK:2 * TOK]]
    t.xstr_full = xs
    t.rows = at("rows", [128, TOK], F32, 204800)
    t.stage = at("stage", [128, TOK], F32, 204800)
    cb = at("castbuf", [128, DM], BF16, 0)
    t.castbuf = [cb[:, 0:TOK], cb[:, TOK:2 * TOK]]
    sq = at("sqbuf", [128, DM], BF16, 4096 * 1)
    t.sqbuf = [sq[:, 0:TOK], sq[:, TOK:2 * TOK]]
    t.rowA = at("rowA", [1, TOK], F32, 0)
    t.rowC = at("rowC", [1, TOK], F32, 4096)
    off = 208896
    t.mask = at("mask", [128, 128], F32, off); off += 512
    t.ident = at("ident", [128, 128], BF16, off); off += 256
    t.b1t = at("b1t_s", [128, NFB], F32, off); off += 256
    t.b2t = at("b2t_s", [128, NCH], F32, off); off += 64
    t.halfb2t = at("halfb2t", [128, NCH], F32, off); off += 64
    t.thr = at("thr", [128, 16], F32, off); off += 64
    t.nthr = at("nthr", [128, 16], F32, off); off += 64
    t.eps = at("eps", [128, 1], F32, off); off += 64
    t.scalepart = at("scalepart", [128, 16], F32, off); off += 64
    t.onescol = at("onescol", [128, 1], F32, off); off += 64
    t.ones_bf = at("ones_bf", [128, 1], BF16, off); off += 64
    t.onesrow = at("onesrow", [1, 128], F32, off); off += 512
    t.rowstage = at("rowstage", [1, 16], F32, off); off += 64
    t.coef = at("coef", [128, 8], F32, off); off += 64
    assert base + off <= nc.sbuf_top, (base, off, nc.sbuf_top)
    map_end = base + off

    def xln_view(c):
        return t.xln[:, c * TOK:(c + 1) * TOK]
    env.xln_view = xln_view

    nc.sbuf_base = map_end

    phases = []

    with ctx:
        psums = []
        for i in range(6):
            psums.append(ctx.enter_context(nc.psum_tensor(f"P{i}", [128, HALF], F32)))
        env.p = psums
        env.pt = [ctx.enter_context(nc.psum_tensor(f"T{i}", [128, 128], BF16)) for i in range(2)]
        env.bar_sem = ctx.enter_context(nc.semaphore("bar"))
        out_sem = ctx.enter_context(nc.semaphore("outs"))
        env.fences = {w: [ctx.enter_context(nc.semaphore(f"fence_{w}")), 0]
                      for w in ("v", "t", "p", "g")}

        sel = phase_sel

        def want(name):
            return sel is None or name in sel

        def x0_ap(c):
            return d["x0T"].ap()[c * 128:(c + 1) * 128, :]

        def x1_ap(c):
            return d["x1d"].ap()[c]

        def x2_ap(c):
            return d["x2d"].ap()[c]

        def out_ap(c):
            return d["outT"].ap()[c * 128:(c + 1) * 128, :]

        if sel is None:
            f1a = FFNP1(env, "f1a", 0, prefetched=True, writeback=True)
            phases.append(FrontPhase(env, f1a))
            phases.append(Barrier(env))
            phases.append(f1a)
            phases.append(Barrier(env))
            phases.append(FFNP2(env, "f1b", 1, x0_ap, x1_ap, do_stats=True, writeback=True))
            phases.append(Barrier(env))
        else:
            phases.append(Prologue(env))
            phases.append(Barrier(env))
            if want("ffn1") or want("ffn2"):
                phases.append(ScalePhase(env))
                phases.append(Barrier(env))
            if want("ln1"):
                phases.append(LNStats(env, "l1s", x0_ap))
                phases.append(Barrier(env))
                phases.append(LNRowsApply(env, "l1r", x0_ap))
                phases.append(Barrier(env))
            if want("ffn1"):
                phases.append(FFNP1(env, "f1a", 0))
                phases.append(Barrier(env))
                phases.append(FFNP2(env, "f1b", 1, x0_ap, x1_ap, do_stats=True))
                phases.append(Barrier(env))
        if want("ln2"):
            if not want("ffn1"):
                phases.append(LNStats(env, "l2s", x1_ap))
                phases.append(Barrier(env))
            phases.append(LNRowsApply(env, "l2r", x1_ap))
            phases.append(Barrier(env))
        if want("attn"):
            phases.append(ProjPhase(env))
            phases.append(Barrier(env))
            phases.append(TransposePhase(env))
            phases.append(Barrier(env))
            scan = ScanPhase(env)
            phases.append(scan)
            phases.append(Barrier(env))
            phases.append(Pass2Phase(env, scan))
            phases.append(Barrier(env))
            phases.append(WoPhase(env, x1_ap, x2_ap))
            phases.append(Barrier(env))
        if want("ln3"):
            if not want("attn"):
                phases.append(LNStats(env, "l3s", x2_ap))
                phases.append(Barrier(env))
            phases.append(LNRowsApply(env, "l3r", x2_ap))
            phases.append(Barrier(env))
        if want("ffn2"):
            cached = sel is None
            phases.append(FFNP1(env, "f2a", 0, cached=cached))
            phases.append(Barrier(env))
            f2b = FFNP2(env, "f2b", 1, x2_ap, out_ap, do_stats=False, cached=cached)
            phases.append(f2b)
            phases.append(Barrier(env))
            phases.append(FinalWait(env, f2b.k("bs"), 16 * NCH))

        tap_handles = []
        for name in taps:
            src = d[name]
            th = nc.dram_tensor(f"tap_{name}", list(src.shape), src.dtype, kind="ExternalOutput")
            tap_handles.append((th, src))
        if tap_handles:
            phases.append(Barrier(env))

            class TapPhase:
                def gpsimd(self, e):
                    n = 0
                    for th, src in tap_handles:
                        e.dma_start(out=th.ap(), in_=src.ap()).then_inc(out_sem, 16)
                        n += 16
                    e.wait_ge(out_sem, n)

                def sync(self, e): pass
                def vector(self, e): pass
                def scalar(self, e): pass
                def tensor(self, e): pass
            phases.append(TapPhase())

        block = ctx.enter_context(nc.Block())

        @block.sync
        def _(e):
            for ph in phases:
                ph.sync(e)

        @block.gpsimd
        def _(e):
            for ph in phases:
                ph.gpsimd(e)

        @block.vector
        def _(e):
            for ph in phases:
                ph.vector(e)

        @block.scalar
        def _(e):
            for ph in phases:
                ph.scalar(e)

        @block.tensor
        def _(e):
            for ph in phases:
                ph.tensor(e)

    return nc



def stage_inputs(L0_attn_x, L0_attn_Wq, L0_attn_Wk, L0_attn_Wv, L0_attn_Wo,
                 L0_ffn_w1, L0_ffn_b1, L0_ffn_w2, L0_ffn_b2):
    x = np.asarray(L0_attn_x, dtype=np.float32)
    w1 = np.ascontiguousarray(np.asarray(L0_ffn_w1, dtype=np.float32))
    w2 = np.ascontiguousarray(np.asarray(L0_ffn_w2, dtype=np.float32))
    b1 = np.asarray(L0_ffn_b1, dtype=np.float32)
    b2 = np.asarray(L0_ffn_b2, dtype=np.float32)

    def pack_ot(W, nb):
        o, i = W.shape
        return np.ascontiguousarray(
            W.reshape(nb, 128, i // 128, 128).transpose(0, 3, 2, 1).reshape(nb, 128, i))

    w1p = pack_ot(w1, NFB)
    w2p = pack_ot(w2, NCH)
    wqp = pack_ot(np.asarray(L0_attn_Wq, np.float32), NCH)
    wkp = pack_ot(np.asarray(L0_attn_Wk, np.float32), NCH)
    wvp = pack_ot(np.asarray(L0_attn_Wv, np.float32), NCH)
    wop = pack_ot(np.asarray(L0_attn_Wo, np.float32), NCH)
    b1t = np.ascontiguousarray(b1.reshape(NFB, 128).T)
    b2t = np.ascontiguousarray(b2.reshape(NCH, 128).T)

    in_maps = []
    for c in range(NCORES):
        b = c // 4
        t0 = (c % 4) * TOK
        x0T = np.ascontiguousarray(x[b, t0:t0 + TOK, :].T)
        w1sl = w1p[c * 8:(c + 1) * 8]
        w2sl = w2p[c * 2:(c + 1) * 2].reshape(2, 128, 4, 2048).transpose(0, 2, 1, 3).reshape(8, 128, 2048)
        w12slice = np.ascontiguousarray(np.concatenate([w1sl, w2sl], axis=0))
        coef = np.zeros((128, 8), np.float32)
        g0 = 4 * (c // 4)
        for r in range(g0, c):
            coef[:, r] = 1.0
        in_maps.append(dict(
            x0T=x0T, w1p=w1p, w2p=w2p, wqp=wqp, wkp=wkp, wvp=wvp, wop=wop,
            b1t=b1t, b2t=b2t, w12slice=w12slice, prefix_coef=coef,
        ))
    return in_maps


def kernel(**inputs):
    in_maps = stage_inputs(**inputs)
    nc = build_program()
    res = run_bass_kernel_spmd(nc, in_maps, core_ids=list(range(NCORES)))
    out = np.empty((2, 4096, DM), dtype=np.float32)
    for c in range(NCORES):
        b = c // 4
        t0 = (c % 4) * TOK
        out[b, t0:t0 + TOK, :] = res.results[c]["outT"].T
    return out
